# version 05
# baseline (speedup 1.0000x reference)
import numpy as np
import ml_dtypes
from contextlib import ExitStack

import concourse.bass as bass
import concourse.mybir as mybir
from concourse.bass_utils import run_bass_kernel_spmd

F32 = mybir.dt.float32
BF16 = mybir.dt.bfloat16
FP8 = mybir.dt.float8e4
AF = mybir.ActivationFunctionType
ALU = mybir.AluOpType
AX = mybir.AxisListType

D = 1024
HID = 4096
NCH = D // 128
NHC = HID // 128
SEQ = 8192
BATCH = 4
NCORE = 8
TPC = SEQ // 2
TAU = 256
NST = TPC // TAU
TAU2 = 512
NST2 = TPC // TAU2
RNDC = 1.5 * 2.0**23
EPS5 = 1e-5
EPS6 = 1e-6

ENGINES = ("sync", "scalar", "vector", "tensor", "gpsimd")

import os as _os
CHAIN_EDGES = _os.environ.get("BASS_SIM_FP64") == "1" or \
    _os.environ.get("KERNEL_FORCE_CHAIN_EDGES") == "1"
DEBUG_DUMPS = False



class Prog:

    def __init__(self, nc, es):
        self.nc = nc
        self.es = es
        self.ops = {e: [] for e in ENGINES}
        self.sems = {}
        self.cnt = {}

    def sem(self, name):
        if name not in self.sems:
            self.sems[name] = self.es.enter_context(self.nc.semaphore(name))
            self.cnt[name] = 0
        return name

    def add(self, eng, emit, waits=None, inc=None, amt=1):
        w = dict(waits or {})
        incs = []
        if inc is not None:
            self.sem(inc)
            self.cnt[inc] += amt
            incs = [(inc, amt)]
        for s in w:
            self.sem(s)
        self.ops[eng].append((w, emit, incs))

    def c(self, name):
        self.sem(name)
        return self.cnt[name]

    def dma(self, eng, emit, waits=None, stream="L"):
        w = dict(waits or {})
        self.sem(stream)
        w[stream] = self.cnt[stream]
        self.add(eng, emit, w, inc=stream, amt=16)

    def realize(self):
        with self.nc.Block() as blk:
            for eng in ENGINES:
                seq = self.ops[eng]
                if not seq:
                    continue
                chain_sem = None
                if CHAIN_EDGES and eng in ("vector", "scalar", "gpsimd"):
                    chain_sem = self.es.enter_context(
                        self.nc.semaphore(f"chain_{eng}"))

                def make(seq, chain_sem):
                    def body(e):
                        ee = _ChainEngine(e, chain_sem) if chain_sem else e
                        for (waits, emit, incs) in seq:
                            if chain_sem:
                                ee.flush_pending()
                            for s, v in waits.items():
                                if v > 0:
                                    e.wait_ge(self.sems[s], v)
                            ins = emit(ee)
                            for s, a in incs:
                                if ins is None:
                                    ins = e.engine_nop()
                                    if chain_sem:
                                        ee.chain_bare(ins)
                                if chain_sem:
                                    ee.take_inc(ins, self.sems[s], a)
                                else:
                                    ins.then_inc(self.sems[s], a)
                        if chain_sem:
                            ee.finalize()
                    return body
                getattr(blk, eng)(make(list(seq), chain_sem))


class _ChainEngine:

    _SKIP = {"dma_start", "wait_ge", "then_inc", "engine_nop", "bass",
             "value_load", "partition_id", "flush_pending", "take_inc",
             "finalize", "chain_bare"}

    def __init__(self, eng, sem):
        self._eng = eng
        self._sem = sem
        self._n = 0
        self._last = None
        self._pending = []
        self._reccnt = {}

    def _finalize_last(self):
        if self._last is not None:
            self._last.then_inc(self._sem, 1)
            self._n += 1
            self._last = None

    def flush_pending(self):
        for (s, v) in self._pending:
            self._eng.wait_ge(s, v)
        self._pending = []

    def take_inc(self, ins, sem_handle, amt):
        key = id(sem_handle)
        self._reccnt[key] = self._reccnt.get(key, 0) + amt
        if ins is self._last:
            self._last = None
            ins.then_inc(sem_handle, amt)
            self._pending.append((sem_handle, self._reccnt[key]))
        else:
            ins.then_inc(sem_handle, amt)

    def chain_bare(self, ins):
        self._finalize_last()
        if self._n > 0:
            ins._wait_ge(self._sem, self._n)

    def finalize(self):
        self._finalize_last()

    def __getattr__(self, name):
        f = getattr(self._eng, name)
        if not callable(f) or name.startswith("_") or name in self._SKIP:
            return f

        def wrap(*a, **k):
            self._finalize_last()
            self.flush_pending()
            r = f(*a, **k)
            if hasattr(r, "then_inc"):
                if self._n > 0:
                    r._wait_ge(self._sem, self._n)
                self._last = r
            return r
        return wrap



def emit_weight_quant(p, sb, g, stage, wabs):
    nc = p.nc
    w1t, w2t = g["w1t"], g["w2t"]
    w1q = sb("w1q", [128, NCH * HID], FP8)
    w2q = sb("w2q", [128, NHC * D], FP8)
    acc = sb("wacc", [128, NCH + NHC], F32)
    accr = sb("waccr", [128, 2], F32)
    rowt = sb("wrowt", [1, 2], F32)
    mwc = sb("mwc", [1, 2], F32)
    ccol = sb("ccol", [128, 2], F32)
    onesf = sb("onesf", [1, 128], F32)
    onesb = sb("onesb", [128, 1], BF16)
    onesc = sb("onesc", [128, 1], F32)
    with nc.psum_tensor("cps", [128, 2], F32) as cps:
        pass

    plans = [(w1t, NCH, HID, 0, 127.0), (w2t, NHC, D, 1, 254.0)]
    p.add("vector", lambda v: v.memset(onesf[:], 1.0))
    p.add("vector", lambda v: v.memset(onesb[:], 1.0))
    p.add("vector", lambda v: v.memset(onesc[:], 1.0), inc="wq_init")

    for (wt, n, flen, wi, div) in plans:
        col0 = 0 if wi == 0 else NCH
        for k in range(n):
            p.dma("sync",
                  (lambda wt=wt, k=k, flen=flen:
                   lambda s: s.dma_start(out=stage[:, 0:flen], in_=wt[k],
                                         max_dma_last_dim=2048))(),
                  waits={"wq_abs": max(0, p.c("wq_abs"))},
                  stream="wq_ld")
            p.add("vector",
                  (lambda k=k, flen=flen, col0=col0:
                   lambda v: v.tensor_reduce(
                       acc[:, col0 + k:col0 + k + 1], stage[:, 0:flen],
                       AX.X, ALU.add, apply_absolute_value=True))(),
                  waits={"wq_ld": p.c("wq_ld")},
                  inc="wq_abs")
        total = 128.0 * n * flen
        p.add("vector",
              (lambda n=n, wi=wi, col0=col0: lambda v: v.tensor_reduce(
                  accr[:, wi:wi + 1], acc[:, col0:col0 + n], AX.X,
                  ALU.add))(),
              waits={"wq_abs": p.c("wq_abs")}, inc="wq_red")
        p.add("gpsimd",
              (lambda wi=wi: lambda gp: gp.tensor_reduce(
                  rowt[0:1, wi:wi + 1], accr[:, wi:wi + 1], AX.C, ALU.add))(),
              waits={"wq_red": p.c("wq_red")}, inc="wq_gp")
        p.add("vector",
              (lambda wi=wi, total=total: lambda v: v.tensor_scalar(
                  rowt[0:1, wi:wi + 1], rowt[0:1, wi:wi + 1],
                  1.0 / total, EPS5, ALU.mult, ALU.max))(),
              waits={"wq_gp": p.c("wq_gp")}, inc="wq_rmA")
        p.add("vector",
              (lambda wi=wi, div=div: lambda v: v.tensor_scalar(
                  mwc[0:1, wi:wi + 1], rowt[0:1, wi:wi + 1],
                  1.0 / div, None, ALU.mult))(),
              waits={"wq_rmA": p.c("wq_rmA")}, inc="wq_rmB")
        p.add("vector",
              (lambda wi=wi: lambda v: v.reciprocal(
                  rowt[0:1, wi:wi + 1], rowt[0:1, wi:wi + 1]))(),
              waits={"wq_rmB": p.c("wq_rmB")}, inc="wq_rm")
        p.add("tensor",
              (lambda wi=wi: lambda t: t.matmul(
                  cps[:, wi:wi + 1], onesf[:], rowt[0:1, wi:wi + 1],
                  start=True, stop=True))(),
              waits={"wq_rm": p.c("wq_rm"), "wq_init": 1,
                     "wq_bc": p.c("wq_bc")}, inc="wq_bc")
        p.add("scalar",
              (lambda wi=wi: lambda sc: sc.copy(ccol[:, wi:wi + 1],
                                                cps[:, wi:wi + 1]))(),
              waits={"wq_bc": p.c("wq_bc")}, inc="wq_cc")

    for (wt, n, flen, wi, div) in plans:
        wq = w1q if wi == 0 else w2q
        for k in range(n):
            ldg = {"wq_q": p.c("wq_q")}
            if wi == 0 and k == 0:
                ldg["wq_abs"] = NCH + NHC
            p.dma("sync",
                  (lambda wt=wt, k=k, flen=flen:
                   lambda s: s.dma_start(out=stage[:, 0:flen], in_=wt[k],
                                         max_dma_last_dim=2048))(),
                  waits=ldg, stream="wq_ld")
            def _quant(v, k=k, flen=flen, wi=wi, wq=wq):
                v.tensor_scalar(stage[:, 0:flen], stage[:, 0:flen],
                                ccol[:, wi:wi + 1], RNDC, ALU.mult, ALU.add)
                v.tensor_scalar(stage[:, 0:flen], stage[:, 0:flen],
                                RNDC, 1.0, ALU.subtract, ALU.min)
                return v.tensor_scalar(wq[:, k * flen:(k + 1) * flen],
                                       stage[:, 0:flen], -1.0, None, ALU.max)
            p.add("vector", _quant,
                  waits={"wq_ld": p.c("wq_ld"), "wq_cc": 1 + wi},
                  inc="wq_q")
    return dict(w1q=w1q, w2q=w2q, mwc=mwc, onesf=onesf, onesb=onesb,
                onesc=onesc,
                wq_done={"wq_q": p.c("wq_q")})



def build_ffn_launch(second, n_subtiles=NST, debug_stage=99):
    nc = bass.Bass("TRN2", target_bir_lowering=False)
    es = ExitStack()
    dram = lambda name, shape, dt, kind: nc.dram_tensor(
        name, shape, dt, kind=kind).ap()

    g = {}
    g["w1t"] = dram("w1t", [NCH, 128, HID], F32, "ExternalInput")
    g["w2t"] = dram("w2t", [NHC, 128, D], F32, "ExternalInput")
    b1v = dram("b1v", [128, NHC], F32, "ExternalInput")
    b2v = dram("b2v", [128, NCH], F32, "ExternalInput")
    if not second:
        x_in = dram("x_in", [NCH, 128, TPC], F32, "ExternalInput")
        x1_out = dram("x1_out", [NCH, 128, TPC], F32, "ExternalOutput")
        rows_out = dram("rows_out", [2, TPC], F32, "ExternalOutput")
    else:
        x1_in = dram("x1_in", [NCH, 128, TPC], F32, "ExternalInput")
        hloc_in = dram("hloc_in", [NCH, 128, TPC], F32, "ExternalInput")
        p_in = dram("p_in", [NCH, 128, TPC], BF16, "ExternalInput")
        hcarry = dram("hcarry", [128, NCH], F32, "ExternalInput")
        wot = dram("wot", [NCH, 128, D], F32, "ExternalInput")
        bov = dram("bov", [128, NCH], F32, "ExternalInput")
        out_fm = dram("out_fm", [NCH, 128, TPC], F32, "ExternalOutput")

    with es:
        p = Prog(nc, es)
        sb = lambda *a: es.enter_context(nc.sbuf_tensor(*a))
        hbuf = sb("hbuf", [128, NHC * TAU], F32)
        stage = hbuf.ap()[:, 0:HID]
        wabs = hbuf.ap()[:, HID:2 * HID]
        W = emit_weight_quant(p, sb, g, stage, wabs)
        w1q, w2q = W["w1q"], W["w2q"]
        onesf, onesb, mwc = W["onesf"], W["onesb"], W["mwc"]
        onesc = W["onesc"]

        if not second:
            xt = [sb(f"xt{i}", [128, NCH * TAU], F32) for i in range(2)]
        else:
            xt = [sb("xt0", [128, NCH * TAU], F32)] * 2
        if not second:
            xsq = [sb(f"xsq{i}", [128, NCH * TAU], F32) for i in range(2)]
        else:
            xsq = [sb("xsq0", [128, NCH * TAU], F32)] * 2
        sacc = sb("sacc", [128, TAU], F32)
        sacc2 = sb("sacc2", [128, TAU], F32)
        qtmp2 = sacc
        if not second:
            k1 = [sb(f"k1_{i}", [128, NCH * TAU], BF16) for i in range(2)]
        else:
            k1 = [sb("k1_0", [128, NCH * TAU], BF16)] * 2
        qtmp = None
        hsq = [sb(f"hsq{i}", [128, TAU], F32) for i in range(2)]
        htmp = [sb(f"htmp{i}", [128, TAU], F32) for i in range(2)]
        k2 = sb("k2", [128, NHC * TAU], BF16)
        x1t = sb("x1t", [128, NCH * TAU], F32)
        if not second:
            x1sq = sb("x1sq", [128, NCH * TAU], F32)
        cmx = sb("cmx", [128, TAU], F32)
        cmn = sb("cmn", [128, TAU], F32)
        hmax = cmx
        e1sb = sb("e1sb", [128, TAU], F32)
        e2sb = sb("e2sb", [128, TAU], F32)
        b1sb = sb("b1sb", [128, NHC], F32)
        b2sb = sb("b2sb", [128, NCH], F32)
        if not second:
            m2row = sb("m2row", [1, TAU], F32)
            r2row = sb("r2row", [1, TAU], F32)
        epsA = sb("epsA", [1, 1], F32)
        epsB = sb("epsB", [1, 1], F32)
        rw = {}
        for grp in (("m",), ("e2", "rr", "t4"), ("R", "A2"), ("t3",),
                    ("M1", "a2"), ("a", "aa2"), ("aa", "Mn"), ("s", "s2"),
                    ("Mx", "m2"), ("v",), ("r",),
                    ("vr", "h2", "lnt"), ("q", "q2"), ("S1", "S2"),
                    ("mS1", "e2t"), ("e1",)):
            t = sb(f"rw_{grp[0]}", [1, TAU], F32).ap()
            for n in grp:
                rw[n] = t
        if second:
            x1in_t = [sb("x1in0", [128, NCH * TAU], F32)] * 2
            hl_t = [sb("hl0", [128, NCH * TAU], F32)] * 2
            p_t = [sb("pt0", [128, NCH * TAU], BF16)] * 2
            hfix = [sb("hfix0", [128, NCH * TAU], BF16)] * 2
            hcar = sb("hcar", [128, NCH], F32)
            bosb = sb("bosb", [128, NCH], F32)
            wotb = sb("wotb", [128, NCH * D], BF16)

        pt = lambda *a: es.enter_context(nc.psum_tensor(*a))
        hps = [pt(f"hps{i}", [128, TAU], F32) for i in range(2)]
        ops = [pt(f"ops{i}", [128, TAU], F32) for i in range(2)]
        statb = pt("statb", [1, 2 * TAU], F32)
        stat2b = pt("stat2b", [1, 2 * TAU], F32)
        stat3b = statb
        repA = pt("repA", [128, 2 * TAU], F32)
        repB = pt("repB", [128, 2 * TAU], F32)

        p.dma("sync", lambda s: s.dma_start(out=b1sb[:], in_=b1v[:]),
              stream="ld0")
        p.dma("sync", lambda s: s.dma_start(out=b2sb[:], in_=b2v[:]),
              stream="ld0")
        if second:
            p.dma("sync", lambda s: s.dma_start(out=bosb[:], in_=bov[:]),
                  stream="ld0")
            p.dma("sync", lambda s: s.dma_start(out=hcar[:], in_=hcarry[:]),
                  stream="ld0")
        p.add("vector", lambda v: v.memset(epsA[:], EPS5))
        p.add("vector", lambda v: v.memset(epsB[:], EPS6))
        p.add("scalar", lambda sc: sc.mul(b2sb[:], b2sb[:], 0.5),
              waits={"ld0": 32}, inc="bias_ok")
        if second:
            wostage = stage[:, 0:D]
            for kc in range(NCH):
                wol = {"wo_cast": p.c("wo_cast")}
                if kc == 0:
                    wol["wq_q"] = NCH + NHC
                p.dma("sync",
                      (lambda kc=kc: lambda s: s.dma_start(
                          out=wostage, in_=wot[kc]))(),
                      waits=wol, stream="wo_ld")
                p.add("scalar",
                      (lambda kc=kc: lambda sc: sc.copy(
                          wotb[:, kc * D:(kc + 1) * D], wostage))(),
                      waits={"wo_ld": 16 * (kc + 1)}, inc="wo_cast")

        x_src = x_in if not second else x1_in

        def fm_dram(t, t0, n):
            return t.rearrange("c p t -> p c t")[:, :, t0:t0 + n]

        def fm_sbuf(t):
            return t.ap().rearrange("p (c t) -> p c t", c=NCH)

        for i in range(n_subtiles):
            t0 = i * TAU
            sl = i % 2
            CS = lambda c: slice(c * TAU, (c + 1) * TAU)

            ldw = {}
            if i >= 2:
                ldw["X1"] = p.c("X1") - 1
                ldw["STT"] = p.c("STT") - 1
            if not second:
                p.dma("sync",
                      (lambda sl=sl, t0=t0: lambda s: s.dma_start(
                          out=fm_sbuf(xt[sl]), in_=fm_dram(x_src, t0, TAU)))(),
                      waits=ldw, stream="Lx")
            else:
                if i >= 1:
                    ldw["FXD"] = p.c("FXD")
                    ldw["OPC"] = p.c("OPC") - 8
                p.dma("sync",
                      (lambda sl=sl, t0=t0: lambda s: s.dma_start(
                          out=fm_sbuf(x1in_t[sl]),
                          in_=fm_dram(x1_in, t0, TAU)))(),
                      waits=ldw, stream="Lx")
                p.dma("sync",
                      (lambda sl=sl, t0=t0: lambda s: s.dma_start(
                          out=fm_sbuf(hl_t[sl]),
                          in_=fm_dram(hloc_in, t0, TAU)))(),
                      waits=dict(ldw), stream="Lh")
                p.dma("sync",
                      (lambda sl=sl, t0=t0: lambda s: s.dma_start(
                          out=fm_sbuf(p_t[sl]),
                          in_=fm_dram(p_in, t0, TAU)))(),
                      waits=dict(ldw), stream="Lp")
            l_now = {"Lx": 16 * (i + 1)}
            if second:
                l_now.update({"Lh": 16 * (i + 1), "Lp": 16 * (i + 1)})

            if second:
                def _fixup(v, sl=sl):
                    last = None
                    for c in range(NCH):
                        last = v.scalar_tensor_tensor(
                            hfix[sl][:, CS(c)], p_t[sl][:, CS(c)],
                            hcar[:, c:c + 1], hl_t[sl][:, CS(c)],
                            ALU.mult, ALU.add)
                    return last
                fw = dict(l_now); fw["ld0"] = 64
                if i >= 1:
                    fw["WOMM"] = p.c("WOMM")
                p.add("vector", _fixup, waits=fw, inc="FXD")
                for oc in range(NCH):
                    ww = {"FXD": p.c("FXD"), "wo_cast": NCH}
                    if i > 0 or oc >= 2:
                        ww["OPC"] = p.c("OPC") - 1 if (i > 0 or oc >= 2) else 0
                    def _womm(t, oc=oc, sl=sl):
                        last = None
                        for kc in range(NCH):
                            lt = wotb[:, kc * D + oc * 128:
                                      kc * D + (oc + 1) * 128]
                            last = t.matmul(
                                ops[oc % 2][:],
                                lt, hfix[sl][:, CS(kc)],
                                start=(kc == 0), stop=(kc == NCH - 1))
                        return last
                    p.add("tensor", _womm, waits=ww, inc="WOMM")
                    xw = {"WOMM": p.c("WOMM"), "bias_ok": 1}
                    if i >= 1:
                        xw["X1"] = p.c("X1")
                    p.add("vector",
                          (lambda oc=oc, sl=sl: lambda v:
                           v.scalar_tensor_tensor(
                               xt[sl][:, CS(oc)],
                               ops[oc % 2][:],
                               bosb[:, oc:oc + 1], x1in_t[sl][:, CS(oc)],
                               ALU.add, ALU.add))(),
                          waits=xw, inc="OPC")
                xready = {"OPC": p.c("OPC")}
            else:
                xready = dict(l_now)

            castw = dict(xready)
            if second and i >= 1:
                castw["STT"] = p.c("STT")
            p.add("scalar",
                  (lambda sl=sl: lambda sc: sc.activation(
                      xsq[sl][:], xt[sl][:], AF.Square))(),
                  waits=castw, inc="SC1")
            scw = {"SC1": p.c("SC1")}
            scw.update(xready)
            if i >= 1:
                scw["STT"] = p.c("STT")
            if i >= 1 and not second:
                scw["LNS"] = p.c("LNS")
            def _chunksums(v, sl=sl):
                v.tensor_reduce(sacc[:],
                                xt[sl].ap().rearrange("p (c t) -> p t c",
                                                      c=NCH),
                                AX.X, ALU.add)
                return v.tensor_reduce(
                    sacc2[:], xsq[sl].ap().rearrange("p (c t) -> p t c",
                                                     c=NCH),
                    AX.X, ALU.add)
            p.add("vector", _chunksums, waits=scw, inc="SCC")
            if debug_stage <= 1:
                continue
            stw = {"SCC": p.c("SCC"), "wq_init": 1, "STT": p.c("STT")}
            if i >= 1:
                stw["RA"] = p.c("RA")
                stw["LNV"] = p.c("LNV")
            def _ln1stats(t, sl=sl):
                t.matmul(statb[0:1, 0:TAU], onesc[:], sacc[:],
                         start=True, stop=True)
                return t.matmul(statb[0:1, TAU:2 * TAU], onesc[:], sacc2[:],
                                start=True, stop=True)
            p.add("tensor", _ln1stats, waits=stw, inc="STT")

            mmw = dict(xready)
            if i >= 1:
                mmw["GP1"] = p.c("GP1")
                mmw["GP2"] = p.c("GP2")
            def _cmaxmin(v, sl=sl):
                xv = xt[sl].ap().rearrange("p (c t) -> p t c", c=NCH)
                v.tensor_reduce(cmx[:], xv, AX.X, ALU.max)
                return v.tensor_reduce(cmn[:], xv, AX.X, ALU.min, negate=True)
            p.add("vector", _cmaxmin, waits=mmw, inc="MMR")
            p.add("gpsimd",
                  lambda gp: gp.tensor_reduce(rw["Mx"][:], cmx[:], AX.C,
                                              ALU.max),
                  waits={"MMR": p.c("MMR")})
            p.add("gpsimd",
                  lambda gp: gp.tensor_reduce(rw["Mn"][:], cmn[:], AX.C,
                                              ALU.max),
                  inc="GP1")

            raw = {"STT": p.c("STT"), "GP1": p.c("GP1")}
            if i >= 1:
                raw["BC1"] = p.c("BC1")
                raw["LNR"] = p.c("LNR")
            def _rmA(v):
                v.tensor_scalar(rw["m"][:], statb[0:1, 0:TAU], 1.0 / D, None,
                                ALU.mult)
                v.tensor_scalar(rw["e2"][:], statb[0:1, TAU:2 * TAU], 1.0 / D,
                                None, ALU.mult)
                v.tensor_tensor(rw["v"][:], rw["m"][:], rw["m"][:], ALU.mult)
                return v.tensor_tensor(rw["v"][:], rw["e2"][:], rw["v"][:],
                                       ALU.subtract)
            p.add("vector", _rmA, waits=raw, inc="RA")
            p.add("scalar",
                  lambda sc: sc.activation(rw["r"][:], rw["v"][:], AF.Sqrt,
                                           bias=epsA[:]),
                  waits={"RA": p.c("RA")}, inc="RB")
            def _rmB(v):
                v.reciprocal(rw["r"][:], rw["r"][:])
                v.tensor_tensor(rw["rr"][:], rw["r"][:], rw["r"][:], ALU.mult)
                return v.tensor_tensor(rw["vr"][:], rw["v"][:], rw["rr"][:],
                                       ALU.mult)
            p.add("vector", _rmB, waits={"RB": p.c("RB")}, inc="RC")
            p.add("scalar",
                  lambda sc: sc.activation(rw["q"][:], rw["vr"][:], AF.Sqrt,
                                           bias=epsB[:]),
                  waits={"RC": p.c("RC")}, inc="RD")
            def _rmC(v):
                v.reciprocal(rw["q"][:], rw["q"][:])
                v.tensor_tensor(rw["R"][:], rw["r"][:], rw["q"][:], ALU.mult)
                v.tensor_tensor(rw["t3"][:], rw["Mx"][:], rw["m"][:],
                                ALU.subtract)
                v.tensor_tensor(rw["t4"][:], rw["m"][:], rw["Mn"][:],
                                ALU.add)
                v.tensor_tensor(rw["M1"][:], rw["t3"][:], rw["t4"][:], ALU.max)
                v.tensor_tensor(rw["a"][:], rw["R"][:], rw["M1"][:], ALU.mult)
                v.tensor_scalar(rw["aa"][:], rw["a"][:], EPS5, None, ALU.max)
                v.reciprocal(rw["s"][:], rw["aa"][:])
                v.tensor_scalar(rw["s"][:], rw["s"][:], 127.0, None, ALU.mult)
                v.tensor_tensor(rw["S1"][:], rw["R"][:], rw["s"][:], ALU.mult)
                v.tensor_tensor(rw["mS1"][:], rw["m"][:], rw["S1"][:],
                                ALU.mult)
                return v.tensor_scalar(rw["e1"][:], rw["aa"][:],
                                       mwc[0:1, 0:1], None, ALU.mult)
            p.add("vector", _rmC, waits={"RD": p.c("RD")}, inc="RM1")

            bcw = {"RM1": p.c("RM1"), "BC1": p.c("BC1")}
            if i >= 1:
                bcw["K1"] = p.c("K1")
                bcw["EC2"] = p.c("EC2")
                bcw["K2"] = p.c("K2")
            def _bc1(t):
                t.matmul(repA[:, 0:TAU], onesf[:], rw["S1"][:], start=True,
                         stop=True)
                t.matmul(repA[:, TAU:2 * TAU], onesf[:], rw["mS1"][:],
                         start=True, stop=True)
                return t.matmul(repB[:, 0:TAU], onesf[:], rw["e1"][:],
                                start=True, stop=True)
            p.add("tensor", _bc1, waits=bcw, inc="BC1")
            ecw = {"BC1": p.c("BC1")}
            if i >= 1:
                ecw["HEV"] = p.c("HEV")
                ecw["K2"] = p.c("K2")
            p.add("scalar",
                  lambda sc: sc.copy(e1sb[:], repB[:, 0:TAU]),
                  waits=ecw, inc="EC1")

            k1w = {"BC1": p.c("BC1")}
            k1w.update(xready)
            if second and i >= 1:
                k1w["MM1"] = p.c("MM1")
            elif i >= 2:
                k1w["MM1"] = p.c("MM1") - NHC
            def _k1(v, sl=sl):
                last = None
                for c in range(NCH):
                    v.tensor_tensor(sacc[:], xt[sl][:, CS(c)], repA[:, 0:TAU],
                                    ALU.mult)
                    v.tensor_tensor(sacc[:], sacc[:], repA[:, TAU:2 * TAU],
                                    ALU.subtract)
                    v.tensor_scalar(sacc[:], sacc[:], RNDC, RNDC, ALU.add,
                                    ALU.subtract)
                    last = v.tensor_scalar(k1[sl][:, CS(c)], sacc[:], 127.0,
                                           -128.0, ALU.min, ALU.max)
                return last
            if debug_stage <= 2:
                continue
            p.add("vector", _k1, waits=k1w, inc="K1")
            if debug_stage <= 3:
                continue

            base_hev = p.c("HEV")
            base_hgd = p.c("HGD")
            for mc in range(NHC):
                w1w = {"K1": p.c("K1")}
                w1w.update(W["wq_done"] if i == 0 and mc == 0 else {})
                if mc >= 2:
                    w1w["HEV"] = base_hev + mc - 1
                elif i >= 1:
                    w1w["HEV"] = base_hev
                if mc < 2:
                    w1w["MM1"] = p.c("MM1")
                def _mm1(t, mc=mc, sl=sl):
                    last = None
                    for kc in range(NCH):
                        lt = w1q[:, kc * HID + mc * 128:
                                 kc * HID + (mc + 1) * 128]
                        last = t.matmul(
                            hps[mc % 2][:], lt,
                            k1[sl][:, CS(kc)],
                            start=(kc == 0), stop=(kc == NCH - 1))
                    return last
                p.add("tensor", _mm1, waits=w1w, inc="MM1")
                hevw = {"MM1": p.c("MM1"), "EC1": p.c("EC1")}
                if i >= 1 and mc < 2:
                    hevw["HGD"] = base_hgd
                p.add("vector",
                      (lambda mc=mc: lambda v: v.tensor_tensor(
                          htmp[mc % 2][:],
                          hps[mc % 2][:],
                          e1sb[:], ALU.mult))(),
                      waits=hevw, inc="HEV")
                gw = {"HEV": p.c("HEV")}
                if i >= 1:
                    gw["K2"] = p.c("K2")
                    gw["AMX"] = p.c("AMX")
                def _gelu(sc, mc=mc):
                    sc.activation(hbuf[:, mc * TAU:(mc + 1) * TAU],
                                  htmp[mc % 2][:], AF.Gelu,
                                  bias=b1sb[:, mc:mc + 1])
                    return sc.activation(hsq[mc % 2][:],
                                         hbuf[:, mc * TAU:(mc + 1) * TAU],
                                         AF.Square)
                p.add("scalar", _gelu, waits=gw, inc="HGD")
                sqw = {"HGD": p.c("HGD")}
                if mc == 0 and i >= 1:
                    sqw["RE"] = p.c("RE")
                    sqw["LNV"] = p.c("LNV")
                p.add("tensor",
                      (lambda mc=mc: lambda t: t.matmul(
                          stat2b[0:1, 0:TAU], onesc[:], hsq[mc % 2][:],
                          start=(mc == 0), stop=(mc == NHC - 1)))(),
                      waits=sqw, inc="SQS")

            if debug_stage <= 4:
                continue
            p.add("vector",
                  lambda v: v.tensor_reduce(
                      hmax[:], hbuf.ap().rearrange("p (c t) -> p t c", c=NHC),
                      AX.X, ALU.max, apply_absolute_value=True),
                  waits={"HGD": p.c("HGD"), "GP1": p.c("GP1")},
                  inc="AMX")
            p.add("gpsimd",
                  lambda gp: gp.tensor_reduce(rw["A2"][:], hmax[:], AX.C,
                                              ALU.max),
                  waits={"AMX": p.c("AMX")}, inc="GP2")
            p.add("vector",
                  lambda v: v.tensor_scalar(rw["h2"][:], stat2b[0:1, 0:TAU],
                                            1.0 / HID, None, ALU.mult),
                  waits={"SQS": p.c("SQS")}, inc="RE")
            p.add("scalar",
                  lambda sc: sc.activation(rw["q2"][:], rw["h2"][:], AF.Sqrt,
                                           bias=epsB[:]),
                  waits={"RE": p.c("RE")}, inc="RF")
            def _rm2(v):
                v.reciprocal(rw["q2"][:], rw["q2"][:])
                v.tensor_tensor(rw["a2"][:], rw["q2"][:], rw["A2"][:],
                                ALU.mult)
                v.tensor_scalar(rw["aa2"][:], rw["a2"][:], EPS5, None,
                                ALU.max)
                v.reciprocal(rw["s2"][:], rw["aa2"][:])
                v.tensor_scalar(rw["s2"][:], rw["s2"][:], 127.0, None,
                                ALU.mult)
                v.tensor_tensor(rw["S2"][:], rw["q2"][:], rw["s2"][:],
                                ALU.mult)
                return v.tensor_scalar(rw["e2t"][:], rw["aa2"][:],
                                       mwc[0:1, 1:2], None, ALU.mult)
            p.add("vector", _rm2, waits={"RF": p.c("RF"), "GP2": p.c("GP2")},
                  inc="RM2")
            bc2w = {"RM2": p.c("RM2"), "EC1": p.c("EC1"),
                    "BC2": p.c("BC2")}
            if i >= 1:
                bc2w["K2"] = p.c("K2")
            def _bc2(t):
                t.matmul(repB[:, TAU:2 * TAU], onesf[:], rw["S2"][:],
                         start=True, stop=True)
                return t.matmul(repB[:, 0:TAU], onesf[:], rw["e2t"][:],
                                start=True, stop=True)
            p.add("tensor", _bc2, waits=bc2w, inc="BC2")
            ec2w = {"BC2": p.c("BC2")}
            if i >= 1:
                ec2w["OPC"] = p.c("OPC")
            p.add("scalar",
                  lambda sc: sc.copy(e2sb[:], repB[:, 0:TAU]),
                  waits=ec2w, inc="EC2")

            if debug_stage <= 5:
                continue
            base_mm2 = p.c("MM2")
            k2w = {"BC2": p.c("BC2"), "EC2": p.c("EC2")}
            if i >= 1:
                k2w["MM2"] = base_mm2
            def _k2(v):
                last = None
                for hc in range(NHC):
                    v.tensor_tensor(sacc[:],
                                    hbuf[:, hc * TAU:(hc + 1) * TAU],
                                    repB[:, TAU:2 * TAU], ALU.mult)
                    v.tensor_scalar(sacc[:], sacc[:], RNDC, RNDC,
                                    ALU.add, ALU.subtract)
                    last = v.tensor_scalar(
                        k2[:, hc * TAU:(hc + 1) * TAU],
                        sacc[:], 127.0, -128.0, ALU.min, ALU.max)
                return last
            p.add("vector", _k2, waits=k2w, inc="K2")

            if debug_stage <= 6:
                continue
            for oc in range(NCH):
                m2w = {"K2": p.c("K2")}
                if i == 0 and oc == 0:
                    m2w.update(W["wq_done"])
                if oc >= 2:
                    m2w["OPC"] = p.c("OPC") - 1
                elif second or i >= 1:
                    m2w["OPC"] = p.c("OPC")
                def _mm2(t, oc=oc):
                    last = None
                    for kc in range(NHC):
                        lt = w2q[:, kc * D + oc * 128:kc * D + (oc + 1) * 128]
                        last = t.matmul(
                            ops[oc % 2][:], lt,
                            k2[:, kc * TAU:(kc + 1) * TAU],
                            start=(kc == 0), stop=(kc == NHC - 1))
                    return last
                p.add("tensor", _mm2, waits=m2w, inc="MM2")
                ew = {"MM2": p.c("MM2"), "EC2": p.c("EC2"), "bias_ok": 1}
                if i >= 1 and oc == 0:
                    ew["STx1"] = p.c("STx1")
                    ew["SQMs"] = p.c("SQMs")
                    ew["SC2"] = p.c("SC2")
                def _epi(v, oc=oc, sl=sl):
                    v.tensor_tensor(x1t[:, CS(oc)],
                                    ops[oc % 2][:],
                                    e2sb[:], ALU.mult)
                    return v.scalar_tensor_tensor(
                        x1t[:, CS(oc)], x1t[:, CS(oc)], b2sb[:, oc:oc + 1],
                        xt[sl][:, CS(oc)], ALU.add, ALU.add)
                p.add("vector", _epi, waits=ew, inc="OPC")
            p.add("vector", lambda v: None, waits={"OPC": p.c("OPC")},
                  inc="X1")

            if debug_stage <= 7:
                continue
            if second:
                p.dma("gpsimd",
                      (lambda t0=t0: lambda gp: gp.dma_start(
                          out=fm_dram(out_fm, t0, TAU),
                          in_=fm_sbuf(x1t)))(),
                      waits={"X1": p.c("X1")}, stream="STx1")
                continue
            cguard = {"X1": p.c("X1")}
            if i >= 1:
                cguard["LNS"] = p.c("LNS")
            p.add("scalar",
                  lambda sc: sc.activation(x1sq[:], x1t[:], AF.Square),
                  waits=dict(cguard), inc="SQMs")
            sc2w = {"SQMs": p.c("SQMs"), "X1": p.c("X1"),
                    "STT": p.c("STT")}
            def _chunksums2(v):
                v.tensor_reduce(sacc[:],
                                x1t.ap().rearrange("p (c t) -> p t c",
                                                   c=NCH),
                                AX.X, ALU.add)
                return v.tensor_reduce(
                    sacc2[:], x1sq.ap().rearrange("p (c t) -> p t c",
                                                  c=NCH),
                    AX.X, ALU.add)
            p.add("vector", _chunksums2, waits=sc2w, inc="SC2")
            lnw = {"SC2": p.c("SC2"), "STT": p.c("STT"),
                   "LNS": p.c("LNS")}
            if i >= 1:
                lnw["LNR"] = p.c("LNR")
            def _ln2stats(t):
                t.matmul(stat3b[0:1, 0:TAU], onesc[:], sacc[:],
                         start=True, stop=True)
                return t.matmul(stat2b[0:1, TAU:2 * TAU], onesc[:], sacc2[:],
                                start=True, stop=True)
            p.add("tensor", _ln2stats, waits=lnw, inc="LNS")
            def _lnrows(v):
                v.tensor_scalar(rw["m2"][:], stat3b[0:1, 0:TAU], 1.0 / D,
                                None, ALU.mult)
                v.tensor_scalar(m2row[:], rw["m2"][:], 1.0, None,
                                ALU.mult)
                v.tensor_scalar(rw["v"][:], stat2b[0:1, TAU:2 * TAU], 1.0 / D,
                                None, ALU.mult)
                v.tensor_tensor(rw["t3"][:], rw["m2"][:], rw["m2"][:],
                                ALU.mult)
                return v.tensor_tensor(rw["v"][:], rw["v"][:], rw["t3"][:],
                                       ALU.subtract)
            rwW = {"LNS": p.c("LNS"), "RA": p.c("RA"), "BC1": p.c("BC1"),
                   "STm2": p.c("STm2"), "STr2": p.c("STr2")}
            p.add("vector", _lnrows, waits=rwW, inc="LNV")
            p.add("scalar",
                  lambda sc: sc.activation(rw["lnt"][:], rw["v"][:], AF.Sqrt,
                                           bias=epsA[:]),
                  waits={"LNV": p.c("LNV")}, inc="LNQ")
            p.add("vector",
                  lambda v: v.reciprocal(r2row[:], rw["lnt"][:]),
                  waits={"LNQ": p.c("LNQ")}, inc="LNR")

            p.dma("gpsimd",
                  (lambda t0=t0: lambda gp: gp.dma_start(
                      out=fm_dram(x1_out, t0, TAU),
                      in_=fm_sbuf(x1t)))(),
                  waits={"X1": p.c("X1")}, stream="STx1")
            p.dma("gpsimd",
                  (lambda t0=t0: lambda gp: gp.dma_start(
                      out=rows_out[0:1, t0:t0 + TAU], in_=m2row[:]))(),
                  waits={"LNR": p.c("LNR")}, stream="STm2")
            p.dma("gpsimd",
                  (lambda t0=t0: lambda gp: gp.dma_start(
                      out=rows_out[1:2, t0:t0 + TAU], in_=r2row[:]))(),
                  waits={"LNR": p.c("LNR")}, stream="STr2")

        if DEBUG_DUMPS and n_subtiles == 1 and not second:
            dbg = {
                "dbg_k1": (k1[0], BF16, NCH * TAU),
                "dbg_h": (hbuf, F32, NHC * TAU),
                "dbg_k2": (k2, BF16, NHC * TAU),
                "dbg_e1": (e1sb, F32, TAU),
                "dbg_e2": (e2sb, F32, TAU),
            }
            allw = {s: p.c(s) for s in ("X1", "K1", "K2", "HGD", "EC1",
                                        "EC2") if s in p.sems}
            for nm, (tt, dt, w) in dbg.items():
                od = nc.dram_tensor(nm, [128, w], dt, kind="ExternalOutput").ap()
                p.dma("gpsimd",
                      (lambda od=od, tt=tt, w=w: lambda gp: gp.dma_start(
                          out=od[:], in_=tt.ap()[:, 0:w] if hasattr(tt, 'ap')
                          else tt[:, 0:w]))(),
                      waits=dict(allw), stream="STdbg")
            rowd = nc.dram_tensor("dbg_rows", [16, TAU], F32,
                                  kind="ExternalOutput").ap()
            for ri, nm in enumerate(("m", "v", "r", "q", "R", "Mx", "Mn",
                                     "aa", "S1", "mS1", "e1", "h2", "q2",
                                     "aa2", "S2", "e2t")):
                p.dma("gpsimd",
                      (lambda nm=nm, ri=ri: lambda gp: gp.dma_start(
                          out=rowd[ri:ri + 1, :], in_=rw[nm][:]))(),
                      waits=dict(allw), stream="STdbg")

        fin = {s: p.c(s) for s in ("STx1", "STxb", "STm2", "STr2", "STdbg")
               if s in p.sems}
        p.add("gpsimd", lambda gp: None, waits=fin)
        p.realize()
    return nc



def build_lru_launch():
    nc = bass.Bass("TRN2", target_bir_lowering=False)
    es = ExitStack()
    dram = lambda name, shape, dt, kind: nc.dram_tensor(
        name, shape, dt, kind=kind).ap()
    x1_in = dram("x1_in", [NCH, 128, TPC], F32, "ExternalInput")
    rows_in = dram("rows_in", [2, TPC], F32, "ExternalInput")
    wit = dram("wit", [NCH, 128, D], F32, "ExternalInput")
    wgt = dram("wgt", [NCH, 128, D], F32, "ExternalInput")
    biv = dram("biv", [128, NCH], F32, "ExternalInput")
    bgv = dram("bgv", [128, NCH], F32, "ExternalInput")
    hloc_out = dram("hloc_out", [NCH, 128, TPC], F32, "ExternalOutput")
    p_out = dram("p_out", [NCH, 128, TPC], BF16, "ExternalOutput")
    carry_out = dram("carry_out", [128, NCH], F32, "ExternalOutput")

    with es:
        p = Prog(nc, es)
        sb = lambda *a: es.enter_context(nc.sbuf_tensor(*a))
        witb = sb("witb", [128, NCH * D], BF16)
        wgtb = sb("wgtb", [128, NCH * D], BF16)
        stage = sb("stage", [128, D], F32)
        onesf = sb("onesf", [1, 128], F32)
        bisb = sb("bisb", [128, NCH], F32)
        bgsb = sb("bgsb", [128, NCH], F32)
        onecol = sb("onecol", [128, 1], F32)
        zeros = sb("zeros", [128, TAU2], F32)
        x1bt = [sb(f"x1bt{i}", [128, NCH * TAU2], F32) for i in range(2)]
        rmt = [sb(f"rmt{i}", [1, TAU2], F32) for i in range(2)]
        rrt = [sb(f"rrt{i}", [1, TAU2], F32) for i in range(2)]
        xlf = sb("xlf", [128, TAU2], F32)
        xln = sb("xln", [128, NCH * TAU2], BF16)
        ur = [sb(f"ur{i}", [128, TAU2], F32) for i in range(2)]
        gr = [sb(f"gr{i}", [128, TAU2], F32) for i in range(2)]
        ogr = [sb(f"ogr{i}", [128, TAU2], F32) for i in range(2)]
        br = [sb(f"br{i}", [128, TAU2], F32) for i in range(2)]
        hb = [sb(f"hb{i}", [128, NCH * TAU2], F32) for i in range(2)]
        pb = [sb(f"pb{i}", [128, NCH * TAU2], F32) for i in range(2)]
        pbf = sb("pbf", [128, NCH * TAU2], BF16)
        hinit = [sb(f"hinit{c}", [128, 1], F32) for c in range(NCH)]
        pinit = [sb(f"pinit{c}", [128, 1], F32) for c in range(NCH)]


        pt = lambda *a: es.enter_context(nc.psum_tensor(*a))
        ups = pt("ups", [128, 2 * TAU2], F32)
        gps = pt("gps", [128, 2 * TAU2], F32)
        repm = pt("repm", [128, TAU2], F32)
        repr_ = pt("repr", [128, TAU2], F32)

        p.add("vector", lambda v: v.memset(onesf[:], 1.0))
        p.add("vector", lambda v: v.memset(onecol[:], 1.0))
        p.add("vector", lambda v: v.memset(zeros[:], 0.0), inc="init")
        p.dma("sync", lambda s: s.dma_start(out=bisb[:], in_=biv[:]),
              stream="ld0")
        p.dma("sync", lambda s: s.dma_start(out=bgsb[:], in_=bgv[:]),
              stream="ld0")
        for wi, (wt, wb) in enumerate(((wit, witb), (wgt, wgtb))):
            for kc in range(NCH):
                p.dma("sync",
                      (lambda wt=wt, kc=kc: lambda s: s.dma_start(
                          out=stage[:], in_=wt[kc]))(),
                      waits={"wcast": p.c("wcast")}, stream="wld")
                p.add("scalar",
                      (lambda wb=wb, kc=kc: lambda sc: sc.copy(
                          wb[:, kc * D:(kc + 1) * D], stage[:]))(),
                      waits={"wld": 16 * (p.c("wld") // 16)}, inc="wcast")

        def fm_dram(t, t0, n):
            return t.rearrange("c p t -> p c t")[:, :, t0:t0 + n]

        def fm_sbuf(t, tau):
            return t.ap().rearrange("p (c t) -> p c t", c=NCH)

        for j in range(NST2):
            t0 = j * TAU2
            sl = j % 2
            CS = lambda c: slice(c * TAU2, (c + 1) * TAU2)

            ldw = {}
            if j >= 2:
                ldw["XLN"] = p.c("XLN") - 1
                ldw["BCM"] = p.c("BCM") - 1
            p.dma("sync",
                  (lambda sl=sl, t0=t0: lambda s: s.dma_start(
                      out=fm_sbuf(x1bt[sl], TAU2),
                      in_=fm_dram(x1_in, t0, TAU2)))(),
                  waits=ldw, stream="Lx")
            p.dma("sync",
                  (lambda sl=sl, t0=t0: lambda s: s.dma_start(
                      out=rmt[sl][:], in_=rows_in[0:1, t0:t0 + TAU2]))(),
                  waits=dict(ldw), stream="Lm")
            p.dma("sync",
                  (lambda sl=sl, t0=t0: lambda s: s.dma_start(
                      out=rrt[sl][:], in_=rows_in[1:2, t0:t0 + TAU2]))(),
                  waits=dict(ldw), stream="Lr")
            lnow = {"Lx": 16 * (j + 1), "Lm": 16 * (j + 1),
                    "Lr": 16 * (j + 1)}

            bw = dict(lnow); bw["init"] = 1
            if j >= 1:
                bw["XLN"] = p.c("XLN")
            def _bc(t, sl=sl):
                t.matmul(repm[:], onesf[:], rmt[sl][:], start=True,
                         stop=True)
                return t.matmul(repr_[:], onesf[:], rrt[sl][:],
                                start=True, stop=True)
            p.add("tensor", _bc, waits=bw, inc="BCM")

            xw = {"BCM": p.c("BCM")}
            if j >= 1:
                xw["UMM"] = p.c("UMM")
            def _xln(v, sl=sl):
                last = None
                for c in range(NCH):
                    v.tensor_tensor(xlf[:], x1bt[sl][:, CS(c)], repm[:],
                                    ALU.subtract)
                    last = v.tensor_tensor(xln[:, CS(c)], xlf[:], repr_[:],
                                          ALU.mult)
                return last
            p.add("vector", _xln, waits=xw, inc="XLN")

            for oc in range(NCH):
                mw = {"XLN": p.c("XLN"), "wcast": 2 * NCH,
                      "UMM": p.c("UMM")}
                if oc >= 2 or j >= 1:
                    mw["UGD"] = p.c("UGD") - (1 if oc >= 2 else 0)
                def _umm(t, oc=oc):
                    for kc in range(NCH):
                        lt = witb[:, kc * D + oc * 128:kc * D + (oc + 1) * 128]
                        t.matmul(ups[:, (oc % 2) * TAU2:(oc % 2 + 1) * TAU2],
                                 lt, xln[:, CS(kc)], start=(kc == 0),
                                 stop=(kc == NCH - 1))
                    last = None
                    for kc in range(NCH):
                        lt = wgtb[:, kc * D + oc * 128:kc * D + (oc + 1) * 128]
                        last = t.matmul(
                            gps[:, (oc % 2) * TAU2:(oc % 2 + 1) * TAU2],
                            lt, xln[:, CS(kc)], start=(kc == 0),
                            stop=(kc == NCH - 1))
                    return last
                p.add("tensor", _umm, waits=mw, inc="UMM")

                def _acts(sc, oc=oc):
                    sc.activation(ur[oc % 2][:],
                                  ups[:, (oc % 2) * TAU2:(oc % 2 + 1) * TAU2],
                                  AF.Identity, bias=bisb[:, oc:oc + 1])
                    sc.activation(gr[oc % 2][:],
                                  gps[:, (oc % 2) * TAU2:(oc % 2 + 1) * TAU2],
                                  AF.Sigmoid, bias=bgsb[:, oc:oc + 1])
                    return sc.activation(ogr[oc % 2][:], gr[oc % 2][:],
                                         AF.Identity, bias=onecol[:],
                                         scale=-1.0)
                aw = {"UMM": p.c("UMM"), "ld0": 32}
                if j >= 1 or oc >= 2:
                    aw["SCN"] = p.c("SCN") - (1 if oc >= 2 else 0)
                p.add("scalar", _acts, waits=aw, inc="UGD")

                def _bin(v, oc=oc):
                    return v.tensor_tensor(br[oc % 2][:], ogr[oc % 2][:],
                                           ur[oc % 2][:], ALU.mult)
                p.add("vector", _bin, waits={"UGD": p.c("UGD")}, inc="BIN")

                def _scan(v, oc=oc, sl=sl, j=j):
                    if j == 0:
                        ih, ip = 0.0, 1.0
                    else:
                        ih, ip = hinit[oc][:], pinit[oc][:]
                    v.tensor_tensor_scan(hb[sl][:, CS(oc)], gr[oc % 2][:],
                                         br[oc % 2][:], ih, ALU.mult, ALU.add)
                    v.tensor_tensor_scan(pb[sl][:, CS(oc)],
                                         gr[oc % 2][:], zeros[:], ip,
                                         ALU.mult, ALU.add)
                    v.tensor_copy(hinit[oc][:],
                                  hb[sl][:, oc * TAU2 + TAU2 - 1:
                                         oc * TAU2 + TAU2])
                    return v.tensor_copy(pinit[oc][:],
                                         pb[sl][:, oc * TAU2 + TAU2 - 1:
                                                oc * TAU2 + TAU2])
                sw = {"BIN": p.c("BIN"), "init": 1}
                if j >= 2 and oc == 0:
                    sw["SH"] = p.c("SH") - 16
                    sw["SP"] = p.c("SP") - 16
                p.add("vector", _scan, waits=sw, inc="SCN")

            p.add("scalar", lambda sc: sc.copy(carry_sb0[:], carry_sb0[:])
                  if False else None, waits={"SCN": p.c("SCN")}, inc="HCP") \
                if False else None
            p.dma("gpsimd",
                  (lambda t0=t0, sl=sl: lambda gp: gp.dma_start(
                      out=fm_dram(hloc_out, t0, TAU2),
                      in_=fm_sbuf(hb[sl], TAU2)))(),
                  waits={"SCN": p.c("SCN")}, stream="SH")
            cwp = {"SCN": p.c("SCN")}
            if j >= 1:
                cwp["SP"] = p.c("SP")
            p.add("scalar",
                  (lambda sl=sl: lambda sc: sc.activation(
                      pbf[:], pb[sl][:], AF.Identity))(),
                  waits=cwp, inc="HCP")
            p.dma("gpsimd",
                  (lambda t0=t0, sl=sl: lambda gp: gp.dma_start(
                      out=fm_dram(p_out, t0, TAU2),
                      in_=fm_sbuf(pbf, TAU2)))(),
                  waits={"HCP": p.c("HCP")}, stream="SP")
            if j == NST2 - 1:
                carry_sb = sb("carry_sb", [128, NCH], F32)
                def _carrycp(v):
                    last = None
                    for c in range(NCH):
                        last = v.tensor_copy(carry_sb[:, c:c + 1],
                                             hinit[c][:])
                    return last
                p.add("vector", _carrycp,
                      waits={"SCN": p.c("SCN")}, inc="CCP")
                p.dma("gpsimd",
                      lambda gp: gp.dma_start(out=carry_out[:],
                                              in_=carry_sb[:]),
                      waits={"CCP": p.c("CCP")}, stream="SC")

        fin = {s: p.c(s) for s in ("SH", "SP", "SC") if s in p.sems}
        p.add("gpsimd", lambda gp: None, waits=fin)
        p.realize()
    return nc



def _prep_host(inputs):
    x = np.ascontiguousarray(np.asarray(inputs["L0_attn_x"], dtype=np.float32))
    w1 = np.asarray(inputs["L0_ffn_w1"], np.float32)
    w2 = np.asarray(inputs["L0_ffn_w2"], np.float32)
    wi = np.asarray(inputs["L0_attn_Wi"], np.float32)
    wg = np.asarray(inputs["L0_attn_Wg"], np.float32)
    wo = np.asarray(inputs["L0_attn_Wo"], np.float32)
    C = lambda a: np.ascontiguousarray(a)
    g = {
        "w1t": C(w1.T.reshape(NCH, 128, HID)),
        "w2t": C(w2.T.reshape(NHC, 128, D)),
        "wit": C(wi.T.reshape(NCH, 128, D)),
        "wgt": C(wg.T.reshape(NCH, 128, D)),
        "wot": C(wo.T.reshape(NCH, 128, D)),
        "b1v": C(np.asarray(inputs["L0_ffn_b1"], np.float32)
                 .reshape(NHC, 128).T),
        "b2v": C(np.asarray(inputs["L0_ffn_b2"], np.float32)
                 .reshape(NCH, 128).T),
        "biv": C(np.asarray(inputs["L0_attn_bi"], np.float32)
                 .reshape(NCH, 128).T),
        "bgv": C(np.asarray(inputs["L0_attn_bg"], np.float32)
                 .reshape(NCH, 128).T),
        "bov": C(np.asarray(inputs["L0_attn_bo"], np.float32)
                 .reshape(NCH, 128).T),
    }
    xs = []
    for c in range(NCORE):
        s, hc = c // 2, c % 2
        sl = x[s, hc * TPC:(hc + 1) * TPC, :]
        xs.append(C(sl.T.reshape(NCH, 128, TPC)))
    return g, xs


def kernel(**inputs):
    g, xs = _prep_host(inputs)
    cores = list(range(NCORE))

    nc1 = build_ffn_launch(second=False)
    maps1 = [{"w1t": g["w1t"], "w2t": g["w2t"], "b1v": g["b1v"],
              "b2v": g["b2v"], "x_in": xs[c]} for c in cores]
    r1 = run_bass_kernel_spmd(nc1, maps1, core_ids=cores).results

    nc2 = build_lru_launch()
    maps2 = [{"x1_in": r1[c]["x1_out"], "rows_in": r1[c]["rows_out"],
              "wit": g["wit"], "wgt": g["wgt"], "biv": g["biv"],
              "bgv": g["bgv"]} for c in cores]
    r2 = run_bass_kernel_spmd(nc2, maps2, core_ids=cores).results

    nc3 = build_ffn_launch(second=True)
    zcar = np.zeros((128, NCH), np.float32)
    maps3 = []
    for c in cores:
        car = zcar if c % 2 == 0 else r2[c - 1]["carry_out"]
        maps3.append({"w1t": g["w1t"], "w2t": g["w2t"], "b1v": g["b1v"],
                      "b2v": g["b2v"], "x1_in": r1[c]["x1_out"],
                      "hloc_in": r2[c]["hloc_out"], "p_in": r2[c]["p_out"],
                      "hcarry": np.ascontiguousarray(car, np.float32),
                      "wot": g["wot"], "bov": g["bov"]})
    r3 = run_bass_kernel_spmd(nc3, maps3, core_ids=cores).results

    out = np.empty((BATCH, SEQ, D), np.float32)
    for c in cores:
        s, hc = c // 2, c % 2
        o = np.asarray(r3[c]["out_fm"]).reshape(D, TPC)
        out[s, hc * TPC:(hc + 1) * TPC, :] = o.T
    return out


# version 07
# speedup vs baseline: 1.0282x; 1.0282x over previous
import numpy as np
import ml_dtypes
from contextlib import ExitStack

import concourse.bass as bass
import concourse.mybir as mybir
from concourse.bass_utils import run_bass_kernel_spmd

F32 = mybir.dt.float32
BF16 = mybir.dt.bfloat16
FP8 = mybir.dt.float8e4
AF = mybir.ActivationFunctionType
ALU = mybir.AluOpType
AX = mybir.AxisListType

D = 1024
HID = 4096
NCH = D // 128
NHC = HID // 128
SEQ = 8192
BATCH = 4
NCORE = 8
TPC = SEQ // 2
TAU = 256
NST = TPC // TAU
TAU2 = 512
NST2 = TPC // TAU2
RNDC = 1.5 * 2.0**23
EPS5 = 1e-5
EPS6 = 1e-6

ENGINES = ("sync", "scalar", "vector", "tensor", "gpsimd")

import os as _os
CHAIN_EDGES = _os.environ.get("BASS_SIM_FP64") == "1" or \
    _os.environ.get("KERNEL_FORCE_CHAIN_EDGES") == "1"
DEBUG_DUMPS = False



class Prog:

    def __init__(self, nc, es):
        self.nc = nc
        self.es = es
        self.ops = {e: [] for e in ENGINES}
        self.sems = {}
        self.cnt = {}

    def sem(self, name):
        if name not in self.sems:
            self.sems[name] = self.es.enter_context(self.nc.semaphore(name))
            self.cnt[name] = 0
        return name

    def add(self, eng, emit, waits=None, inc=None, amt=1):
        w = dict(waits or {})
        incs = []
        if inc is not None:
            self.sem(inc)
            self.cnt[inc] += amt
            incs = [(inc, amt)]
        for s in w:
            self.sem(s)
        self.ops[eng].append((w, emit, incs))

    def c(self, name):
        self.sem(name)
        return self.cnt[name]

    def dma(self, eng, emit, waits=None, stream="L"):
        w = dict(waits or {})
        self.sem(stream)
        w[stream] = self.cnt[stream]
        self.add(eng, emit, w, inc=stream, amt=16)

    def realize(self):
        with self.nc.Block() as blk:
            for eng in ENGINES:
                seq = self.ops[eng]
                if not seq:
                    continue
                chain_sem = None
                if CHAIN_EDGES and eng in ("vector", "scalar", "gpsimd"):
                    chain_sem = self.es.enter_context(
                        self.nc.semaphore(f"chain_{eng}"))

                def make(seq, chain_sem):
                    def body(e):
                        ee = _ChainEngine(e, chain_sem) if chain_sem else e
                        for (waits, emit, incs) in seq:
                            if chain_sem:
                                ee.flush_pending()
                            for s, v in waits.items():
                                if v > 0:
                                    e.wait_ge(self.sems[s], v)
                            ins = emit(ee)
                            for s, a in incs:
                                if ins is None:
                                    ins = e.engine_nop()
                                    if chain_sem:
                                        ee.chain_bare(ins)
                                if chain_sem:
                                    ee.take_inc(ins, self.sems[s], a)
                                else:
                                    ins.then_inc(self.sems[s], a)
                        if chain_sem:
                            ee.finalize()
                    return body
                getattr(blk, eng)(make(list(seq), chain_sem))


class _ChainEngine:

    _SKIP = {"dma_start", "wait_ge", "then_inc", "engine_nop", "bass",
             "value_load", "partition_id", "flush_pending", "take_inc",
             "finalize", "chain_bare"}

    def __init__(self, eng, sem):
        self._eng = eng
        self._sem = sem
        self._n = 0
        self._last = None
        self._pending = []
        self._reccnt = {}

    def _finalize_last(self):
        if self._last is not None:
            self._last.then_inc(self._sem, 1)
            self._n += 1
            self._last = None

    def flush_pending(self):
        for (s, v) in self._pending:
            self._eng.wait_ge(s, v)
        self._pending = []

    def take_inc(self, ins, sem_handle, amt):
        key = id(sem_handle)
        self._reccnt[key] = self._reccnt.get(key, 0) + amt
        if ins is self._last:
            self._last = None
            ins.then_inc(sem_handle, amt)
            self._pending.append((sem_handle, self._reccnt[key]))
        else:
            ins.then_inc(sem_handle, amt)

    def chain_bare(self, ins):
        self._finalize_last()
        if self._n > 0:
            ins._wait_ge(self._sem, self._n)

    def finalize(self):
        self._finalize_last()

    def __getattr__(self, name):
        f = getattr(self._eng, name)
        if not callable(f) or name.startswith("_") or name in self._SKIP:
            return f

        def wrap(*a, **k):
            self._finalize_last()
            self.flush_pending()
            r = f(*a, **k)
            if hasattr(r, "then_inc"):
                if self._n > 0:
                    r._wait_ge(self._sem, self._n)
                self._last = r
            return r
        return wrap


def _ap(t):
    a = getattr(t, "ap", None)
    return t.ap() if callable(a) else t


def emit_weight_quant(p, sb, g, stage, wabs):
    nc = p.nc
    w1t, w2t = g["w1t"], g["w2t"]
    w1q = sb("w1q", [128, NCH * HID], FP8)
    w2q = sb("w2q", [128, NHC * D], FP8)
    acc = sb("wacc", [128, NCH + NHC], F32)
    accr = sb("waccr", [128, 2], F32)
    rowt = sb("wrowt", [1, 2], F32)
    mwc = sb("mwc", [1, 2], F32)
    ccol = sb("ccol", [128, 2], F32)
    onesf = sb("onesf", [1, 128], F32)
    onesb = sb("onesb", [128, 1], BF16)
    onesc = sb("onesc", [128, 1], F32)
    with nc.psum_tensor("cps", [128, 2], F32) as cps:
        pass

    plans = [(w1t, NCH, HID, 0, 127.0), (w2t, NHC, D, 1, 254.0)]
    p.add("vector", lambda v: v.memset(onesf[:], 1.0))
    p.add("vector", lambda v: v.memset(onesb[:], 1.0))
    p.add("vector", lambda v: v.memset(onesc[:], 1.0), inc="wq_init")

    for (wt, n, flen, wi, div) in plans:
        col0 = 0 if wi == 0 else NCH
        for k in range(n):
            p.dma("sync",
                  (lambda wt=wt, k=k, flen=flen:
                   lambda s: s.dma_start(out=stage[:, 0:flen], in_=wt[k],
                                         max_dma_last_dim=2048))(),
                  waits={"wq_abs": max(0, p.c("wq_abs"))},
                  stream="wq_ld")
            p.add("vector",
                  (lambda k=k, flen=flen, col0=col0:
                   lambda v: v.tensor_reduce(
                       acc[:, col0 + k:col0 + k + 1], stage[:, 0:flen],
                       AX.X, ALU.add, apply_absolute_value=True))(),
                  waits={"wq_ld": p.c("wq_ld")},
                  inc="wq_abs")
        total = 128.0 * n * flen
        p.add("vector",
              (lambda n=n, wi=wi, col0=col0: lambda v: v.tensor_reduce(
                  accr[:, wi:wi + 1], acc[:, col0:col0 + n], AX.X,
                  ALU.add))(),
              waits={"wq_abs": p.c("wq_abs")}, inc="wq_red")
        p.add("gpsimd",
              (lambda wi=wi: lambda gp: gp.tensor_reduce(
                  rowt[0:1, wi:wi + 1], accr[:, wi:wi + 1], AX.C, ALU.add))(),
              waits={"wq_red": p.c("wq_red")}, inc="wq_gp")
        p.add("vector",
              (lambda wi=wi, total=total: lambda v: v.tensor_scalar(
                  rowt[0:1, wi:wi + 1], rowt[0:1, wi:wi + 1],
                  1.0 / total, EPS5, ALU.mult, ALU.max))(),
              waits={"wq_gp": p.c("wq_gp")}, inc="wq_rmA")
        p.add("vector",
              (lambda wi=wi, div=div: lambda v: v.tensor_scalar(
                  mwc[0:1, wi:wi + 1], rowt[0:1, wi:wi + 1],
                  1.0 / div, None, ALU.mult))(),
              waits={"wq_rmA": p.c("wq_rmA")}, inc="wq_rmB")
        p.add("vector",
              (lambda wi=wi: lambda v: v.reciprocal(
                  rowt[0:1, wi:wi + 1], rowt[0:1, wi:wi + 1]))(),
              waits={"wq_rmB": p.c("wq_rmB")}, inc="wq_rm")
        p.add("tensor",
              (lambda wi=wi: lambda t: t.matmul(
                  cps[:, wi:wi + 1], onesf[:], rowt[0:1, wi:wi + 1],
                  start=True, stop=True))(),
              waits={"wq_rm": p.c("wq_rm"), "wq_init": 1,
                     "wq_bc": p.c("wq_bc")}, inc="wq_bc")
        p.add("scalar",
              (lambda wi=wi: lambda sc: sc.copy(ccol[:, wi:wi + 1],
                                                cps[:, wi:wi + 1]))(),
              waits={"wq_bc": p.c("wq_bc")}, inc="wq_cc")

    for (wt, n, flen, wi, div) in plans:
        wq = w1q if wi == 0 else w2q
        for k in range(n):
            ldg = {"wq_q": p.c("wq_q")}
            if wi == 0 and k == 0:
                ldg["wq_abs"] = NCH + NHC
            p.dma("sync",
                  (lambda wt=wt, k=k, flen=flen:
                   lambda s: s.dma_start(out=stage[:, 0:flen], in_=wt[k],
                                         max_dma_last_dim=2048))(),
                  waits=ldg, stream="wq_ld")
            def _quant(v, k=k, flen=flen, wi=wi, wq=wq):
                v.tensor_scalar(stage[:, 0:flen], stage[:, 0:flen],
                                ccol[:, wi:wi + 1], RNDC, ALU.mult, ALU.add)
                v.tensor_scalar(stage[:, 0:flen], stage[:, 0:flen],
                                RNDC, 1.0, ALU.subtract, ALU.min)
                return v.tensor_scalar(wq[:, k * flen:(k + 1) * flen],
                                       stage[:, 0:flen], -1.0, None, ALU.max)
            p.add("vector", _quant,
                  waits={"wq_ld": p.c("wq_ld"), "wq_cc": 1 + wi},
                  inc="wq_q")
    return dict(w1q=w1q, w2q=w2q, mwc=mwc, onesf=onesf, onesb=onesb,
                onesc=onesc,
                wq_done={"wq_q": p.c("wq_q")})



def emit_ffn_phase(p, nc, pes, W, io, second, n_subtiles=NST,
                   debug_stage=99, gate=None):
    gate = dict(gate or {})
    b1v, b2v = io["b1v"], io["b2v"]
    if not second:
        x_in, x1_out, rows_out = io["x_in"], io["x1_out"], io["rows_out"]
    else:
        x1_in, hloc_in, p_in = io["x1_in"], io["hloc_in"], io["p_in"]
        bov, out_fm = io["bov"], io["out_fm"]
        hcar, wotb = io["hcar"], io["wotb"]
    tag = "fb_" if second else "fa_"
    if True:
        sb = lambda nm, *a: pes.enter_context(nc.sbuf_tensor(tag + nm, *a))
        hbuf = io["bigbuf"].ap()[:, 0:NHC * TAU]
        w1q, w2q = W["w1q"], W["w2q"]
        onesf, onesb, mwc = W["onesf"], W["onesb"], W["mwc"]
        onesc = W["onesc"]

        if not second:
            xt = [sb(f"xt{i}", [128, NCH * TAU], F32) for i in range(2)]
        else:
            xt = [sb("xt0", [128, NCH * TAU], F32)] * 2
        xsq = [sb("xsq0", [128, NCH * TAU], F32)] * 2
        sacc = sb("sacc", [128, TAU], F32)
        sacc2 = sb("sacc2", [128, TAU], F32)
        qtmp2 = sacc
        if not second:
            k1 = [sb(f"k1_{i}", [128, NCH * TAU], BF16) for i in range(2)]
        else:
            k1 = [sb("k1_0", [128, NCH * TAU], BF16)] * 2
        qtmp = None
        hsq = [sb(f"hsq{i}", [128, TAU], F32) for i in range(2)]
        htmp = [sb(f"htmp{i}", [128, TAU], F32) for i in range(2)]
        k2 = sb("k2", [128, NHC * TAU], BF16)
        x1t = sb("x1t", [128, NCH * TAU], F32)
        if not second:
            x1sq = sb("x1sq", [128, NCH * TAU], F32)
        cmx = sb("cmx", [128, TAU], F32)
        cmn = sb("cmn", [128, TAU], F32)
        hmax = cmx
        e1sb = sb("e1sb", [128, TAU], F32)
        e2sb = sb("e2sb", [128, TAU], F32)
        b1sb = sb("b1sb", [128, NHC], F32)
        b2sb = sb("b2sb", [128, NCH], F32)
        if not second:
            m2row = sb("m2row", [1, TAU], F32)
            r2row = sb("r2row", [1, TAU], F32)
        epsA = sb("epsA", [1, 1], F32)
        epsB = sb("epsB", [1, 1], F32)
        rw = {}
        for grp in (("m",), ("e2", "rr", "t4"), ("R", "A2"), ("t3",),
                    ("M1", "a2"), ("a", "aa2"), ("aa", "Mn"), ("s", "s2"),
                    ("Mx", "m2"), ("v",), ("r",),
                    ("vr", "h2", "lnt"), ("q", "q2"), ("S1", "S2"),
                    ("mS1", "e2t"), ("e1",)):
            t = sb(f"rw_{grp[0]}", [1, TAU], F32).ap()
            for n in grp:
                rw[n] = t
        if second:
            x1in_t = [x1t] * 2
            hl_t = [sb("hl0", [128, NCH * TAU], F32)] * 2
            p_t = [sb("pt0", [128, NCH * TAU], BF16)] * 2
            hfix = [sb("hfix0", [128, NCH * TAU], BF16)] * 2
            bosb = sb("bosb", [128, NCH], F32)

        pt = lambda nm, *a: pes.enter_context(nc.psum_tensor(tag + nm, *a))
        hps = [pt(f"hps{i}", [128, TAU], F32) for i in range(2)]
        ops = [pt(f"ops{i}", [128, TAU], F32) for i in range(2)]
        statb = pt("statb", [1, 2 * TAU], F32)
        stat2b = pt("stat2b", [1, 2 * TAU], F32)
        stat3b = statb
        repA = pt("repA", [128, 2 * TAU], F32)
        repB = pt("repB", [128, 2 * TAU], F32)

        p.dma("sync", lambda s: s.dma_start(out=b1sb[:], in_=b1v[:]),
              waits=dict(gate), stream="ld0")
        p.dma("sync", lambda s: s.dma_start(out=b2sb[:], in_=b2v[:]),
              waits=dict(gate), stream="ld0")
        if second:
            p.dma("sync", lambda s: s.dma_start(out=bosb[:], in_=bov[:]),
                  waits=dict(gate), stream="ld0")
        p.add("vector", lambda v: v.memset(epsA[:], EPS5))
        p.add("vector", lambda v: v.memset(epsB[:], EPS6))
        p.add("scalar", lambda sc: sc.mul(b2sb[:], b2sb[:], 0.5),
              waits={"ld0": p.c("ld0")}, inc="bias_ok")
        BOK = p.c("bias_ok")

        x_src = x_in if not second else x1_in

        def fm_dram(t, t0, n):
            return t.rearrange("c p t -> p c t")[:, :, t0:t0 + n]

        def fm_sbuf(t):
            return _ap(t).rearrange("p (c t) -> p c t", c=NCH)

        for i in range(n_subtiles):
            t0 = i * TAU
            sl = i % 2
            CS = lambda c: slice(c * TAU, (c + 1) * TAU)

            ldw = {}
            if i == 0:
                ldw.update(gate)
            if i >= 2:
                ldw["X1"] = p.c("X1") - 1
                ldw["STT"] = p.c("STT") - 1
            if not second:
                p.dma("sync",
                      (lambda sl=sl, t0=t0: lambda s: s.dma_start(
                          out=fm_sbuf(xt[sl]), in_=fm_dram(x_src, t0, TAU)))(),
                      waits=ldw, stream="Lx")
            else:
                if i >= 1:
                    ldw["FXD"] = p.c("FXD")
                    ldw["OPC"] = p.c("OPC") - 8
                    ldw["STx1"] = p.c("STx1")
                p.dma("sync",
                      (lambda sl=sl, t0=t0: lambda s: s.dma_start(
                          out=fm_sbuf(x1in_t[sl]),
                          in_=fm_dram(x1_in, t0, TAU)))(),
                      waits=ldw, stream="Lx")
                p.dma("sync",
                      (lambda sl=sl, t0=t0: lambda s: s.dma_start(
                          out=fm_sbuf(hl_t[sl]),
                          in_=fm_dram(hloc_in, t0, TAU)))(),
                      waits=dict(ldw), stream="Lh")
                p.dma("sync",
                      (lambda sl=sl, t0=t0: lambda s: s.dma_start(
                          out=fm_sbuf(p_t[sl]),
                          in_=fm_dram(p_in, t0, TAU)))(),
                      waits=dict(ldw), stream="Lp")
            l_now = {"Lx": p.c("Lx")}
            if second:
                l_now.update({"Lh": p.c("Lh"), "Lp": p.c("Lp")})

            if second:
                def _fixup(v, sl=sl):
                    last = None
                    for c in range(NCH):
                        last = v.scalar_tensor_tensor(
                            hfix[sl][:, CS(c)], p_t[sl][:, CS(c)],
                            hcar[:, c:c + 1], hl_t[sl][:, CS(c)],
                            ALU.mult, ALU.add)
                    return last
                fw = dict(l_now)
                fw["HCAR"] = p.c("HCAR")
                if i == 0:
                    fw.update(gate)
                if i >= 1:
                    fw["WOMM"] = p.c("WOMM")
                p.add("vector", _fixup, waits=fw, inc="FXD")
                for oc in range(NCH):
                    ww = {"FXD": p.c("FXD"), "wo_cast": NCH}
                    if i > 0 or oc >= 2:
                        ww["OPC"] = p.c("OPC") - 1 if (i > 0 or oc >= 2) else 0
                    def _womm(t, oc=oc, sl=sl):
                        last = None
                        for kc in range(NCH):
                            lt = wotb[:, kc * D + oc * 128:
                                      kc * D + (oc + 1) * 128]
                            last = t.matmul(
                                ops[oc % 2][:],
                                lt, hfix[sl][:, CS(kc)],
                                start=(kc == 0), stop=(kc == NCH - 1))
                        return last
                    p.add("tensor", _womm, waits=ww, inc="WOMM")
                    xw = {"WOMM": p.c("WOMM"), "bias_ok": BOK}
                    if i >= 1:
                        xw["X1"] = p.c("X1")
                    p.add("vector",
                          (lambda oc=oc, sl=sl: lambda v:
                           v.scalar_tensor_tensor(
                               xt[sl][:, CS(oc)],
                               ops[oc % 2][:],
                               bosb[:, oc:oc + 1], x1in_t[sl][:, CS(oc)],
                               ALU.add, ALU.add))(),
                          waits=xw, inc="OPC")
                xready = {"OPC": p.c("OPC")}
            else:
                xready = dict(l_now)

            castw = dict(xready)
            if i == 0:
                castw.update(gate)
            if i >= 1:
                castw["STT"] = p.c("STT")
            p.add("scalar",
                  (lambda sl=sl: lambda sc: sc.activation(
                      xsq[sl][:], xt[sl][:], AF.Square))(),
                  waits=castw, inc="SC1")
            scw = {"SC1": p.c("SC1")}
            scw.update(xready)
            if i >= 1:
                scw["STT"] = p.c("STT")
            if i >= 1 and not second:
                scw["LNS"] = p.c("LNS")
            def _chunksums(v, sl=sl):
                v.tensor_reduce(sacc[:],
                                xt[sl].ap().rearrange("p (c t) -> p t c",
                                                      c=NCH),
                                AX.X, ALU.add)
                return v.tensor_reduce(
                    sacc2[:], xsq[sl].ap().rearrange("p (c t) -> p t c",
                                                     c=NCH),
                    AX.X, ALU.add)
            p.add("vector", _chunksums, waits=scw, inc="SCC")
            if debug_stage <= 1:
                continue
            stw = {"SCC": p.c("SCC"), "wq_init": 1, "STT": p.c("STT")}
            if i >= 1:
                stw["RA"] = p.c("RA")
                stw["LNV"] = p.c("LNV")
            def _ln1stats(t, sl=sl):
                t.matmul(statb[0:1, 0:TAU], onesc[:], sacc[:],
                         start=True, stop=True)
                return t.matmul(statb[0:1, TAU:2 * TAU], onesc[:], sacc2[:],
                                start=True, stop=True)
            p.add("tensor", _ln1stats, waits=stw, inc="STT")

            mmw = dict(xready)
            if i == 0:
                mmw.update(gate)
            if i >= 1:
                mmw["GP1"] = p.c("GP1")
                mmw["GP2"] = p.c("GP2")
            def _cmaxmin(v, sl=sl):
                xv = xt[sl].ap().rearrange("p (c t) -> p t c", c=NCH)
                v.tensor_reduce(cmx[:], xv, AX.X, ALU.max)
                return v.tensor_reduce(cmn[:], xv, AX.X, ALU.min, negate=True)
            p.add("vector", _cmaxmin, waits=mmw, inc="MMR")
            p.add("gpsimd",
                  lambda gp: gp.tensor_reduce(rw["Mx"][:], cmx[:], AX.C,
                                              ALU.max),
                  waits={"MMR": p.c("MMR")})
            p.add("gpsimd",
                  lambda gp: gp.tensor_reduce(rw["Mn"][:], cmn[:], AX.C,
                                              ALU.max),
                  inc="GP1")

            raw = {"STT": p.c("STT"), "GP1": p.c("GP1")}
            if i >= 1:
                raw["BC1"] = p.c("BC1")
                raw["LNR"] = p.c("LNR")
            def _rmA(v):
                v.tensor_scalar(rw["m"][:], statb[0:1, 0:TAU], 1.0 / D, None,
                                ALU.mult)
                v.tensor_scalar(rw["e2"][:], statb[0:1, TAU:2 * TAU], 1.0 / D,
                                None, ALU.mult)
                v.tensor_tensor(rw["v"][:], rw["m"][:], rw["m"][:], ALU.mult)
                return v.tensor_tensor(rw["v"][:], rw["e2"][:], rw["v"][:],
                                       ALU.subtract)
            p.add("vector", _rmA, waits=raw, inc="RA")
            p.add("scalar",
                  lambda sc: sc.activation(rw["r"][:], rw["v"][:], AF.Sqrt,
                                           bias=epsA[:]),
                  waits={"RA": p.c("RA")}, inc="RB")
            def _rmB(v):
                v.reciprocal(rw["r"][:], rw["r"][:])
                v.tensor_tensor(rw["rr"][:], rw["r"][:], rw["r"][:], ALU.mult)
                return v.tensor_tensor(rw["vr"][:], rw["v"][:], rw["rr"][:],
                                       ALU.mult)
            p.add("vector", _rmB, waits={"RB": p.c("RB")}, inc="RC")
            p.add("scalar",
                  lambda sc: sc.activation(rw["q"][:], rw["vr"][:], AF.Sqrt,
                                           bias=epsB[:]),
                  waits={"RC": p.c("RC")}, inc="RD")
            def _rmC(v):
                v.reciprocal(rw["q"][:], rw["q"][:])
                v.tensor_tensor(rw["R"][:], rw["r"][:], rw["q"][:], ALU.mult)
                v.tensor_tensor(rw["t3"][:], rw["Mx"][:], rw["m"][:],
                                ALU.subtract)
                v.tensor_tensor(rw["t4"][:], rw["m"][:], rw["Mn"][:],
                                ALU.add)
                v.tensor_tensor(rw["M1"][:], rw["t3"][:], rw["t4"][:], ALU.max)
                v.tensor_tensor(rw["a"][:], rw["R"][:], rw["M1"][:], ALU.mult)
                v.tensor_scalar(rw["aa"][:], rw["a"][:], EPS5, None, ALU.max)
                v.reciprocal(rw["s"][:], rw["aa"][:])
                v.tensor_scalar(rw["s"][:], rw["s"][:], 127.0, None, ALU.mult)
                v.tensor_tensor(rw["S1"][:], rw["R"][:], rw["s"][:], ALU.mult)
                v.tensor_tensor(rw["mS1"][:], rw["m"][:], rw["S1"][:],
                                ALU.mult)
                return v.tensor_scalar(rw["e1"][:], rw["aa"][:],
                                       mwc[0:1, 0:1], None, ALU.mult)
            p.add("vector", _rmC, waits={"RD": p.c("RD")}, inc="RM1")

            bcw = {"RM1": p.c("RM1"), "BC1": p.c("BC1")}
            if i >= 1:
                bcw["K1"] = p.c("K1")
                bcw["EC2"] = p.c("EC2")
                bcw["K2"] = p.c("K2")
            def _bc1(t):
                t.matmul(repA[:, 0:TAU], onesf[:], rw["S1"][:], start=True,
                         stop=True)
                t.matmul(repA[:, TAU:2 * TAU], onesf[:], rw["mS1"][:],
                         start=True, stop=True)
                return t.matmul(repB[:, 0:TAU], onesf[:], rw["e1"][:],
                                start=True, stop=True)
            p.add("tensor", _bc1, waits=bcw, inc="BC1")
            ecw = {"BC1": p.c("BC1")}
            if i >= 1:
                ecw["HEV"] = p.c("HEV")
                ecw["K2"] = p.c("K2")
            p.add("scalar",
                  lambda sc: sc.copy(e1sb[:], repB[:, 0:TAU]),
                  waits=ecw, inc="EC1")

            k1w = {"BC1": p.c("BC1")}
            k1w.update(xready)
            if second and i >= 1:
                k1w["MM1"] = p.c("MM1")
            elif i >= 2:
                k1w["MM1"] = p.c("MM1") - NHC
            def _k1(v, sl=sl):
                last = None
                for c in range(NCH):
                    v.tensor_tensor(sacc[:], xt[sl][:, CS(c)], repA[:, 0:TAU],
                                    ALU.mult)
                    v.tensor_tensor(sacc[:], sacc[:], repA[:, TAU:2 * TAU],
                                    ALU.subtract)
                    v.tensor_scalar(sacc[:], sacc[:], RNDC, RNDC, ALU.add,
                                    ALU.subtract)
                    last = v.tensor_scalar(k1[sl][:, CS(c)], sacc[:], 127.0,
                                           -128.0, ALU.min, ALU.max)
                return last
            if debug_stage <= 2:
                continue
            p.add("vector", _k1, waits=k1w, inc="K1")
            if debug_stage <= 3:
                continue

            base_hev = p.c("HEV")
            base_hgd = p.c("HGD")
            for mc in range(NHC):
                w1w = {"K1": p.c("K1")}
                w1w.update(W["wq_done"] if i == 0 and mc == 0 else {})
                if mc >= 2:
                    w1w["HEV"] = base_hev + mc - 1
                elif i >= 1:
                    w1w["HEV"] = base_hev
                if mc < 2:
                    w1w["MM1"] = p.c("MM1")
                def _mm1(t, mc=mc, sl=sl):
                    last = None
                    for kc in range(NCH):
                        lt = w1q[:, kc * HID + mc * 128:
                                 kc * HID + (mc + 1) * 128]
                        last = t.matmul(
                            hps[mc % 2][:], lt,
                            k1[sl][:, CS(kc)],
                            start=(kc == 0), stop=(kc == NCH - 1))
                    return last
                p.add("tensor", _mm1, waits=w1w, inc="MM1")
                hevw = {"MM1": p.c("MM1"), "EC1": p.c("EC1")}
                if i >= 1 and mc < 2:
                    hevw["HGD"] = base_hgd
                p.add("vector",
                      (lambda mc=mc: lambda v: v.tensor_tensor(
                          htmp[mc % 2][:],
                          hps[mc % 2][:],
                          e1sb[:], ALU.mult))(),
                      waits=hevw, inc="HEV")
                gw = {"HEV": p.c("HEV")}
                if i >= 1:
                    gw["K2"] = p.c("K2")
                    gw["AMX"] = p.c("AMX")
                def _gelu(sc, mc=mc):
                    sc.activation(hbuf[:, mc * TAU:(mc + 1) * TAU],
                                  htmp[mc % 2][:], AF.Gelu,
                                  bias=b1sb[:, mc:mc + 1])
                    return sc.activation(hsq[mc % 2][:],
                                         hbuf[:, mc * TAU:(mc + 1) * TAU],
                                         AF.Square)
                p.add("scalar", _gelu, waits=gw, inc="HGD")
                sqw = {"HGD": p.c("HGD")}
                if mc == 0 and i >= 1:
                    sqw["RE"] = p.c("RE")
                    sqw["LNV"] = p.c("LNV")
                p.add("tensor",
                      (lambda mc=mc: lambda t: t.matmul(
                          stat2b[0:1, 0:TAU], onesc[:], hsq[mc % 2][:],
                          start=(mc == 0), stop=(mc == NHC - 1)))(),
                      waits=sqw, inc="SQS")

            if debug_stage <= 4:
                continue
            p.add("vector",
                  lambda v: v.tensor_reduce(
                      hmax[:], _ap(hbuf).rearrange("p (c t) -> p t c", c=NHC),
                      AX.X, ALU.max, apply_absolute_value=True),
                  waits={"HGD": p.c("HGD"), "GP1": p.c("GP1")},
                  inc="AMX")
            p.add("gpsimd",
                  lambda gp: gp.tensor_reduce(rw["A2"][:], hmax[:], AX.C,
                                              ALU.max),
                  waits={"AMX": p.c("AMX")}, inc="GP2")
            p.add("vector",
                  lambda v: v.tensor_scalar(rw["h2"][:], stat2b[0:1, 0:TAU],
                                            1.0 / HID, None, ALU.mult),
                  waits={"SQS": p.c("SQS")}, inc="RE")
            p.add("scalar",
                  lambda sc: sc.activation(rw["q2"][:], rw["h2"][:], AF.Sqrt,
                                           bias=epsB[:]),
                  waits={"RE": p.c("RE")}, inc="RF")
            def _rm2(v):
                v.reciprocal(rw["q2"][:], rw["q2"][:])
                v.tensor_tensor(rw["a2"][:], rw["q2"][:], rw["A2"][:],
                                ALU.mult)
                v.tensor_scalar(rw["aa2"][:], rw["a2"][:], EPS5, None,
                                ALU.max)
                v.reciprocal(rw["s2"][:], rw["aa2"][:])
                v.tensor_scalar(rw["s2"][:], rw["s2"][:], 127.0, None,
                                ALU.mult)
                v.tensor_tensor(rw["S2"][:], rw["q2"][:], rw["s2"][:],
                                ALU.mult)
                return v.tensor_scalar(rw["e2t"][:], rw["aa2"][:],
                                       mwc[0:1, 1:2], None, ALU.mult)
            p.add("vector", _rm2, waits={"RF": p.c("RF"), "GP2": p.c("GP2")},
                  inc="RM2")
            bc2w = {"RM2": p.c("RM2"), "EC1": p.c("EC1"),
                    "BC2": p.c("BC2")}
            if i >= 1:
                bc2w["K2"] = p.c("K2")
            def _bc2(t):
                t.matmul(repB[:, TAU:2 * TAU], onesf[:], rw["S2"][:],
                         start=True, stop=True)
                return t.matmul(repB[:, 0:TAU], onesf[:], rw["e2t"][:],
                                start=True, stop=True)
            p.add("tensor", _bc2, waits=bc2w, inc="BC2")
            ec2w = {"BC2": p.c("BC2")}
            if i >= 1:
                ec2w["OPC"] = p.c("OPC")
            p.add("scalar",
                  lambda sc: sc.copy(e2sb[:], repB[:, 0:TAU]),
                  waits=ec2w, inc="EC2")

            if debug_stage <= 5:
                continue
            base_mm2 = p.c("MM2")
            k2w = {"BC2": p.c("BC2"), "EC2": p.c("EC2")}
            if i >= 1:
                k2w["MM2"] = base_mm2
            def _k2(v):
                last = None
                for hc in range(NHC):
                    v.tensor_tensor(sacc[:],
                                    hbuf[:, hc * TAU:(hc + 1) * TAU],
                                    repB[:, TAU:2 * TAU], ALU.mult)
                    v.tensor_scalar(sacc[:], sacc[:], RNDC, RNDC,
                                    ALU.add, ALU.subtract)
                    last = v.tensor_scalar(
                        k2[:, hc * TAU:(hc + 1) * TAU],
                        sacc[:], 127.0, -128.0, ALU.min, ALU.max)
                return last
            p.add("vector", _k2, waits=k2w, inc="K2")

            if debug_stage <= 6:
                continue
            for oc in range(NCH):
                m2w = {"K2": p.c("K2")}
                if i == 0 and oc == 0:
                    m2w.update(W["wq_done"])
                if oc >= 2:
                    m2w["OPC"] = p.c("OPC") - 1
                elif second or i >= 1:
                    m2w["OPC"] = p.c("OPC")
                def _mm2(t, oc=oc):
                    last = None
                    for kc in range(NHC):
                        lt = w2q[:, kc * D + oc * 128:kc * D + (oc + 1) * 128]
                        last = t.matmul(
                            ops[oc % 2][:], lt,
                            k2[:, kc * TAU:(kc + 1) * TAU],
                            start=(kc == 0), stop=(kc == NHC - 1))
                    return last
                p.add("tensor", _mm2, waits=m2w, inc="MM2")
                ew = {"MM2": p.c("MM2"), "EC2": p.c("EC2"), "bias_ok": BOK}
                if i >= 1 and oc == 0:
                    ew["STx1"] = p.c("STx1")
                    ew["SQMs"] = p.c("SQMs")
                    ew["SC2"] = p.c("SC2")
                def _epi(v, oc=oc, sl=sl):
                    v.tensor_tensor(x1t[:, CS(oc)],
                                    ops[oc % 2][:],
                                    e2sb[:], ALU.mult)
                    return v.scalar_tensor_tensor(
                        x1t[:, CS(oc)], x1t[:, CS(oc)], b2sb[:, oc:oc + 1],
                        xt[sl][:, CS(oc)], ALU.add, ALU.add)
                p.add("vector", _epi, waits=ew, inc="OPC")
            p.add("vector", lambda v: None, waits={"OPC": p.c("OPC")},
                  inc="X1")

            if debug_stage <= 7:
                continue
            if second:
                p.dma("gpsimd",
                      (lambda t0=t0: lambda gp: gp.dma_start(
                          out=fm_dram(out_fm, t0, TAU),
                          in_=fm_sbuf(x1t)))(),
                      waits={"X1": p.c("X1")}, stream="STx1")
                continue
            cguard = {"X1": p.c("X1")}
            if i >= 1:
                cguard["LNS"] = p.c("LNS")
            p.add("scalar",
                  lambda sc: sc.activation(x1sq[:], x1t[:], AF.Square),
                  waits=dict(cguard), inc="SQMs")
            sc2w = {"SQMs": p.c("SQMs"), "X1": p.c("X1"),
                    "STT": p.c("STT")}
            def _chunksums2(v):
                v.tensor_reduce(sacc[:],
                                x1t.ap().rearrange("p (c t) -> p t c",
                                                   c=NCH),
                                AX.X, ALU.add)
                return v.tensor_reduce(
                    sacc2[:], x1sq.ap().rearrange("p (c t) -> p t c",
                                                  c=NCH),
                    AX.X, ALU.add)
            p.add("vector", _chunksums2, waits=sc2w, inc="SC2")
            lnw = {"SC2": p.c("SC2"), "STT": p.c("STT"),
                   "LNS": p.c("LNS")}
            if i >= 1:
                lnw["LNR"] = p.c("LNR")
            def _ln2stats(t):
                t.matmul(stat3b[0:1, 0:TAU], onesc[:], sacc[:],
                         start=True, stop=True)
                return t.matmul(stat2b[0:1, TAU:2 * TAU], onesc[:], sacc2[:],
                                start=True, stop=True)
            p.add("tensor", _ln2stats, waits=lnw, inc="LNS")
            def _lnrows(v):
                v.tensor_scalar(rw["m2"][:], stat3b[0:1, 0:TAU], 1.0 / D,
                                None, ALU.mult)
                v.tensor_scalar(m2row[:], rw["m2"][:], 1.0, None,
                                ALU.mult)
                v.tensor_scalar(rw["v"][:], stat2b[0:1, TAU:2 * TAU], 1.0 / D,
                                None, ALU.mult)
                v.tensor_tensor(rw["t3"][:], rw["m2"][:], rw["m2"][:],
                                ALU.mult)
                return v.tensor_tensor(rw["v"][:], rw["v"][:], rw["t3"][:],
                                       ALU.subtract)
            rwW = {"LNS": p.c("LNS"), "RA": p.c("RA"), "BC1": p.c("BC1"),
                   "STm2": p.c("STm2"), "STr2": p.c("STr2")}
            p.add("vector", _lnrows, waits=rwW, inc="LNV")
            p.add("scalar",
                  lambda sc: sc.activation(rw["lnt"][:], rw["v"][:], AF.Sqrt,
                                           bias=epsA[:]),
                  waits={"LNV": p.c("LNV")}, inc="LNQ")
            p.add("vector",
                  lambda v: v.reciprocal(r2row[:], rw["lnt"][:]),
                  waits={"LNQ": p.c("LNQ")}, inc="LNR")

            p.dma("gpsimd",
                  (lambda t0=t0: lambda gp: gp.dma_start(
                      out=fm_dram(x1_out, t0, TAU),
                      in_=fm_sbuf(x1t)))(),
                  waits={"X1": p.c("X1")}, stream="STx1")
            p.dma("gpsimd",
                  (lambda t0=t0: lambda gp: gp.dma_start(
                      out=rows_out[0:1, t0:t0 + TAU], in_=m2row[:]))(),
                  waits={"LNR": p.c("LNR")}, stream="STm2")
            p.dma("gpsimd",
                  (lambda t0=t0: lambda gp: gp.dma_start(
                      out=rows_out[1:2, t0:t0 + TAU], in_=r2row[:]))(),
                  waits={"LNR": p.c("LNR")}, stream="STr2")

        if False and DEBUG_DUMPS and n_subtiles == 1 and not second:
            dbg = {
                "dbg_k1": (k1[0], BF16, NCH * TAU),
                "dbg_h": (hbuf, F32, NHC * TAU),
                "dbg_k2": (k2, BF16, NHC * TAU),
                "dbg_e1": (e1sb, F32, TAU),
                "dbg_e2": (e2sb, F32, TAU),
            }
            allw = {s: p.c(s) for s in ("X1", "K1", "K2", "HGD", "EC1",
                                        "EC2") if s in p.sems}
            for nm, (tt, dt, w) in dbg.items():
                od = nc.dram_tensor(nm, [128, w], dt, kind="ExternalOutput").ap()
                p.dma("gpsimd",
                      (lambda od=od, tt=tt, w=w: lambda gp: gp.dma_start(
                          out=od[:], in_=tt.ap()[:, 0:w] if hasattr(tt, 'ap')
                          else tt[:, 0:w]))(),
                      waits=dict(allw), stream="STdbg")
            rowd = nc.dram_tensor("dbg_rows", [16, TAU], F32,
                                  kind="ExternalOutput").ap()
            for ri, nm in enumerate(("m", "v", "r", "q", "R", "Mx", "Mn",
                                     "aa", "S1", "mS1", "e1", "h2", "q2",
                                     "aa2", "S2", "e2t")):
                p.dma("gpsimd",
                      (lambda nm=nm, ri=ri: lambda gp: gp.dma_start(
                          out=rowd[ri:ri + 1, :], in_=rw[nm][:]))(),
                      waits=dict(allw), stream="STdbg")

    return None



def emit_lru_phase(p, nc, pes, io, W, gate=None):
    gate = dict(gate or {})
    x1_in, rows_in = io["x1_in"], io["rows_in"]
    wit, wgt = io["wit"], io["wgt"]
    biv, bgv = io["biv"], io["bgv"]
    hloc_out, p_out = io["hloc_out"], io["p_out"]
    if True:
        sb = lambda *a: pes.enter_context(nc.sbuf_tensor(*a))
        witb = sb("witb", [128, NCH * D], BF16)
        wgtb = sb("wgtb", [128, NCH * D], BF16)
        stage = sb("lstage", [128, D], F32)
        onesf = W["onesf"]
        bisb = sb("bisb", [128, NCH], F32)
        bgsb = sb("bgsb", [128, NCH], F32)
        onecol = sb("onecol", [128, 1], F32)
        zeros = sb("zeros", [128, TAU2], F32)
        x1bt = [sb("x1bt0", [128, NCH * TAU2], F32)] * 2
        rmt = [sb("rmt0", [1, TAU2], F32)] * 2
        rrt = [sb("rrt0", [1, TAU2], F32)] * 2
        xlf = sb("xlf", [128, TAU2], F32)
        xln = sb("xln", [128, NCH * TAU2], BF16)
        ur = [sb(f"ur{i}", [128, TAU2], F32) for i in range(2)]
        gr = [sb(f"gr{i}", [128, TAU2], F32) for i in range(2)]
        ogr = [sb(f"ogr{i}", [128, TAU2], F32) for i in range(2)]
        br = [sb(f"br{i}", [128, TAU2], F32) for i in range(2)]
        hb = [io["bigbuf"].ap()[:, 0:NCH * TAU2]] * 2
        pb = [io["bigbuf"].ap()[:, NCH * TAU2:2 * NCH * TAU2]] * 2
        pbf = sb("pbf", [128, NCH * TAU2], BF16)
        hinit = [sb(f"hinit{c}", [128, 1], F32) for c in range(NCH)]
        pinit = [sb(f"pinit{c}", [128, 1], F32) for c in range(NCH)]


        pt = lambda *a: pes.enter_context(nc.psum_tensor(*a))
        ups = pt("ups", [128, 2 * TAU2], F32)
        gps = pt("gps", [128, 2 * TAU2], F32)
        repm = pt("repm", [128, TAU2], F32)
        repr_ = pt("repr", [128, TAU2], F32)

        p.add("vector", lambda v: v.memset(onecol[:], 1.0))
        p.add("vector", lambda v: v.memset(zeros[:], 0.0), inc="init")
        p.dma("sync", lambda s: s.dma_start(out=bisb[:], in_=biv[:]),
              waits=dict(gate), stream="ld0")
        p.dma("sync", lambda s: s.dma_start(out=bgsb[:], in_=bgv[:]),
              waits=dict(gate), stream="ld0")
        LD0C = p.c("ld0")
        for wi, (wt, wb) in enumerate(((wit, witb), (wgt, wgtb))):
            for kc in range(NCH):
                wldw = {"wcast": p.c("wcast")}
                if wi == 0 and kc == 0:
                    wldw.update(gate)
                p.dma("sync",
                      (lambda wt=wt, kc=kc: lambda s: s.dma_start(
                          out=stage[:], in_=wt[kc]))(),
                      waits=wldw, stream="wld")
                p.add("scalar",
                      (lambda wb=wb, kc=kc: lambda sc: sc.copy(
                          wb[:, kc * D:(kc + 1) * D], stage[:]))(),
                      waits={"wld": 16 * (p.c("wld") // 16)}, inc="wcast")

        def fm_dram(t, t0, n):
            return t.rearrange("c p t -> p c t")[:, :, t0:t0 + n]

        def fm_sbuf(t, tau):
            return _ap(t).rearrange("p (c t) -> p c t", c=NCH)

        for j in range(NST2):
            t0 = j * TAU2
            sl = j % 2
            CS = lambda c: slice(c * TAU2, (c + 1) * TAU2)

            ldw = {}
            if j == 0:
                ldw.update(gate)
            if j >= 1:
                ldw["XLN"] = p.c("XLN")
                ldw["BCM"] = p.c("BCM")
            p.dma("sync",
                  (lambda sl=sl, t0=t0: lambda s: s.dma_start(
                      out=fm_sbuf(x1bt[sl], TAU2),
                      in_=fm_dram(x1_in, t0, TAU2)))(),
                  waits=ldw, stream="Lx")
            p.dma("sync",
                  (lambda sl=sl, t0=t0: lambda s: s.dma_start(
                      out=rmt[sl][:], in_=rows_in[0:1, t0:t0 + TAU2]))(),
                  waits=dict(ldw), stream="Lm")
            p.dma("sync",
                  (lambda sl=sl, t0=t0: lambda s: s.dma_start(
                      out=rrt[sl][:], in_=rows_in[1:2, t0:t0 + TAU2]))(),
                  waits=dict(ldw), stream="Lr")
            lnow = {"Lx": p.c("Lx"), "Lm": p.c("Lm"), "Lr": p.c("Lr")}

            bw = dict(lnow); bw["init"] = p.c("init")
            if j >= 1:
                bw["XLN"] = p.c("XLN")
            def _bc(t, sl=sl):
                t.matmul(repm[:], onesf[:], rmt[sl][:], start=True,
                         stop=True)
                return t.matmul(repr_[:], onesf[:], rrt[sl][:],
                                start=True, stop=True)
            p.add("tensor", _bc, waits=bw, inc="BCM")

            xw = {"BCM": p.c("BCM")}
            if j >= 1:
                xw["UMM"] = p.c("UMM")
            def _xln(v, sl=sl):
                last = None
                for c in range(NCH):
                    v.tensor_tensor(xlf[:], x1bt[sl][:, CS(c)], repm[:],
                                    ALU.subtract)
                    last = v.tensor_tensor(xln[:, CS(c)], xlf[:], repr_[:],
                                          ALU.mult)
                return last
            p.add("vector", _xln, waits=xw, inc="XLN")

            for oc in range(NCH):
                mw = {"XLN": p.c("XLN"), "wcast": 2 * NCH,
                      "UMM": p.c("UMM")}
                if oc >= 2 or j >= 1:
                    mw["UGD"] = p.c("UGD") - (1 if oc >= 2 else 0)
                def _umm(t, oc=oc):
                    for kc in range(NCH):
                        lt = witb[:, kc * D + oc * 128:kc * D + (oc + 1) * 128]
                        t.matmul(ups[:, (oc % 2) * TAU2:(oc % 2 + 1) * TAU2],
                                 lt, xln[:, CS(kc)], start=(kc == 0),
                                 stop=(kc == NCH - 1))
                    last = None
                    for kc in range(NCH):
                        lt = wgtb[:, kc * D + oc * 128:kc * D + (oc + 1) * 128]
                        last = t.matmul(
                            gps[:, (oc % 2) * TAU2:(oc % 2 + 1) * TAU2],
                            lt, xln[:, CS(kc)], start=(kc == 0),
                            stop=(kc == NCH - 1))
                    return last
                p.add("tensor", _umm, waits=mw, inc="UMM")

                def _acts(sc, oc=oc):
                    sc.activation(ur[oc % 2][:],
                                  ups[:, (oc % 2) * TAU2:(oc % 2 + 1) * TAU2],
                                  AF.Identity, bias=bisb[:, oc:oc + 1])
                    sc.activation(gr[oc % 2][:],
                                  gps[:, (oc % 2) * TAU2:(oc % 2 + 1) * TAU2],
                                  AF.Sigmoid, bias=bgsb[:, oc:oc + 1])
                    return sc.activation(ogr[oc % 2][:], gr[oc % 2][:],
                                         AF.Identity, bias=onecol[:],
                                         scale=-1.0)
                aw = {"UMM": p.c("UMM"), "ld0": LD0C}
                if j >= 1 or oc >= 2:
                    aw["SCN"] = p.c("SCN") - (1 if oc >= 2 else 0)
                p.add("scalar", _acts, waits=aw, inc="UGD")

                def _bin(v, oc=oc):
                    return v.tensor_tensor(br[oc % 2][:], ogr[oc % 2][:],
                                           ur[oc % 2][:], ALU.mult)
                p.add("vector", _bin, waits={"UGD": p.c("UGD")}, inc="BIN")

                def _scan(v, oc=oc, sl=sl, j=j):
                    if j == 0:
                        ih, ip = 0.0, 1.0
                    else:
                        ih, ip = hinit[oc][:], pinit[oc][:]
                    v.tensor_tensor_scan(hb[sl][:, CS(oc)], gr[oc % 2][:],
                                         br[oc % 2][:], ih, ALU.mult, ALU.add)
                    v.tensor_tensor_scan(pb[sl][:, CS(oc)],
                                         gr[oc % 2][:], zeros[:], ip,
                                         ALU.mult, ALU.add)
                    v.tensor_copy(hinit[oc][:],
                                  hb[sl][:, oc * TAU2 + TAU2 - 1:
                                         oc * TAU2 + TAU2])
                    return v.tensor_copy(pinit[oc][:],
                                         pb[sl][:, oc * TAU2 + TAU2 - 1:
                                                oc * TAU2 + TAU2])
                sw = {"BIN": p.c("BIN"), "init": p.c("init")}
                if j >= 1 and oc == 0:
                    sw["SH"] = p.c("SH")
                    sw["SP"] = p.c("SP")
                p.add("vector", _scan, waits=sw, inc="SCN")

            p.add("scalar", lambda sc: sc.copy(carry_sb0[:], carry_sb0[:])
                  if False else None, waits={"SCN": p.c("SCN")}, inc="HCP") \
                if False else None
            p.dma("gpsimd",
                  (lambda t0=t0, sl=sl: lambda gp: gp.dma_start(
                      out=fm_dram(hloc_out, t0, TAU2),
                      in_=fm_sbuf(hb[sl], TAU2)))(),
                  waits={"SCN": p.c("SCN")}, stream="SH")
            cwp = {"SCN": p.c("SCN")}
            if j >= 1:
                cwp["SP"] = p.c("SP")
            p.add("scalar",
                  (lambda sl=sl: lambda sc: sc.activation(
                      pbf[:], pb[sl][:], AF.Identity))(),
                  waits=cwp, inc="HCP")
            p.dma("gpsimd",
                  (lambda t0=t0, sl=sl: lambda gp: gp.dma_start(
                      out=fm_dram(p_out, t0, TAU2),
                      in_=fm_sbuf(pbf, TAU2)))(),
                  waits={"HCP": p.c("HCP")}, stream="SP")
            if j == NST2 - 1:
                carry_sb = sb("carry_sb", [128, NCH], F32)
                def _carrycp(v):
                    last = None
                    for c in range(NCH):
                        last = v.tensor_copy(carry_sb[:, c:c + 1],
                                             hinit[c][:])
                    return last
                p.add("vector", _carrycp,
                      waits={"SCN": p.c("SCN")}, inc="CCP")
                p.dma("gpsimd",
                      lambda gp: gp.dma_start(out=io["carry_out"][:],
                                              in_=carry_sb[:]),
                      waits={"CCP": p.c("CCP")}, stream="SC")
    return



def build_fused():
    nc = bass.Bass("TRN2", target_bir_lowering=False)
    es = ExitStack()
    dram = lambda name, shape, dt, kind="Internal", space="Local": \
        nc.dram_tensor(name, shape, dt, kind=kind, addr_space=space).ap()

    io = {}
    io["w1t"] = dram("w1t", [NCH, 128, HID], F32, "ExternalInput")
    io["w2t"] = dram("w2t", [NHC, 128, D], F32, "ExternalInput")
    io["wit"] = dram("wit", [NCH, 128, D], F32, "ExternalInput")
    io["wgt"] = dram("wgt", [NCH, 128, D], F32, "ExternalInput")
    io["wot"] = dram("wot", [NCH, 128, D], F32, "ExternalInput")
    io["b1v"] = dram("b1v", [128, NHC], F32, "ExternalInput")
    io["b2v"] = dram("b2v", [128, NCH], F32, "ExternalInput")
    io["biv"] = dram("biv", [128, NCH], F32, "ExternalInput")
    io["bgv"] = dram("bgv", [128, NCH], F32, "ExternalInput")
    io["bov"] = dram("bov", [128, NCH], F32, "ExternalInput")
    io["selv"] = dram("selv", [128, 2], F32, "ExternalInput")
    io["x_in"] = dram("x_in", [NCH, 128, TPC], F32, "ExternalInput")
    io["out_fm"] = dram("out_fm", [NCH, 128, TPC], F32, "ExternalOutput")
    io["x1_out"] = io["x1_in"] = dram("x1d", [NCH, 128, TPC], F32)
    io["rows_out"] = io["rows_in"] = dram("rowsd", [2, TPC], F32)
    io["hloc_out"] = io["hloc_in"] = dram("hd", [NCH, 128, TPC], F32)
    io["p_out"] = io["p_in"] = dram("pd", [NCH, 128, TPC], BF16)
    io["carry_out"] = dram("carryd", [128, NCH], F32)
    io["carry_gath"] = dram("cgath", [2, 128, NCH], F32, "Internal", "Local")

    with es:
        p = Prog(nc, es)
        sb = lambda *a: es.enter_context(nc.sbuf_tensor(*a))
        bigbuf = sb("bigbuf", [128, NHC * TAU], F32)
        io["bigbuf"] = bigbuf
        stage = bigbuf.ap()[:, 0:HID]
        wabs = bigbuf.ap()[:, HID:2 * HID]
        W = emit_weight_quant(p, sb, io, stage, wabs)
        wotb = sb("wotb", [128, NCH * D], BF16)
        io["wotb"] = wotb
        wostage = stage[:, 0:D]
        for kc in range(NCH):
            wol = {"wo_cast": p.c("wo_cast")}
            if kc == 0:
                wol["wq_q"] = NCH + NHC
            p.dma("sync",
                  (lambda kc=kc: lambda s: s.dma_start(
                      out=wostage, in_=io["wot"][kc]))(),
                  waits=wol, stream="wo_ld")
            p.add("scalar",
                  (lambda kc=kc: lambda sc: sc.copy(
                      wotb[:, kc * D:(kc + 1) * D], wostage))(),
                  waits={"wo_ld": p.c("wo_ld")}, inc="wo_cast")
        selsb = sb("selsb", [128, 2], F32)
        gathsb = sb("gathsb", [128, 2 * NCH], F32)
        hcar = sb("hcar", [128, NCH], F32)
        io["hcar"] = hcar
        p.dma("sync", lambda s: s.dma_start(out=selsb[:], in_=io["selv"][:]),
              stream="ld0")

        with ExitStack() as pes:
            emit_ffn_phase(p, nc, pes, W, io, second=False,
                           gate={"wo_cast": NCH})
        g1 = {"STx1": p.c("STx1"), "STm2": p.c("STm2"),
              "STr2": p.c("STr2"), "LNR": p.c("LNR")}

        with ExitStack() as pes:
            emit_lru_phase(p, nc, pes, io, W, gate=g1)

        p.add("gpsimd",
              lambda gp: gp.collective_compute(
                  "AllGather", ALU.bypass,
                  replica_groups=[[0, 1], [2, 3], [4, 5], [6, 7]],
                  ins=[io["carry_out"][:]], outs=[io["carry_gath"][:]]),
              waits={"SC": p.c("SC")}, inc="CCG")
        p.dma("gpsimd",
              lambda gp: gp.dma_start(out=gathsb[:, 0:NCH],
                                      in_=io["carry_gath"][0]),
              waits={"CCG": p.c("CCG")}, stream="LG")
        p.dma("gpsimd",
              lambda gp: gp.dma_start(out=gathsb[:, NCH:2 * NCH],
                                      in_=io["carry_gath"][1]),
              waits={"CCG": p.c("CCG")}, stream="LG")
        def _sel(v):
            v.tensor_scalar(hcar[:], gathsb[:, 0:NCH], selsb[:, 0:1], None,
                            ALU.mult)
            return v.scalar_tensor_tensor(hcar[:], gathsb[:, NCH:2 * NCH],
                                          selsb[:, 1:2], hcar[:],
                                          ALU.mult, ALU.add)
        p.add("vector", _sel,
              waits={"LG": p.c("LG"), "ld0": p.c("ld0")}, inc="HCAR")
        g2 = {"SH": p.c("SH"), "SP": p.c("SP"), "HCAR": p.c("HCAR"),
              "HCP": p.c("HCP"), "CCP": p.c("CCP")}

        with ExitStack() as pes:
            emit_ffn_phase(p, nc, pes, W, io, second=True, gate=g2)

        fin = {"STx1": p.c("STx1")}
        p.add("gpsimd", lambda gp: None, waits=fin)
        p.realize()
    return nc



def _prep_host(inputs):
    x = np.ascontiguousarray(np.asarray(inputs["L0_attn_x"], dtype=np.float32))
    w1 = np.asarray(inputs["L0_ffn_w1"], np.float32)
    w2 = np.asarray(inputs["L0_ffn_w2"], np.float32)
    wi = np.asarray(inputs["L0_attn_Wi"], np.float32)
    wg = np.asarray(inputs["L0_attn_Wg"], np.float32)
    wo = np.asarray(inputs["L0_attn_Wo"], np.float32)
    C = lambda a: np.ascontiguousarray(a)
    g = {
        "w1t": C(w1.T.reshape(NCH, 128, HID)),
        "w2t": C(w2.T.reshape(NHC, 128, D)),
        "wit": C(wi.T.reshape(NCH, 128, D)),
        "wgt": C(wg.T.reshape(NCH, 128, D)),
        "wot": C(wo.T.reshape(NCH, 128, D)),
        "b1v": C(np.asarray(inputs["L0_ffn_b1"], np.float32)
                 .reshape(NHC, 128).T),
        "b2v": C(np.asarray(inputs["L0_ffn_b2"], np.float32)
                 .reshape(NCH, 128).T),
        "biv": C(np.asarray(inputs["L0_attn_bi"], np.float32)
                 .reshape(NCH, 128).T),
        "bgv": C(np.asarray(inputs["L0_attn_bg"], np.float32)
                 .reshape(NCH, 128).T),
        "bov": C(np.asarray(inputs["L0_attn_bo"], np.float32)
                 .reshape(NCH, 128).T),
    }
    xs = []
    for c in range(NCORE):
        s, hc = c // 2, c % 2
        sl = x[s, hc * TPC:(hc + 1) * TPC, :]
        xs.append(C(sl.T.reshape(NCH, 128, TPC)))
    return g, xs


def kernel(**inputs):
    g, xs = _prep_host(inputs)
    cores = list(range(NCORE))
    nc = build_fused()
    sel0 = np.zeros((128, 2), np.float32)
    sel1 = np.zeros((128, 2), np.float32)
    sel1[:, 0] = 1.0
    maps = []
    for c in cores:
        maps.append({
            "w1t": g["w1t"], "w2t": g["w2t"], "wit": g["wit"],
            "wgt": g["wgt"], "wot": g["wot"], "b1v": g["b1v"],
            "b2v": g["b2v"], "biv": g["biv"], "bgv": g["bgv"],
            "bov": g["bov"], "x_in": xs[c],
            "selv": sel1 if c % 2 else sel0,
        })
    res = run_bass_kernel_spmd(nc, maps, core_ids=cores).results
    out = np.empty((BATCH, SEQ, D), np.float32)
    for c in cores:
        s, hc = c // 2, c % 2
        o = np.asarray(res[c]["out_fm"]).reshape(D, TPC)
        out[s, hc * TPC:(hc + 1) * TPC, :] = o.T
    return out


# version 08
# speedup vs baseline: 1.2007x; 1.1679x over previous
import numpy as np
import ml_dtypes
from contextlib import ExitStack

import concourse.bass as bass
import concourse.mybir as mybir
from concourse.bass_utils import run_bass_kernel_spmd

F32 = mybir.dt.float32
BF16 = mybir.dt.bfloat16
FP8 = mybir.dt.float8e4
AF = mybir.ActivationFunctionType
ALU = mybir.AluOpType
AX = mybir.AxisListType

D = 1024
HID = 4096
NCH = D // 128
NHC = HID // 128
SEQ = 8192
BATCH = 4
NCORE = 8
TPC = SEQ // 2
TAU = 256
NST = TPC // TAU
TAU2 = 512
NST2 = TPC // TAU2
RNDC = 1.5 * 2.0**23
EPS5 = 1e-5
EPS6 = 1e-6

ENGINES = ("sync", "scalar", "vector", "tensor", "gpsimd")

import os as _os
CHAIN_EDGES = _os.environ.get("BASS_SIM_FP64") == "1" or \
    _os.environ.get("KERNEL_FORCE_CHAIN_EDGES") == "1"
DEBUG_DUMPS = False



class Prog:

    def __init__(self, nc, es):
        self.nc = nc
        self.es = es
        self.ops = {e: [] for e in ENGINES}
        self.sems = {}
        self.cnt = {}

    def sem(self, name):
        if name not in self.sems:
            self.sems[name] = self.es.enter_context(self.nc.semaphore(name))
            self.cnt[name] = 0
        return name

    def add(self, eng, emit, waits=None, inc=None, amt=1):
        w = dict(waits or {})
        incs = []
        if inc is not None:
            self.sem(inc)
            self.cnt[inc] += amt
            incs = [(inc, amt)]
        for s in w:
            self.sem(s)
        self.ops[eng].append((w, emit, incs))

    def c(self, name):
        self.sem(name)
        return self.cnt[name]

    def dma(self, eng, emit, waits=None, stream="L"):
        w = dict(waits or {})
        self.sem(stream)
        w[stream] = self.cnt[stream]
        self.add(eng, emit, w, inc=stream, amt=16)

    def realize(self):
        with self.nc.Block() as blk:
            for eng in ENGINES:
                seq = self.ops[eng]
                if not seq:
                    continue
                chain_sem = None
                if CHAIN_EDGES and eng in ("vector", "scalar", "gpsimd"):
                    chain_sem = self.es.enter_context(
                        self.nc.semaphore(f"chain_{eng}"))

                def make(seq, chain_sem):
                    def body(e):
                        ee = _ChainEngine(e, chain_sem) if chain_sem else e
                        for (waits, emit, incs) in seq:
                            if chain_sem:
                                ee.flush_pending()
                            for s, v in waits.items():
                                if v > 0:
                                    e.wait_ge(self.sems[s], v)
                            ins = emit(ee)
                            for s, a in incs:
                                if ins is None:
                                    ins = e.engine_nop()
                                    if chain_sem:
                                        ee.chain_bare(ins)
                                if chain_sem:
                                    ee.take_inc(ins, self.sems[s], a)
                                else:
                                    ins.then_inc(self.sems[s], a)
                        if chain_sem:
                            ee.finalize()
                    return body
                getattr(blk, eng)(make(list(seq), chain_sem))


class _ChainEngine:

    _SKIP = {"dma_start", "wait_ge", "then_inc", "engine_nop", "bass",
             "value_load", "partition_id", "flush_pending", "take_inc",
             "finalize", "chain_bare"}

    def __init__(self, eng, sem):
        self._eng = eng
        self._sem = sem
        self._n = 0
        self._last = None
        self._pending = []
        self._reccnt = {}

    def _finalize_last(self):
        if self._last is not None:
            self._last.then_inc(self._sem, 1)
            self._n += 1
            self._last = None

    def flush_pending(self):
        for (s, v) in self._pending:
            self._eng.wait_ge(s, v)
        self._pending = []

    def take_inc(self, ins, sem_handle, amt):
        key = id(sem_handle)
        self._reccnt[key] = self._reccnt.get(key, 0) + amt
        if ins is self._last:
            self._last = None
            ins.then_inc(sem_handle, amt)
            self._pending.append((sem_handle, self._reccnt[key]))
        else:
            ins.then_inc(sem_handle, amt)

    def chain_bare(self, ins):
        self._finalize_last()
        if self._n > 0:
            ins._wait_ge(self._sem, self._n)

    def finalize(self):
        self._finalize_last()

    def __getattr__(self, name):
        f = getattr(self._eng, name)
        if not callable(f) or name.startswith("_") or name in self._SKIP:
            return f

        def wrap(*a, **k):
            self._finalize_last()
            self.flush_pending()
            r = f(*a, **k)
            if hasattr(r, "then_inc"):
                if self._n > 0:
                    r._wait_ge(self._sem, self._n)
                self._last = r
            return r
        return wrap


def _ap(t):
    a = getattr(t, "ap", None)
    return t.ap() if callable(a) else t


def emit_weight_quant(p, sb, g, stage, wabs):
    nc = p.nc
    w1t, w2t = g["w1t"], g["w2t"]
    w1q = sb("w1q", [128, NCH * HID], FP8)
    w2q = sb("w2q", [128, NHC * D], FP8)
    acc = sb("wacc", [128, NCH + NHC], F32)
    accr = sb("waccr", [128, 2], F32)
    rowt = sb("wrowt", [1, 2], F32)
    mwc = sb("mwc", [1, 2], F32)
    ccol = sb("ccol", [128, 2], F32)
    onesf = sb("onesf", [1, 128], F32)
    onesb = sb("onesb", [128, 1], BF16)
    onesc = sb("onesc", [128, 1], F32)
    with nc.psum_tensor("cps", [128, 2], F32) as cps:
        pass

    plans = [(w1t, NCH, HID, 0, 127.0), (w2t, NHC, D, 1, 254.0)]
    p.add("vector", lambda v: v.memset(onesf[:], 1.0))
    p.add("vector", lambda v: v.memset(onesb[:], 1.0))
    p.add("vector", lambda v: v.memset(onesc[:], 1.0), inc="wq_init")

    for (wt, n, flen, wi, div) in plans:
        col0 = 0 if wi == 0 else NCH
        for k in range(n):
            p.dma("sync",
                  (lambda wt=wt, k=k, flen=flen:
                   lambda s: s.dma_start(out=stage[:, 0:flen], in_=wt[k],
                                         max_dma_last_dim=2048))(),
                  waits={"wq_abs": max(0, p.c("wq_abs"))},
                  stream="wq_ld")
            p.add("vector",
                  (lambda k=k, flen=flen, col0=col0:
                   lambda v: v.tensor_reduce(
                       acc[:, col0 + k:col0 + k + 1], stage[:, 0:flen],
                       AX.X, ALU.add, apply_absolute_value=True))(),
                  waits={"wq_ld": p.c("wq_ld")},
                  inc="wq_abs")
        total = 128.0 * n * flen
        p.add("vector",
              (lambda n=n, wi=wi, col0=col0: lambda v: v.tensor_reduce(
                  accr[:, wi:wi + 1], acc[:, col0:col0 + n], AX.X,
                  ALU.add))(),
              waits={"wq_abs": p.c("wq_abs")}, inc="wq_red")
        p.add("tensor",
              (lambda wi=wi: lambda t: t.matmul(
                  cps[0:1, wi:wi + 1], onesc[:], accr[:, wi:wi + 1],
                  start=True, stop=True))(),
              waits={"wq_red": p.c("wq_red"), "wq_init": 1,
                     "wq_cc": p.c("wq_cc")}, inc="wq_gp")
        p.add("vector",
              (lambda wi=wi, total=total: lambda v: v.tensor_scalar(
                  rowt[0:1, wi:wi + 1], cps[0:1, wi:wi + 1],
                  1.0 / total, EPS5, ALU.mult, ALU.max))(),
              waits={"wq_gp": p.c("wq_gp")}, inc="wq_rmA")
        p.add("vector",
              (lambda wi=wi, div=div: lambda v: v.tensor_scalar(
                  mwc[0:1, wi:wi + 1], rowt[0:1, wi:wi + 1],
                  1.0 / div, None, ALU.mult))(),
              waits={"wq_rmA": p.c("wq_rmA")}, inc="wq_rmB")
        p.add("vector",
              (lambda wi=wi: lambda v: v.reciprocal(
                  rowt[0:1, wi:wi + 1], rowt[0:1, wi:wi + 1]))(),
              waits={"wq_rmB": p.c("wq_rmB")}, inc="wq_rm")
        p.add("tensor",
              (lambda wi=wi: lambda t: t.matmul(
                  cps[:, wi:wi + 1], onesf[:], rowt[0:1, wi:wi + 1],
                  start=True, stop=True))(),
              waits={"wq_rm": p.c("wq_rm"), "wq_init": 1,
                     "wq_bc": p.c("wq_bc")}, inc="wq_bc")
        p.add("scalar",
              (lambda wi=wi: lambda sc: sc.copy(ccol[:, wi:wi + 1],
                                                cps[:, wi:wi + 1]))(),
              waits={"wq_bc": p.c("wq_bc")}, inc="wq_cc")

    for (wt, n, flen, wi, div) in plans:
        wq = w1q if wi == 0 else w2q
        for k in range(n):
            ldg = {"wq_q": p.c("wq_q")}
            if wi == 0 and k == 0:
                ldg["wq_abs"] = NCH + NHC
            p.dma("sync",
                  (lambda wt=wt, k=k, flen=flen:
                   lambda s: s.dma_start(out=stage[:, 0:flen], in_=wt[k],
                                         max_dma_last_dim=2048))(),
                  waits=ldg, stream="wq_ld")
            def _quant(v, k=k, flen=flen, wi=wi, wq=wq):
                v.tensor_scalar(stage[:, 0:flen], stage[:, 0:flen],
                                ccol[:, wi:wi + 1], RNDC, ALU.mult, ALU.add)
                v.tensor_scalar(stage[:, 0:flen], stage[:, 0:flen],
                                RNDC, 1.0, ALU.subtract, ALU.min)
                return v.tensor_scalar(wq[:, k * flen:(k + 1) * flen],
                                       stage[:, 0:flen], -1.0, None, ALU.max)
            p.add("vector", _quant,
                  waits={"wq_ld": p.c("wq_ld"), "wq_cc": 1 + wi},
                  inc="wq_q")
    return dict(w1q=w1q, w2q=w2q, mwc=mwc, onesf=onesf, onesb=onesb,
                onesc=onesc,
                wq_done={"wq_q": p.c("wq_q")})



def emit_ffn_phase(p, nc, pes, W, io, second, n_subtiles=NST,
                   debug_stage=99, gate=None):
    gate = dict(gate or {})
    b1v, b2v = io["b1v"], io["b2v"]
    if not second:
        x_in, x1_out, rows_out = io["x_in"], io["x1_out"], io["rows_out"]
    else:
        x1_in, hloc_in, p_in = io["x1_in"], io["hloc_in"], io["p_in"]
        bov, out_fm = io["bov"], io["out_fm"]
        hcar, wotb = io["hcar"], io["wotb"]
    tag = "fb_" if second else "fa_"
    if True:
        sb = lambda nm, *a: pes.enter_context(nc.sbuf_tensor(tag + nm, *a))
        hbuf = io["bigbuf"].ap()[:, 0:NHC * TAU]
        w1q, w2q = W["w1q"], W["w2q"]
        onesf, onesb, mwc = W["onesf"], W["onesb"], W["mwc"]
        onesc = W["onesc"]

        if not second:
            xt = [sb(f"xt{i}", [128, NCH * TAU], F32) for i in range(2)]
        else:
            xt = [sb("xt0", [128, NCH * TAU], F32)] * 2
        xsq = [sb("xsq0", [128, NCH * TAU], F32)] * 2
        sacc = sb("sacc", [128, TAU], F32)
        sacc2 = sb("sacc2", [128, TAU], F32)
        qtmp2 = sacc
        if not second:
            k1 = [sb(f"k1_{i}", [128, NCH * TAU], BF16) for i in range(2)]
        else:
            k1 = [sb("k1_0", [128, NCH * TAU], BF16)] * 2
        qtmp = None
        hsq = [sb(f"hsq{i}", [128, TAU], F32) for i in range(2)]
        htmp = [sb(f"htmp{i}", [128, TAU], F32) for i in range(2)]
        k2 = sb("k2", [128, NHC * TAU], BF16)
        x1t = sb("x1t", [128, NCH * TAU], F32)
        if not second:
            x1sq = sb("x1sq", [128, NCH * TAU], F32)
        cmx = sb("cmx", [128, TAU], F32)
        cmn = sb("cmn", [128, TAU], F32)
        hmax = cmx
        colb = sb("colb", [128, 4], F32)
        e1sb = sb("e1sb", [128, TAU], F32)
        e2sb = sb("e2sb", [128, TAU], F32)
        b1sb = sb("b1sb", [128, NHC], F32)
        b2sb = sb("b2sb", [128, NCH], F32)
        if not second:
            m2row = sb("m2row", [1, TAU], F32)
            r2row = sb("r2row", [1, TAU], F32)
        epsA = sb("epsA", [1, 1], F32)
        epsB = sb("epsB", [1, 1], F32)
        rw = {}
        for grp in (("m",), ("e2", "rr", "t4"), ("R",), ("t3",),
                    ("M1", "a2"), ("a", "aa2"), ("aa",), ("s", "s2"),
                    ("m2",), ("v",), ("r",),
                    ("vr", "h2", "lnt"), ("q", "q2"), ("S1", "S2"),
                    ("mS1", "e2t"), ("e1",)):
            t = sb(f"rw_{grp[0]}", [1, TAU], F32).ap()
            for n in grp:
                rw[n] = t
        if second:
            x1in_t = [x1t] * 2
            hl_t = [sb("hl0", [128, NCH * TAU], F32)] * 2
            p_t = [sb("pt0", [128, NCH * TAU], BF16)] * 2
            hfix = [sb("hfix0", [128, NCH * TAU], BF16)] * 2
            bosb = sb("bosb", [128, NCH], F32)

        pt = lambda nm, *a: pes.enter_context(nc.psum_tensor(tag + nm, *a))
        hps = [pt(f"hps{i}", [128, TAU], F32) for i in range(2)]
        ops = [pt(f"ops{i}", [128, TAU], F32) for i in range(2)]
        statb = pt("statb", [1, 2 * TAU], F32)
        stat2b = pt("stat2b", [1, 2 * TAU], F32)
        stat3b = statb
        repA = pt("repA", [128, 2 * TAU], F32)
        repB = pt("repB", [128, 2 * TAU], F32)
        rw["Mx"] = repA[0:1, TAU:2 * TAU]
        rw["Mn"] = repB[0:1, TAU:2 * TAU]
        rw["A2"] = repA[0:1, TAU:2 * TAU]

        p.dma("sync", lambda s: s.dma_start(out=b1sb[:], in_=b1v[:]),
              waits=dict(gate), stream="ld0")
        p.dma("sync", lambda s: s.dma_start(out=b2sb[:], in_=b2v[:]),
              waits=dict(gate), stream="ld0")
        if second:
            p.dma("sync", lambda s: s.dma_start(out=bosb[:], in_=bov[:]),
                  waits=dict(gate), stream="ld0")
        p.add("vector", lambda v: v.memset(epsA[:], EPS5))
        p.add("vector", lambda v: v.memset(epsB[:], EPS6))
        p.add("scalar", lambda sc: sc.mul(b2sb[:], b2sb[:], 0.5),
              waits={"ld0": p.c("ld0")}, inc="bias_ok")
        BOK = p.c("bias_ok")

        x_src = x_in if not second else x1_in

        def fm_dram(t, t0, n):
            return t.rearrange("c p t -> p c t")[:, :, t0:t0 + n]

        def fm_sbuf(t):
            return _ap(t).rearrange("p (c t) -> p c t", c=NCH)

        for i in range(n_subtiles):
            t0 = i * TAU
            sl = i % 2
            CS = lambda c: slice(c * TAU, (c + 1) * TAU)

            ldw = {}
            if i == 0:
                ldw.update(gate)
            if i >= 2:
                ldw["X1"] = p.c("X1") - 1
                ldw["STT"] = p.c("STT") - 1
            if not second:
                p.dma("sync",
                      (lambda sl=sl, t0=t0: lambda s: s.dma_start(
                          out=fm_sbuf(xt[sl]), in_=fm_dram(x_src, t0, TAU)))(),
                      waits=ldw, stream="Lx")
            else:
                if i >= 1:
                    ldw["FXD"] = p.c("FXD")
                    ldw["OPC"] = p.c("OPC") - 8
                    ldw["STx1"] = p.c("STx1")
                p.dma("sync",
                      (lambda sl=sl, t0=t0: lambda s: s.dma_start(
                          out=fm_sbuf(x1in_t[sl]),
                          in_=fm_dram(x1_in, t0, TAU)))(),
                      waits=ldw, stream="Lx")
                p.dma("sync",
                      (lambda sl=sl, t0=t0: lambda s: s.dma_start(
                          out=fm_sbuf(hl_t[sl]),
                          in_=fm_dram(hloc_in, t0, TAU)))(),
                      waits=dict(ldw), stream="Lh")
                p.dma("sync",
                      (lambda sl=sl, t0=t0: lambda s: s.dma_start(
                          out=fm_sbuf(p_t[sl]),
                          in_=fm_dram(p_in, t0, TAU)))(),
                      waits=dict(ldw), stream="Lp")
            l_now = {"Lx": p.c("Lx")}
            if second:
                l_now.update({"Lh": p.c("Lh"), "Lp": p.c("Lp")})

            if second:
                def _fixup(v, sl=sl):
                    last = None
                    for c in range(NCH):
                        last = v.scalar_tensor_tensor(
                            hfix[sl][:, CS(c)], p_t[sl][:, CS(c)],
                            hcar[:, c:c + 1], hl_t[sl][:, CS(c)],
                            ALU.mult, ALU.add)
                    return last
                fw = dict(l_now)
                fw["HCAR"] = p.c("HCAR")
                if i == 0:
                    fw.update(gate)
                if i >= 1:
                    fw["WOMM"] = p.c("WOMM")
                p.add("vector", _fixup, waits=fw, inc="FXD")
                for oc in range(NCH):
                    ww = {"FXD": p.c("FXD"), "wo_cast": NCH}
                    if i > 0 or oc >= 2:
                        ww["OPC"] = p.c("OPC") - 1 if (i > 0 or oc >= 2) else 0
                    def _womm(t, oc=oc, sl=sl):
                        last = None
                        for kc in range(NCH):
                            lt = wotb[:, kc * D + oc * 128:
                                      kc * D + (oc + 1) * 128]
                            last = t.matmul(
                                ops[oc % 2][:],
                                lt, hfix[sl][:, CS(kc)],
                                start=(kc == 0), stop=(kc == NCH - 1))
                        return last
                    p.add("tensor", _womm, waits=ww, inc="WOMM")
                    xw = {"WOMM": p.c("WOMM"), "bias_ok": BOK}
                    if i >= 1:
                        xw["X1"] = p.c("X1")
                    p.add("vector",
                          (lambda oc=oc, sl=sl: lambda v:
                           v.scalar_tensor_tensor(
                               xt[sl][:, CS(oc)],
                               ops[oc % 2][:],
                               bosb[:, oc:oc + 1], x1in_t[sl][:, CS(oc)],
                               ALU.add, ALU.add))(),
                          waits=xw, inc="OPC")
                xready = {"OPC": p.c("OPC")}
            else:
                xready = dict(l_now)

            castw = dict(xready)
            if i == 0:
                castw.update(gate)
            if i >= 1:
                castw["STT"] = p.c("STT")
            p.add("scalar",
                  (lambda sl=sl: lambda sc: sc.activation(
                      xsq[sl][:], xt[sl][:], AF.Square))(),
                  waits=castw, inc="SC1")
            scw = {"SC1": p.c("SC1")}
            scw.update(xready)
            if i >= 1:
                scw["STT"] = p.c("STT")
            if i >= 1 and not second:
                scw["LNS"] = p.c("LNS")
            def _chunksums(v, sl=sl):
                v.tensor_reduce(sacc[:],
                                xt[sl].ap().rearrange("p (c t) -> p t c",
                                                      c=NCH),
                                AX.X, ALU.add)
                return v.tensor_reduce(
                    sacc2[:], xsq[sl].ap().rearrange("p (c t) -> p t c",
                                                     c=NCH),
                    AX.X, ALU.add)
            p.add("vector", _chunksums, waits=scw, inc="SCC")
            if debug_stage <= 1:
                continue
            stw = {"SCC": p.c("SCC"), "wq_init": 1, "STT": p.c("STT")}
            if i >= 1:
                stw["RA"] = p.c("RA")
                stw["LNV"] = p.c("LNV")
            def _ln1stats(t, sl=sl):
                t.matmul(statb[0:1, 0:TAU], onesc[:], sacc[:],
                         start=True, stop=True)
                return t.matmul(statb[0:1, TAU:2 * TAU], onesc[:], sacc2[:],
                                start=True, stop=True)
            p.add("tensor", _ln1stats, waits=stw, inc="STT")

            mmw = dict(xready)
            if i == 0:
                mmw.update(gate)
            if i >= 1:
                mmw["TPM"] = p.c("TPM")
                mmw["TPA"] = p.c("TPA")
            def _cmaxmin(v, sl=sl):
                xv = xt[sl].ap().rearrange("p (c t) -> p t c", c=NCH)
                v.tensor_reduce(cmx[:], xv, AX.X, ALU.max)
                return v.tensor_reduce(cmn[:], xv, AX.X, ALU.min, negate=True)
            p.add("vector", _cmaxmin, waits=mmw, inc="MMR")
            tbase = p.c("TPM")
            rbase = p.c("CRD")
            k1c, k2c, ec2c = p.c("K1"), p.c("K2"), p.c("EC2")
            tpd = [(cmx, 0, repA, {"K1": k1c, "K2": k2c}),
                   (cmx, 1, repB, {"EC2": ec2c, "K2": k2c}),
                   (cmn, 0, repA, {}),
                   (cmn, 1, repB, {})]
            for tj, (src, h, bank, xw) in enumerate(tpd):
                tw = {"MMR": p.c("MMR")}
                tw.update(xw)
                if i == 0 and tj == 0:
                    tw["Lid"] = 16
                if tj >= 2:
                    tw["CRD"] = rbase + tj - 1
                p.add("tensor",
                      (lambda src=src, h=h, bank=bank: lambda t: t.matmul(
                          bank[:, 0:128], src[:, h * 128:(h + 1) * 128],
                          W["idm"][:], start=True, stop=True,
                          is_transpose=True))(),
                      waits=tw, inc="TPM")
                p.add("vector",
                      (lambda tj=tj, bank=bank: lambda v: v.tensor_reduce(
                          colb[:, tj:tj + 1], bank[:, 0:128], AX.X,
                          ALU.max))(),
                      waits={"TPM": tbase + tj + 1}, inc="CRD")
            def _extmm(t):
                t.matmul(repA[0:1, TAU:TAU + 128], colb[:, 0:1], W["idm"][:],
                         start=True, stop=True, is_transpose=True)
                t.matmul(repA[0:1, TAU + 128:2 * TAU], colb[:, 1:2],
                         W["idm"][:], start=True, stop=True,
                         is_transpose=True)
                t.matmul(repB[0:1, TAU:TAU + 128], colb[:, 2:3], W["idm"][:],
                         start=True, stop=True, is_transpose=True)
                return t.matmul(repB[0:1, TAU + 128:2 * TAU], colb[:, 3:4],
                                W["idm"][:], start=True, stop=True,
                                is_transpose=True)
            p.add("tensor", _extmm,
                  waits={"CRD": rbase + 4, "K1": k1c, "K2": k2c}, inc="EXM")

            raw = {"STT": p.c("STT")}
            if i >= 1:
                raw["BC1"] = p.c("BC1")
                raw["LNR"] = p.c("LNR")
            def _rmA(v):
                v.tensor_scalar(rw["m"][:], statb[0:1, 0:TAU], 1.0 / D, None,
                                ALU.mult)
                v.tensor_scalar(rw["e2"][:], statb[0:1, TAU:2 * TAU], 1.0 / D,
                                None, ALU.mult)
                v.tensor_tensor(rw["v"][:], rw["m"][:], rw["m"][:], ALU.mult)
                return v.tensor_tensor(rw["v"][:], rw["e2"][:], rw["v"][:],
                                       ALU.subtract)
            p.add("vector", _rmA, waits=raw, inc="RA")
            p.add("scalar",
                  lambda sc: sc.activation(rw["r"][:], rw["v"][:], AF.Sqrt,
                                           bias=epsA[:]),
                  waits={"RA": p.c("RA")}, inc="RB")
            def _rmB(v):
                v.reciprocal(rw["r"][:], rw["r"][:])
                v.tensor_tensor(rw["rr"][:], rw["r"][:], rw["r"][:], ALU.mult)
                return v.tensor_tensor(rw["vr"][:], rw["v"][:], rw["rr"][:],
                                       ALU.mult)
            p.add("vector", _rmB, waits={"RB": p.c("RB")}, inc="RC")
            p.add("scalar",
                  lambda sc: sc.activation(rw["q"][:], rw["vr"][:], AF.Sqrt,
                                           bias=epsB[:]),
                  waits={"RC": p.c("RC")}, inc="RD")
            def _rmC(v):
                v.reciprocal(rw["q"][:], rw["q"][:])
                v.tensor_tensor(rw["R"][:], rw["r"][:], rw["q"][:], ALU.mult)
                v.tensor_tensor(rw["t3"][:], rw["Mx"][:], rw["m"][:],
                                ALU.subtract)
                v.tensor_tensor(rw["t4"][:], rw["m"][:], rw["Mn"][:],
                                ALU.add)
                v.tensor_tensor(rw["M1"][:], rw["t3"][:], rw["t4"][:], ALU.max)
                v.tensor_tensor(rw["a"][:], rw["R"][:], rw["M1"][:], ALU.mult)
                v.tensor_scalar(rw["aa"][:], rw["a"][:], EPS5, None, ALU.max)
                v.reciprocal(rw["s"][:], rw["aa"][:])
                v.tensor_scalar(rw["s"][:], rw["s"][:], 127.0, None, ALU.mult)
                v.tensor_tensor(rw["S1"][:], rw["R"][:], rw["s"][:], ALU.mult)
                v.tensor_tensor(rw["mS1"][:], rw["m"][:], rw["S1"][:],
                                ALU.mult)
                return v.tensor_scalar(rw["e1"][:], rw["aa"][:],
                                       mwc[0:1, 0:1], None, ALU.mult)
            p.add("vector", _rmC,
                  waits={"RD": p.c("RD"), "EXM": p.c("EXM")}, inc="RM1")

            bcw = {"RM1": p.c("RM1"), "BC1": p.c("BC1")}
            if i >= 1:
                bcw["K1"] = p.c("K1")
                bcw["EC2"] = p.c("EC2")
                bcw["K2"] = p.c("K2")
            def _bc1(t):
                t.matmul(repA[:, 0:TAU], onesf[:], rw["S1"][:], start=True,
                         stop=True)
                t.matmul(repA[:, TAU:2 * TAU], onesf[:], rw["mS1"][:],
                         start=True, stop=True)
                return t.matmul(repB[:, 0:TAU], onesf[:], rw["e1"][:],
                                start=True, stop=True)
            p.add("tensor", _bc1, waits=bcw, inc="BC1")
            ecw = {"BC1": p.c("BC1")}
            if i >= 1:
                ecw["HEV"] = p.c("HEV")
                ecw["K2"] = p.c("K2")
            p.add("scalar",
                  lambda sc: sc.copy(e1sb[:], repB[:, 0:TAU]),
                  waits=ecw, inc="EC1")

            k1w = {"BC1": p.c("BC1")}
            k1w.update(xready)
            if second and i >= 1:
                k1w["MM1"] = p.c("MM1")
            elif i >= 2:
                k1w["MM1"] = p.c("MM1") - NHC
            def _k1(v, sl=sl):
                last = None
                for c in range(NCH):
                    v.tensor_tensor(sacc[:], xt[sl][:, CS(c)], repA[:, 0:TAU],
                                    ALU.mult)
                    v.tensor_tensor(sacc[:], sacc[:], repA[:, TAU:2 * TAU],
                                    ALU.subtract)
                    v.tensor_scalar(sacc[:], sacc[:], RNDC, RNDC, ALU.add,
                                    ALU.subtract)
                    last = v.tensor_scalar(k1[sl][:, CS(c)], sacc[:], 127.0,
                                           -128.0, ALU.min, ALU.max)
                return last
            if debug_stage <= 2:
                continue
            p.add("vector", _k1, waits=k1w, inc="K1")
            if debug_stage <= 3:
                continue

            base_hev = p.c("HEV")
            base_hgd = p.c("HGD")
            for mc in range(NHC):
                w1w = {"K1": p.c("K1")}
                w1w.update(W["wq_done"] if i == 0 and mc == 0 else {})
                if mc >= 2:
                    w1w["HEV"] = base_hev + mc - 1
                elif i >= 1:
                    w1w["HEV"] = base_hev
                if mc < 2:
                    w1w["MM1"] = p.c("MM1")
                def _mm1(t, mc=mc, sl=sl):
                    last = None
                    for kc in range(NCH):
                        lt = w1q[:, kc * HID + mc * 128:
                                 kc * HID + (mc + 1) * 128]
                        last = t.matmul(
                            hps[mc % 2][:], lt,
                            k1[sl][:, CS(kc)],
                            start=(kc == 0), stop=(kc == NCH - 1))
                    return last
                p.add("tensor", _mm1, waits=w1w, inc="MM1")
                hevw = {"MM1": p.c("MM1"), "EC1": p.c("EC1")}
                if i >= 1 and mc < 2:
                    hevw["HGD"] = base_hgd
                p.add("vector",
                      (lambda mc=mc: lambda v: v.tensor_tensor(
                          htmp[mc % 2][:],
                          hps[mc % 2][:],
                          e1sb[:], ALU.mult))(),
                      waits=hevw, inc="HEV")
                gw = {"HEV": p.c("HEV")}
                if i >= 1:
                    gw["K2"] = p.c("K2")
                    gw["AMX"] = p.c("AMX")
                def _gelu(sc, mc=mc):
                    sc.activation(hbuf[:, mc * TAU:(mc + 1) * TAU],
                                  htmp[mc % 2][:], AF.Gelu,
                                  bias=b1sb[:, mc:mc + 1])
                    return sc.activation(hsq[mc % 2][:],
                                         hbuf[:, mc * TAU:(mc + 1) * TAU],
                                         AF.Square)
                p.add("scalar", _gelu, waits=gw, inc="HGD")
                sqw = {"HGD": p.c("HGD")}
                if mc == 0 and i >= 1:
                    sqw["RE"] = p.c("RE")
                    sqw["LNV"] = p.c("LNV")
                p.add("tensor",
                      (lambda mc=mc: lambda t: t.matmul(
                          stat2b[0:1, 0:TAU], onesc[:], hsq[mc % 2][:],
                          start=(mc == 0), stop=(mc == NHC - 1)))(),
                      waits=sqw, inc="SQS")

            if debug_stage <= 4:
                continue
            p.add("vector",
                  lambda v: v.tensor_reduce(
                      hmax[:], _ap(hbuf).rearrange("p (c t) -> p t c", c=NHC),
                      AX.X, ALU.max, apply_absolute_value=True),
                  waits={"HGD": p.c("HGD"), "TPM": p.c("TPM")},
                  inc="AMX")
            abase = p.c("TPA")
            arbase = p.c("CRA")
            p.add("tensor",
                  lambda t: t.matmul(repA[:, 0:128], hmax[:, 0:128],
                                     W["idm"][:], start=True, stop=True,
                                     is_transpose=True),
                  waits={"AMX": p.c("AMX"), "K1": p.c("K1"),
                         "RM1": p.c("RM1")}, inc="TPA")
            p.add("tensor",
                  lambda t: t.matmul(repB[:, 0:128], hmax[:, 128:256],
                                     W["idm"][:], start=True, stop=True,
                                     is_transpose=True),
                  waits={"AMX": p.c("AMX"), "EC1": p.c("EC1"),
                         "RM1": p.c("RM1")}, inc="TPA")
            p.add("vector",
                  lambda v: v.tensor_reduce(colb[:, 0:1], repA[:, 0:128],
                                            AX.X, ALU.max),
                  waits={"TPA": abase + 1, "EXM": p.c("EXM")}, inc="CRA")
            p.add("vector",
                  lambda v: v.tensor_reduce(colb[:, 1:2], repB[:, 0:128],
                                            AX.X, ALU.max),
                  waits={"TPA": abase + 2}, inc="CRA")
            def _exta(t):
                t.matmul(repA[0:1, TAU:TAU + 128], colb[:, 0:1], W["idm"][:],
                         start=True, stop=True, is_transpose=True)
                return t.matmul(repA[0:1, TAU + 128:2 * TAU], colb[:, 1:2],
                                W["idm"][:], start=True, stop=True,
                                is_transpose=True)
            p.add("tensor", _exta,
                  waits={"CRA": arbase + 2, "RM1": p.c("RM1")}, inc="EXA")
            p.add("vector",
                  lambda v: v.tensor_scalar(rw["h2"][:], stat2b[0:1, 0:TAU],
                                            1.0 / HID, None, ALU.mult),
                  waits={"SQS": p.c("SQS")}, inc="RE")
            p.add("scalar",
                  lambda sc: sc.activation(rw["q2"][:], rw["h2"][:], AF.Sqrt,
                                           bias=epsB[:]),
                  waits={"RE": p.c("RE")}, inc="RF")
            def _rm2(v):
                v.reciprocal(rw["q2"][:], rw["q2"][:])
                v.tensor_tensor(rw["a2"][:], rw["q2"][:], rw["A2"][:],
                                ALU.mult)
                v.tensor_scalar(rw["aa2"][:], rw["a2"][:], EPS5, None,
                                ALU.max)
                v.reciprocal(rw["s2"][:], rw["aa2"][:])
                v.tensor_scalar(rw["s2"][:], rw["s2"][:], 127.0, None,
                                ALU.mult)
                v.tensor_tensor(rw["S2"][:], rw["q2"][:], rw["s2"][:],
                                ALU.mult)
                return v.tensor_scalar(rw["e2t"][:], rw["aa2"][:],
                                       mwc[0:1, 1:2], None, ALU.mult)
            p.add("vector", _rm2, waits={"RF": p.c("RF"), "EXA": p.c("EXA")},
                  inc="RM2")
            bc2w = {"RM2": p.c("RM2"), "EC1": p.c("EC1"),
                    "BC2": p.c("BC2")}
            if i >= 1:
                bc2w["K2"] = p.c("K2")
            def _bc2(t):
                t.matmul(repB[:, TAU:2 * TAU], onesf[:], rw["S2"][:],
                         start=True, stop=True)
                return t.matmul(repB[:, 0:TAU], onesf[:], rw["e2t"][:],
                                start=True, stop=True)
            p.add("tensor", _bc2, waits=bc2w, inc="BC2")
            ec2w = {"BC2": p.c("BC2")}
            if i >= 1:
                ec2w["OPC"] = p.c("OPC")
            p.add("scalar",
                  lambda sc: sc.copy(e2sb[:], repB[:, 0:TAU]),
                  waits=ec2w, inc="EC2")

            if debug_stage <= 5:
                continue
            base_mm2 = p.c("MM2")
            k2w = {"BC2": p.c("BC2"), "EC2": p.c("EC2")}
            if i >= 1:
                k2w["MM2"] = base_mm2
            def _k2(v):
                last = None
                for hc in range(NHC):
                    v.tensor_tensor(sacc[:],
                                    hbuf[:, hc * TAU:(hc + 1) * TAU],
                                    repB[:, TAU:2 * TAU], ALU.mult)
                    v.tensor_scalar(sacc[:], sacc[:], RNDC, RNDC,
                                    ALU.add, ALU.subtract)
                    last = v.tensor_scalar(
                        k2[:, hc * TAU:(hc + 1) * TAU],
                        sacc[:], 127.0, -128.0, ALU.min, ALU.max)
                return last
            p.add("vector", _k2, waits=k2w, inc="K2")

            if debug_stage <= 6:
                continue
            for oc in range(NCH):
                m2w = {"K2": p.c("K2")}
                if i == 0 and oc == 0:
                    m2w.update(W["wq_done"])
                if oc >= 2:
                    m2w["OPC"] = p.c("OPC") - 1
                elif second or i >= 1:
                    m2w["OPC"] = p.c("OPC")
                def _mm2(t, oc=oc):
                    last = None
                    for kc in range(NHC):
                        lt = w2q[:, kc * D + oc * 128:kc * D + (oc + 1) * 128]
                        last = t.matmul(
                            ops[oc % 2][:], lt,
                            k2[:, kc * TAU:(kc + 1) * TAU],
                            start=(kc == 0), stop=(kc == NHC - 1))
                    return last
                p.add("tensor", _mm2, waits=m2w, inc="MM2")
                ew = {"MM2": p.c("MM2"), "EC2": p.c("EC2"), "bias_ok": BOK}
                if i >= 1 and oc == 0:
                    ew["STx1"] = p.c("STx1")
                    ew["SQMs"] = p.c("SQMs")
                    ew["SC2"] = p.c("SC2")
                def _epi(v, oc=oc, sl=sl):
                    v.tensor_tensor(x1t[:, CS(oc)],
                                    ops[oc % 2][:],
                                    e2sb[:], ALU.mult)
                    return v.scalar_tensor_tensor(
                        x1t[:, CS(oc)], x1t[:, CS(oc)], b2sb[:, oc:oc + 1],
                        xt[sl][:, CS(oc)], ALU.add, ALU.add)
                p.add("vector", _epi, waits=ew, inc="OPC")
            p.add("vector", lambda v: None, waits={"OPC": p.c("OPC")},
                  inc="X1")

            if debug_stage <= 7:
                continue
            if second:
                p.dma("gpsimd",
                      (lambda t0=t0: lambda gp: gp.dma_start(
                          out=fm_dram(out_fm, t0, TAU),
                          in_=fm_sbuf(x1t)))(),
                      waits={"X1": p.c("X1")}, stream="STx1")
                continue
            cguard = {"X1": p.c("X1")}
            if i >= 1:
                cguard["LNS"] = p.c("LNS")
            p.add("scalar",
                  lambda sc: sc.activation(x1sq[:], x1t[:], AF.Square),
                  waits=dict(cguard), inc="SQMs")
            sc2w = {"SQMs": p.c("SQMs"), "X1": p.c("X1"),
                    "STT": p.c("STT")}
            def _chunksums2(v):
                v.tensor_reduce(sacc[:],
                                x1t.ap().rearrange("p (c t) -> p t c",
                                                   c=NCH),
                                AX.X, ALU.add)
                return v.tensor_reduce(
                    sacc2[:], x1sq.ap().rearrange("p (c t) -> p t c",
                                                  c=NCH),
                    AX.X, ALU.add)
            p.add("vector", _chunksums2, waits=sc2w, inc="SC2")
            lnw = {"SC2": p.c("SC2"), "STT": p.c("STT"),
                   "LNS": p.c("LNS")}
            if i >= 1:
                lnw["LNR"] = p.c("LNR")
            def _ln2stats(t):
                t.matmul(stat3b[0:1, 0:TAU], onesc[:], sacc[:],
                         start=True, stop=True)
                return t.matmul(stat2b[0:1, TAU:2 * TAU], onesc[:], sacc2[:],
                                start=True, stop=True)
            p.add("tensor", _ln2stats, waits=lnw, inc="LNS")
            def _lnrows(v):
                v.tensor_scalar(rw["m2"][:], stat3b[0:1, 0:TAU], 1.0 / D,
                                None, ALU.mult)
                v.tensor_scalar(m2row[:], rw["m2"][:], 1.0, None,
                                ALU.mult)
                v.tensor_scalar(rw["v"][:], stat2b[0:1, TAU:2 * TAU], 1.0 / D,
                                None, ALU.mult)
                v.tensor_tensor(rw["t3"][:], rw["m2"][:], rw["m2"][:],
                                ALU.mult)
                return v.tensor_tensor(rw["v"][:], rw["v"][:], rw["t3"][:],
                                       ALU.subtract)
            rwW = {"LNS": p.c("LNS"), "RA": p.c("RA"), "BC1": p.c("BC1"),
                   "STm2": p.c("STm2"), "STr2": p.c("STr2")}
            p.add("vector", _lnrows, waits=rwW, inc="LNV")
            p.add("scalar",
                  lambda sc: sc.activation(rw["lnt"][:], rw["v"][:], AF.Sqrt,
                                           bias=epsA[:]),
                  waits={"LNV": p.c("LNV")}, inc="LNQ")
            p.add("vector",
                  lambda v: v.reciprocal(r2row[:], rw["lnt"][:]),
                  waits={"LNQ": p.c("LNQ")}, inc="LNR")

            p.dma("gpsimd",
                  (lambda t0=t0: lambda gp: gp.dma_start(
                      out=fm_dram(x1_out, t0, TAU),
                      in_=fm_sbuf(x1t)))(),
                  waits={"X1": p.c("X1")}, stream="STx1")
            p.dma("gpsimd",
                  (lambda t0=t0: lambda gp: gp.dma_start(
                      out=rows_out[0:1, t0:t0 + TAU], in_=m2row[:]))(),
                  waits={"LNR": p.c("LNR")}, stream="STm2")
            p.dma("gpsimd",
                  (lambda t0=t0: lambda gp: gp.dma_start(
                      out=rows_out[1:2, t0:t0 + TAU], in_=r2row[:]))(),
                  waits={"LNR": p.c("LNR")}, stream="STr2")

        if False and DEBUG_DUMPS and n_subtiles == 1 and not second:
            dbg = {
                "dbg_k1": (k1[0], BF16, NCH * TAU),
                "dbg_h": (hbuf, F32, NHC * TAU),
                "dbg_k2": (k2, BF16, NHC * TAU),
                "dbg_e1": (e1sb, F32, TAU),
                "dbg_e2": (e2sb, F32, TAU),
            }
            allw = {s: p.c(s) for s in ("X1", "K1", "K2", "HGD", "EC1",
                                        "EC2") if s in p.sems}
            for nm, (tt, dt, w) in dbg.items():
                od = nc.dram_tensor(nm, [128, w], dt, kind="ExternalOutput").ap()
                p.dma("gpsimd",
                      (lambda od=od, tt=tt, w=w: lambda gp: gp.dma_start(
                          out=od[:], in_=tt.ap()[:, 0:w] if hasattr(tt, 'ap')
                          else tt[:, 0:w]))(),
                      waits=dict(allw), stream="STdbg")
            rowd = nc.dram_tensor("dbg_rows", [16, TAU], F32,
                                  kind="ExternalOutput").ap()
            for ri, nm in enumerate(("m", "v", "r", "q", "R", "Mx", "Mn",
                                     "aa", "S1", "mS1", "e1", "h2", "q2",
                                     "aa2", "S2", "e2t")):
                p.dma("gpsimd",
                      (lambda nm=nm, ri=ri: lambda gp: gp.dma_start(
                          out=rowd[ri:ri + 1, :], in_=rw[nm][:]))(),
                      waits=dict(allw), stream="STdbg")

    return None



def emit_lru_phase(p, nc, pes, io, W, gate=None):
    gate = dict(gate or {})
    x1_in, rows_in = io["x1_in"], io["rows_in"]
    wit, wgt = io["wit"], io["wgt"]
    biv, bgv = io["biv"], io["bgv"]
    hloc_out, p_out = io["hloc_out"], io["p_out"]
    if True:
        sb = lambda *a: pes.enter_context(nc.sbuf_tensor(*a))
        witb = sb("witb", [128, NCH * D], BF16)
        wgtb = sb("wgtb", [128, NCH * D], BF16)
        stage = sb("lstage", [128, D], F32)
        onesf = W["onesf"]
        bisb = sb("bisb", [128, NCH], F32)
        bgsb = sb("bgsb", [128, NCH], F32)
        onecol = sb("onecol", [128, 1], F32)
        zeros = sb("zeros", [128, TAU2], F32)
        x1bt = [sb("x1bt0", [128, NCH * TAU2], F32)] * 2
        rmt = [sb("rmt0", [1, TAU2], F32)] * 2
        rrt = [sb("rrt0", [1, TAU2], F32)] * 2
        xlf = sb("xlf", [128, TAU2], F32)
        xln = sb("xln", [128, NCH * TAU2], BF16)
        ur = [sb(f"ur{i}", [128, TAU2], F32) for i in range(2)]
        gr = [sb(f"gr{i}", [128, TAU2], F32) for i in range(2)]
        ogr = [sb(f"ogr{i}", [128, TAU2], F32) for i in range(2)]
        br = [sb(f"br{i}", [128, TAU2], F32) for i in range(2)]
        hb = [io["bigbuf"].ap()[:, 0:NCH * TAU2]] * 2
        pb = [io["bigbuf"].ap()[:, NCH * TAU2:2 * NCH * TAU2]] * 2
        pbf = sb("pbf", [128, NCH * TAU2], BF16)
        hinit = [sb(f"hinit{c}", [128, 1], F32) for c in range(NCH)]
        pinit = [sb(f"pinit{c}", [128, 1], F32) for c in range(NCH)]


        pt = lambda *a: pes.enter_context(nc.psum_tensor(*a))
        ups = pt("ups", [128, 2 * TAU2], F32)
        gps = pt("gps", [128, 2 * TAU2], F32)
        repm = pt("repm", [128, TAU2], F32)
        repr_ = pt("repr", [128, TAU2], F32)

        p.add("vector", lambda v: v.memset(onecol[:], 1.0))
        p.add("vector", lambda v: v.memset(zeros[:], 0.0), inc="init")
        p.dma("sync", lambda s: s.dma_start(out=bisb[:], in_=biv[:]),
              waits=dict(gate), stream="ld0")
        p.dma("sync", lambda s: s.dma_start(out=bgsb[:], in_=bgv[:]),
              waits=dict(gate), stream="ld0")
        LD0C = p.c("ld0")
        for wi, (wt, wb) in enumerate(((wit, witb), (wgt, wgtb))):
            for kc in range(NCH):
                wldw = {"wcast": p.c("wcast")}
                if wi == 0 and kc == 0:
                    wldw.update(gate)
                p.dma("sync",
                      (lambda wt=wt, kc=kc: lambda s: s.dma_start(
                          out=stage[:], in_=wt[kc]))(),
                      waits=wldw, stream="wld")
                p.add("scalar",
                      (lambda wb=wb, kc=kc: lambda sc: sc.copy(
                          wb[:, kc * D:(kc + 1) * D], stage[:]))(),
                      waits={"wld": 16 * (p.c("wld") // 16)}, inc="wcast")

        def fm_dram(t, t0, n):
            return t.rearrange("c p t -> p c t")[:, :, t0:t0 + n]

        def fm_sbuf(t, tau):
            return _ap(t).rearrange("p (c t) -> p c t", c=NCH)

        for j in range(NST2):
            t0 = j * TAU2
            sl = j % 2
            CS = lambda c: slice(c * TAU2, (c + 1) * TAU2)

            ldw = {}
            if j == 0:
                ldw.update(gate)
            if j >= 1:
                ldw["XLN"] = p.c("XLN")
                ldw["BCM"] = p.c("BCM")
            p.dma("sync",
                  (lambda sl=sl, t0=t0: lambda s: s.dma_start(
                      out=fm_sbuf(x1bt[sl], TAU2),
                      in_=fm_dram(x1_in, t0, TAU2)))(),
                  waits=ldw, stream="Lx")
            p.dma("sync",
                  (lambda sl=sl, t0=t0: lambda s: s.dma_start(
                      out=rmt[sl][:], in_=rows_in[0:1, t0:t0 + TAU2]))(),
                  waits=dict(ldw), stream="Lm")
            p.dma("sync",
                  (lambda sl=sl, t0=t0: lambda s: s.dma_start(
                      out=rrt[sl][:], in_=rows_in[1:2, t0:t0 + TAU2]))(),
                  waits=dict(ldw), stream="Lr")
            lnow = {"Lx": p.c("Lx"), "Lm": p.c("Lm"), "Lr": p.c("Lr")}

            bw = dict(lnow); bw["init"] = p.c("init")
            if j >= 1:
                bw["XLN"] = p.c("XLN")
            def _bc(t, sl=sl):
                t.matmul(repm[:], onesf[:], rmt[sl][:], start=True,
                         stop=True)
                return t.matmul(repr_[:], onesf[:], rrt[sl][:],
                                start=True, stop=True)
            p.add("tensor", _bc, waits=bw, inc="BCM")

            xw = {"BCM": p.c("BCM")}
            if j >= 1:
                xw["UMM"] = p.c("UMM")
            def _xln(v, sl=sl):
                last = None
                for c in range(NCH):
                    v.tensor_tensor(xlf[:], x1bt[sl][:, CS(c)], repm[:],
                                    ALU.subtract)
                    last = v.tensor_tensor(xln[:, CS(c)], xlf[:], repr_[:],
                                          ALU.mult)
                return last
            p.add("vector", _xln, waits=xw, inc="XLN")

            for oc in range(NCH):
                mw = {"XLN": p.c("XLN"), "wcast": 2 * NCH,
                      "UMM": p.c("UMM")}
                if oc >= 2 or j >= 1:
                    mw["UGD"] = p.c("UGD") - (1 if oc >= 2 else 0)
                def _umm(t, oc=oc):
                    for kc in range(NCH):
                        lt = witb[:, kc * D + oc * 128:kc * D + (oc + 1) * 128]
                        t.matmul(ups[:, (oc % 2) * TAU2:(oc % 2 + 1) * TAU2],
                                 lt, xln[:, CS(kc)], start=(kc == 0),
                                 stop=(kc == NCH - 1))
                    last = None
                    for kc in range(NCH):
                        lt = wgtb[:, kc * D + oc * 128:kc * D + (oc + 1) * 128]
                        last = t.matmul(
                            gps[:, (oc % 2) * TAU2:(oc % 2 + 1) * TAU2],
                            lt, xln[:, CS(kc)], start=(kc == 0),
                            stop=(kc == NCH - 1))
                    return last
                p.add("tensor", _umm, waits=mw, inc="UMM")

                def _acts(sc, oc=oc):
                    sc.activation(ur[oc % 2][:],
                                  ups[:, (oc % 2) * TAU2:(oc % 2 + 1) * TAU2],
                                  AF.Identity, bias=bisb[:, oc:oc + 1])
                    sc.activation(gr[oc % 2][:],
                                  gps[:, (oc % 2) * TAU2:(oc % 2 + 1) * TAU2],
                                  AF.Sigmoid, bias=bgsb[:, oc:oc + 1])
                    return sc.activation(ogr[oc % 2][:], gr[oc % 2][:],
                                         AF.Identity, bias=onecol[:],
                                         scale=-1.0)
                aw = {"UMM": p.c("UMM"), "ld0": LD0C}
                if j >= 1 or oc >= 2:
                    aw["SCN"] = p.c("SCN") - (1 if oc >= 2 else 0)
                p.add("scalar", _acts, waits=aw, inc="UGD")

                def _bin(v, oc=oc):
                    return v.tensor_tensor(br[oc % 2][:], ogr[oc % 2][:],
                                           ur[oc % 2][:], ALU.mult)
                p.add("vector", _bin, waits={"UGD": p.c("UGD")}, inc="BIN")

                def _scan(v, oc=oc, sl=sl, j=j):
                    if j == 0:
                        ih, ip = 0.0, 1.0
                    else:
                        ih, ip = hinit[oc][:], pinit[oc][:]
                    v.tensor_tensor_scan(hb[sl][:, CS(oc)], gr[oc % 2][:],
                                         br[oc % 2][:], ih, ALU.mult, ALU.add)
                    v.tensor_tensor_scan(pb[sl][:, CS(oc)],
                                         gr[oc % 2][:], zeros[:], ip,
                                         ALU.mult, ALU.add)
                    v.tensor_copy(hinit[oc][:],
                                  hb[sl][:, oc * TAU2 + TAU2 - 1:
                                         oc * TAU2 + TAU2])
                    return v.tensor_copy(pinit[oc][:],
                                         pb[sl][:, oc * TAU2 + TAU2 - 1:
                                                oc * TAU2 + TAU2])
                sw = {"BIN": p.c("BIN"), "init": p.c("init")}
                if j >= 1 and oc == 0:
                    sw["SH"] = p.c("SH")
                    sw["SP"] = p.c("SP")
                p.add("vector", _scan, waits=sw, inc="SCN")

            p.add("scalar", lambda sc: sc.copy(carry_sb0[:], carry_sb0[:])
                  if False else None, waits={"SCN": p.c("SCN")}, inc="HCP") \
                if False else None
            p.dma("gpsimd",
                  (lambda t0=t0, sl=sl: lambda gp: gp.dma_start(
                      out=fm_dram(hloc_out, t0, TAU2),
                      in_=fm_sbuf(hb[sl], TAU2)))(),
                  waits={"SCN": p.c("SCN")}, stream="SH")
            cwp = {"SCN": p.c("SCN")}
            if j >= 1:
                cwp["SP"] = p.c("SP")
            p.add("scalar",
                  (lambda sl=sl: lambda sc: sc.activation(
                      pbf[:], pb[sl][:], AF.Identity))(),
                  waits=cwp, inc="HCP")
            p.dma("gpsimd",
                  (lambda t0=t0, sl=sl: lambda gp: gp.dma_start(
                      out=fm_dram(p_out, t0, TAU2),
                      in_=fm_sbuf(pbf, TAU2)))(),
                  waits={"HCP": p.c("HCP")}, stream="SP")
            if j == NST2 - 1:
                carry_sb = sb("carry_sb", [128, NCH], F32)
                def _carrycp(v):
                    last = None
                    for c in range(NCH):
                        last = v.tensor_copy(carry_sb[:, c:c + 1],
                                             hinit[c][:])
                    return last
                p.add("vector", _carrycp,
                      waits={"SCN": p.c("SCN")}, inc="CCP")
                p.dma("gpsimd",
                      lambda gp: gp.dma_start(out=io["carry_out"][:],
                                              in_=carry_sb[:]),
                      waits={"CCP": p.c("CCP")}, stream="SC")
    return



def build_fused():
    nc = bass.Bass("TRN2", target_bir_lowering=False)
    es = ExitStack()
    dram = lambda name, shape, dt, kind="Internal", space="Local": \
        nc.dram_tensor(name, shape, dt, kind=kind, addr_space=space).ap()

    io = {}
    io["w1t"] = dram("w1t", [NCH, 128, HID], F32, "ExternalInput")
    io["w2t"] = dram("w2t", [NHC, 128, D], F32, "ExternalInput")
    io["wit"] = dram("wit", [NCH, 128, D], F32, "ExternalInput")
    io["wgt"] = dram("wgt", [NCH, 128, D], F32, "ExternalInput")
    io["wot"] = dram("wot", [NCH, 128, D], F32, "ExternalInput")
    io["b1v"] = dram("b1v", [128, NHC], F32, "ExternalInput")
    io["b2v"] = dram("b2v", [128, NCH], F32, "ExternalInput")
    io["biv"] = dram("biv", [128, NCH], F32, "ExternalInput")
    io["bgv"] = dram("bgv", [128, NCH], F32, "ExternalInput")
    io["bov"] = dram("bov", [128, NCH], F32, "ExternalInput")
    io["selv"] = dram("selv", [128, 2], F32, "ExternalInput")
    io["idm"] = dram("idm", [128, 128], F32, "ExternalInput")
    io["x_in"] = dram("x_in", [NCH, 128, TPC], F32, "ExternalInput")
    io["out_fm"] = dram("out_fm", [NCH, 128, TPC], F32, "ExternalOutput")
    io["x1_out"] = io["x1_in"] = dram("x1d", [NCH, 128, TPC], F32)
    io["rows_out"] = io["rows_in"] = dram("rowsd", [2, TPC], F32)
    io["hloc_out"] = io["hloc_in"] = dram("hd", [NCH, 128, TPC], F32)
    io["p_out"] = io["p_in"] = dram("pd", [NCH, 128, TPC], BF16)
    io["carry_out"] = dram("carryd", [128, NCH], F32)
    io["carry_gath"] = dram("cgath", [2, 128, NCH], F32, "Internal", "Local")

    with es:
        p = Prog(nc, es)
        sb = lambda *a: es.enter_context(nc.sbuf_tensor(*a))
        bigbuf = sb("bigbuf", [128, NHC * TAU], F32)
        io["bigbuf"] = bigbuf
        stage = bigbuf.ap()[:, 0:HID]
        wabs = bigbuf.ap()[:, HID:2 * HID]
        W = emit_weight_quant(p, sb, io, stage, wabs)
        idsb = sb("idsb", [128, 128], F32)
        W["idm"] = idsb
        p.dma("sync", lambda s: s.dma_start(out=idsb[:], in_=io["idm"][:]),
              stream="Lid")
        wotb = sb("wotb", [128, NCH * D], BF16)
        io["wotb"] = wotb
        wostage = stage[:, 0:D]
        for kc in range(NCH):
            wol = {"wo_cast": p.c("wo_cast")}
            if kc == 0:
                wol["wq_q"] = NCH + NHC
            p.dma("sync",
                  (lambda kc=kc: lambda s: s.dma_start(
                      out=wostage, in_=io["wot"][kc]))(),
                  waits=wol, stream="wo_ld")
            p.add("scalar",
                  (lambda kc=kc: lambda sc: sc.copy(
                      wotb[:, kc * D:(kc + 1) * D], wostage))(),
                  waits={"wo_ld": p.c("wo_ld")}, inc="wo_cast")
        selsb = sb("selsb", [128, 2], F32)
        gathsb = sb("gathsb", [128, 2 * NCH], F32)
        hcar = sb("hcar", [128, NCH], F32)
        io["hcar"] = hcar
        p.dma("sync", lambda s: s.dma_start(out=selsb[:], in_=io["selv"][:]),
              stream="ld0")

        with ExitStack() as pes:
            emit_ffn_phase(p, nc, pes, W, io, second=False,
                           gate={"wo_cast": NCH})
        g1 = {"STx1": p.c("STx1"), "STm2": p.c("STm2"),
              "STr2": p.c("STr2"), "LNR": p.c("LNR")}

        with ExitStack() as pes:
            emit_lru_phase(p, nc, pes, io, W, gate=g1)

        p.add("gpsimd",
              lambda gp: gp.collective_compute(
                  "AllGather", ALU.bypass,
                  replica_groups=[[0, 1], [2, 3], [4, 5], [6, 7]],
                  ins=[io["carry_out"][:]], outs=[io["carry_gath"][:]]),
              waits={"SC": p.c("SC")}, inc="CCG")
        p.dma("gpsimd",
              lambda gp: gp.dma_start(out=gathsb[:, 0:NCH],
                                      in_=io["carry_gath"][0]),
              waits={"CCG": p.c("CCG")}, stream="LG")
        p.dma("gpsimd",
              lambda gp: gp.dma_start(out=gathsb[:, NCH:2 * NCH],
                                      in_=io["carry_gath"][1]),
              waits={"CCG": p.c("CCG")}, stream="LG")
        def _sel(v):
            v.tensor_scalar(hcar[:], gathsb[:, 0:NCH], selsb[:, 0:1], None,
                            ALU.mult)
            return v.scalar_tensor_tensor(hcar[:], gathsb[:, NCH:2 * NCH],
                                          selsb[:, 1:2], hcar[:],
                                          ALU.mult, ALU.add)
        p.add("vector", _sel,
              waits={"LG": p.c("LG"), "ld0": p.c("ld0")}, inc="HCAR")
        g2 = {"SH": p.c("SH"), "SP": p.c("SP"), "HCAR": p.c("HCAR"),
              "HCP": p.c("HCP"), "CCP": p.c("CCP")}

        with ExitStack() as pes:
            emit_ffn_phase(p, nc, pes, W, io, second=True, gate=g2)

        fin = {"STx1": p.c("STx1")}
        p.add("gpsimd", lambda gp: None, waits=fin)
        p.realize()
    return nc



def _prep_host(inputs):
    x = np.ascontiguousarray(np.asarray(inputs["L0_attn_x"], dtype=np.float32))
    w1 = np.asarray(inputs["L0_ffn_w1"], np.float32)
    w2 = np.asarray(inputs["L0_ffn_w2"], np.float32)
    wi = np.asarray(inputs["L0_attn_Wi"], np.float32)
    wg = np.asarray(inputs["L0_attn_Wg"], np.float32)
    wo = np.asarray(inputs["L0_attn_Wo"], np.float32)
    C = lambda a: np.ascontiguousarray(a)
    g = {
        "w1t": C(w1.T.reshape(NCH, 128, HID)),
        "w2t": C(w2.T.reshape(NHC, 128, D)),
        "wit": C(wi.T.reshape(NCH, 128, D)),
        "wgt": C(wg.T.reshape(NCH, 128, D)),
        "wot": C(wo.T.reshape(NCH, 128, D)),
        "b1v": C(np.asarray(inputs["L0_ffn_b1"], np.float32)
                 .reshape(NHC, 128).T),
        "b2v": C(np.asarray(inputs["L0_ffn_b2"], np.float32)
                 .reshape(NCH, 128).T),
        "biv": C(np.asarray(inputs["L0_attn_bi"], np.float32)
                 .reshape(NCH, 128).T),
        "bgv": C(np.asarray(inputs["L0_attn_bg"], np.float32)
                 .reshape(NCH, 128).T),
        "bov": C(np.asarray(inputs["L0_attn_bo"], np.float32)
                 .reshape(NCH, 128).T),
    }
    xs = []
    for c in range(NCORE):
        s, hc = c // 2, c % 2
        sl = x[s, hc * TPC:(hc + 1) * TPC, :]
        xs.append(C(sl.T.reshape(NCH, 128, TPC)))
    return g, xs


def kernel(**inputs):
    g, xs = _prep_host(inputs)
    cores = list(range(NCORE))
    nc = build_fused()
    sel0 = np.zeros((128, 2), np.float32)
    sel1 = np.zeros((128, 2), np.float32)
    sel1[:, 0] = 1.0
    maps = []
    for c in cores:
        maps.append({
            "w1t": g["w1t"], "w2t": g["w2t"], "wit": g["wit"],
            "wgt": g["wgt"], "wot": g["wot"], "b1v": g["b1v"],
            "b2v": g["b2v"], "biv": g["biv"], "bgv": g["bgv"],
            "bov": g["bov"], "x_in": xs[c],
            "selv": sel1 if c % 2 else sel0,
            "idm": np.eye(128, dtype=np.float32),
        })
    res = run_bass_kernel_spmd(nc, maps, core_ids=cores).results
    out = np.empty((BATCH, SEQ, D), np.float32)
    for c in cores:
        s, hc = c // 2, c % 2
        o = np.asarray(res[c]["out_fm"]).reshape(D, TPC)
        out[s, hc * TPC:(hc + 1) * TPC, :] = o.T
    return out


# version 10 (final)
# speedup vs baseline: 1.2053x; 1.0038x over previous
import numpy as np
import ml_dtypes
from contextlib import ExitStack

import concourse.bass as bass
import concourse.mybir as mybir
from concourse.bass_utils import run_bass_kernel_spmd

F32 = mybir.dt.float32
BF16 = mybir.dt.bfloat16
FP8 = mybir.dt.float8e4
AF = mybir.ActivationFunctionType
ALU = mybir.AluOpType
AX = mybir.AxisListType

D = 1024
HID = 4096
NCH = D // 128
NHC = HID // 128
SEQ = 8192
BATCH = 4
NCORE = 8
TPC = SEQ // 2
TAU = 256
NST = TPC // TAU
TAU2 = 512
NST2 = TPC // TAU2
RNDC = 1.5 * 2.0**23
EPS5 = 1e-5
EPS6 = 1e-6

ENGINES = ("sync", "scalar", "vector", "tensor", "gpsimd")

import os as _os
CHAIN_EDGES = _os.environ.get("BASS_SIM_FP64") == "1" or \
    _os.environ.get("KERNEL_FORCE_CHAIN_EDGES") == "1"
DEBUG_DUMPS = False



class Prog:

    def __init__(self, nc, es):
        self.nc = nc
        self.es = es
        self.ops = {e: [] for e in ENGINES}
        self.sems = {}
        self.cnt = {}

    def sem(self, name):
        if name not in self.sems:
            self.sems[name] = self.es.enter_context(self.nc.semaphore(name))
            self.cnt[name] = 0
        return name

    def add(self, eng, emit, waits=None, inc=None, amt=1):
        w = dict(waits or {})
        incs = []
        if inc is not None:
            self.sem(inc)
            self.cnt[inc] += amt
            incs = [(inc, amt)]
        for s in w:
            self.sem(s)
        self.ops[eng].append((w, emit, incs))

    def c(self, name):
        self.sem(name)
        return self.cnt[name]

    def dma(self, eng, emit, waits=None, stream="L"):
        w = dict(waits or {})
        self.sem(stream)
        w[stream] = self.cnt[stream]
        self.add(eng, emit, w, inc=stream, amt=16)

    def realize(self):
        with self.nc.Block() as blk:
            for eng in ENGINES:
                seq = self.ops[eng]
                if not seq:
                    continue
                chain_sem = None
                if CHAIN_EDGES and eng in ("vector", "scalar", "gpsimd"):
                    chain_sem = self.es.enter_context(
                        self.nc.semaphore(f"chain_{eng}"))

                def make(seq, chain_sem):
                    def body(e):
                        ee = _ChainEngine(e, chain_sem) if chain_sem else e
                        for (waits, emit, incs) in seq:
                            if chain_sem:
                                ee.flush_pending()
                            for s, v in waits.items():
                                if v > 0:
                                    e.wait_ge(self.sems[s], v)
                            ins = emit(ee)
                            for s, a in incs:
                                if ins is None:
                                    ins = e.engine_nop()
                                    if chain_sem:
                                        ee.chain_bare(ins)
                                if chain_sem:
                                    ee.take_inc(ins, self.sems[s], a)
                                else:
                                    ins.then_inc(self.sems[s], a)
                        if chain_sem:
                            ee.finalize()
                    return body
                getattr(blk, eng)(make(list(seq), chain_sem))


class _ChainEngine:

    _SKIP = {"dma_start", "wait_ge", "then_inc", "engine_nop", "bass",
             "value_load", "partition_id", "flush_pending", "take_inc",
             "finalize", "chain_bare"}

    def __init__(self, eng, sem):
        self._eng = eng
        self._sem = sem
        self._n = 0
        self._last = None
        self._pending = []
        self._reccnt = {}

    def _finalize_last(self):
        if self._last is not None:
            self._last.then_inc(self._sem, 1)
            self._n += 1
            self._last = None

    def flush_pending(self):
        for (s, v) in self._pending:
            self._eng.wait_ge(s, v)
        self._pending = []

    def take_inc(self, ins, sem_handle, amt):
        key = id(sem_handle)
        self._reccnt[key] = self._reccnt.get(key, 0) + amt
        if ins is self._last:
            self._last = None
            ins.then_inc(sem_handle, amt)
            self._pending.append((sem_handle, self._reccnt[key]))
        else:
            ins.then_inc(sem_handle, amt)

    def chain_bare(self, ins):
        self._finalize_last()
        if self._n > 0:
            ins._wait_ge(self._sem, self._n)

    def finalize(self):
        self._finalize_last()

    def __getattr__(self, name):
        f = getattr(self._eng, name)
        if not callable(f) or name.startswith("_") or name in self._SKIP:
            return f

        def wrap(*a, **k):
            self._finalize_last()
            self.flush_pending()
            r = f(*a, **k)
            if hasattr(r, "then_inc"):
                if self._n > 0:
                    r._wait_ge(self._sem, self._n)
                self._last = r
            return r
        return wrap


def _ap(t):
    a = getattr(t, "ap", None)
    return t.ap() if callable(a) else t


def emit_weight_quant(p, sb, g, stage, wabs):
    nc = p.nc
    w1t, w2t = g["w1t"], g["w2t"]
    w1q = sb("w1q", [128, NCH * HID], FP8)
    w2q = sb("w2q", [128, NHC * D], FP8)
    acc = sb("wacc", [128, NCH + NHC], F32)
    accr = sb("waccr", [128, 2], F32)
    rowt = sb("wrowt", [1, 2], F32)
    mwc = sb("mwc", [1, 2], F32)
    ccol = sb("ccol", [128, 2], F32)
    onesf = sb("onesf", [1, 128], F32)
    onesb = sb("onesb", [128, 1], BF16)
    onesc = sb("onesc", [128, 1], F32)
    with nc.psum_tensor("cps", [128, 2], F32) as cps:
        pass

    plans = [(w1t, NCH, HID, 0, 127.0), (w2t, NHC, D, 1, 254.0)]
    p.add("vector", lambda v: v.memset(onesf[:], 1.0))
    p.add("vector", lambda v: v.memset(onesb[:], 1.0))
    p.add("vector", lambda v: v.memset(onesc[:], 1.0), inc="wq_init")

    for (wt, n, flen, wi, div) in plans:
        col0 = 0 if wi == 0 else NCH
        for k in range(n):
            p.dma("sync",
                  (lambda wt=wt, k=k, flen=flen:
                   lambda s: s.dma_start(out=stage[:, 0:flen], in_=wt[k],
                                         max_dma_last_dim=2048))(),
                  waits={"wq_abs": max(0, p.c("wq_abs"))},
                  stream="wq_ld")
            p.add("vector",
                  (lambda k=k, flen=flen, col0=col0:
                   lambda v: v.tensor_reduce(
                       acc[:, col0 + k:col0 + k + 1], stage[:, 0:flen],
                       AX.X, ALU.add, apply_absolute_value=True))(),
                  waits={"wq_ld": p.c("wq_ld")},
                  inc="wq_abs")
        total = 128.0 * n * flen
        p.add("vector",
              (lambda n=n, wi=wi, col0=col0: lambda v: v.tensor_reduce(
                  accr[:, wi:wi + 1], acc[:, col0:col0 + n], AX.X,
                  ALU.add))(),
              waits={"wq_abs": p.c("wq_abs")}, inc="wq_red")
        p.add("tensor",
              (lambda wi=wi: lambda t: t.matmul(
                  cps[0:1, wi:wi + 1], onesc[:], accr[:, wi:wi + 1],
                  start=True, stop=True))(),
              waits={"wq_red": p.c("wq_red"), "wq_init": 1,
                     "wq_cc": p.c("wq_cc")}, inc="wq_gp")
        p.add("vector",
              (lambda wi=wi, total=total: lambda v: v.tensor_scalar(
                  rowt[0:1, wi:wi + 1], cps[0:1, wi:wi + 1],
                  1.0 / total, EPS5, ALU.mult, ALU.max))(),
              waits={"wq_gp": p.c("wq_gp")}, inc="wq_rmA")
        p.add("vector",
              (lambda wi=wi, div=div: lambda v: v.tensor_scalar(
                  mwc[0:1, wi:wi + 1], rowt[0:1, wi:wi + 1],
                  1.0 / div, None, ALU.mult))(),
              waits={"wq_rmA": p.c("wq_rmA")}, inc="wq_rmB")
        p.add("vector",
              (lambda wi=wi: lambda v: v.reciprocal(
                  rowt[0:1, wi:wi + 1], rowt[0:1, wi:wi + 1]))(),
              waits={"wq_rmB": p.c("wq_rmB")}, inc="wq_rm")
        p.add("tensor",
              (lambda wi=wi: lambda t: t.matmul(
                  cps[:, wi:wi + 1], onesf[:], rowt[0:1, wi:wi + 1],
                  start=True, stop=True))(),
              waits={"wq_rm": p.c("wq_rm"), "wq_init": 1,
                     "wq_bc": p.c("wq_bc")}, inc="wq_bc")
        p.add("scalar",
              (lambda wi=wi: lambda sc: sc.copy(ccol[:, wi:wi + 1],
                                                cps[:, wi:wi + 1]))(),
              waits={"wq_bc": p.c("wq_bc")}, inc="wq_cc")

    for (wt, n, flen, wi, div) in plans:
        wq = w1q if wi == 0 else w2q
        for k in range(n):
            ldg = {"wq_q": p.c("wq_q")}
            if wi == 0 and k == 0:
                ldg["wq_abs"] = NCH + NHC
            p.dma("sync",
                  (lambda wt=wt, k=k, flen=flen:
                   lambda s: s.dma_start(out=stage[:, 0:flen], in_=wt[k],
                                         max_dma_last_dim=2048))(),
                  waits=ldg, stream="wq_ld")
            def _quant(v, k=k, flen=flen, wi=wi, wq=wq):
                v.tensor_scalar(stage[:, 0:flen], stage[:, 0:flen],
                                ccol[:, wi:wi + 1], RNDC, ALU.mult, ALU.add)
                v.tensor_scalar(stage[:, 0:flen], stage[:, 0:flen],
                                RNDC, 1.0, ALU.subtract, ALU.min)
                return v.tensor_scalar(wq[:, k * flen:(k + 1) * flen],
                                       stage[:, 0:flen], -1.0, None, ALU.max)
            p.add("vector", _quant,
                  waits={"wq_ld": p.c("wq_ld"), "wq_cc": 1 + wi},
                  inc="wq_q")
    return dict(w1q=w1q, w2q=w2q, mwc=mwc, onesf=onesf, onesb=onesb,
                onesc=onesc,
                wq_done={"wq_q": p.c("wq_q")})



def emit_ffn_phase(p, nc, pes, W, io, second, n_subtiles=NST,
                   debug_stage=99, gate=None):
    gate = dict(gate or {})
    b1v, b2v = io["b1v"], io["b2v"]
    if not second:
        x_in, x1_out, rows_out = io["x_in"], io["x1_out"], io["rows_out"]
    else:
        x1_in, hloc_in, p_in = io["x1_in"], io["hloc_in"], io["p_in"]
        bov, out_fm = io["bov"], io["out_fm"]
        hcar, wotb = io["hcar"], io["wotb"]
    tag = "fb_" if second else "fa_"
    if True:
        sb = lambda nm, *a: pes.enter_context(nc.sbuf_tensor(tag + nm, *a))
        hbuf = io["bigbuf"].ap()[:, 0:NHC * TAU]
        w1q, w2q = W["w1q"], W["w2q"]
        onesf, onesb, mwc = W["onesf"], W["onesb"], W["mwc"]
        onesc = W["onesc"]

        if not second:
            xt = [sb(f"xt{i}", [128, NCH * TAU], F32) for i in range(2)]
        else:
            xt = [sb("xt0", [128, NCH * TAU], F32)] * 2
        xsq = [sb("xsq0", [128, NCH * TAU], F32)] * 2
        sacc = sb("sacc", [128, TAU], F32)
        sacc2 = sb("sacc2", [128, TAU], F32)
        qtmp2 = sacc
        if not second:
            k1 = [sb(f"k1_{i}", [128, NCH * TAU], BF16) for i in range(2)]
        else:
            k1 = [sb("k1_0", [128, NCH * TAU], BF16)] * 2
        qtmp = None
        hsq = [sb(f"hsq{i}", [128, TAU], F32) for i in range(2)]
        htmp = [sb(f"htmp{i}", [128, TAU], F32) for i in range(2)]
        k2 = sb("k2", [128, NHC * TAU], BF16)
        x1t = sb("x1t", [128, NCH * TAU], F32)
        if not second:
            x1sq = sb("x1sq", [128, NCH * TAU], F32)
        cmx = sb("cmx", [128, TAU], F32)
        cmn = sb("cmn", [128, TAU], F32)
        hmax = cmx
        colb = sb("colb", [128, 6], F32)
        e1sb = sb("e1sb", [128, TAU], F32)
        e2sb = sb("e2sb", [128, TAU], F32)
        b1sb = sb("b1sb", [128, NHC], F32)
        b2sb = sb("b2sb", [128, NCH], F32)
        if not second:
            m2row = sb("m2row", [1, TAU], F32)
            r2row = sb("r2row", [1, TAU], F32)
        epsA = sb("epsA", [1, 1], F32)
        epsB = sb("epsB", [1, 1], F32)
        rw = {}
        for grp in (("m",), ("e2", "rr", "t4"), ("R",), ("t3",),
                    ("M1", "a2"), ("a", "aa2"), ("aa",), ("s", "s2"),
                    ("m2",), ("v",), ("r",),
                    ("vr", "h2", "lnt"), ("q", "q2"), ("S1", "S2"),
                    ("mS1", "e2t"), ("e1",)):
            t = sb(f"rw_{grp[0]}", [1, TAU], F32).ap()
            for n in grp:
                rw[n] = t
        if second:
            x1in_t = [x1t] * 2
            hl_t = [sb("hl0", [128, NCH * TAU], F32)] * 2
            p_t = [sb("pt0", [128, NCH * TAU], BF16)] * 2
            hfix = [sb("hfix0", [128, NCH * TAU], BF16)] * 2
            bosb = sb("bosb", [128, NCH], F32)

        pt = lambda nm, *a: pes.enter_context(nc.psum_tensor(tag + nm, *a))
        hps = [pt(f"hps{i}", [128, TAU], F32) for i in range(2)]
        ops = [pt(f"ops{i}", [128, TAU], F32) for i in range(2)]
        statb = pt("statb", [1, 2 * TAU], F32)
        stat2b = pt("stat2b", [1, 2 * TAU], F32)
        stat3b = statb
        repA = pt("repA", [128, 2 * TAU], F32)
        repB = pt("repB", [128, 2 * TAU], F32)
        rw["Mx"] = repA[0:1, TAU:2 * TAU]
        rw["Mn"] = repB[0:1, TAU:2 * TAU]
        rw["A2"] = repA[0:1, TAU:2 * TAU]

        p.dma("sync", lambda s: s.dma_start(out=b1sb[:], in_=b1v[:]),
              waits=dict(gate), stream="ld0")
        p.dma("sync", lambda s: s.dma_start(out=b2sb[:], in_=b2v[:]),
              waits=dict(gate), stream="ld0")
        if second:
            p.dma("sync", lambda s: s.dma_start(out=bosb[:], in_=bov[:]),
                  waits=dict(gate), stream="ld0")
        p.add("vector", lambda v: v.memset(epsA[:], EPS5))
        p.add("vector", lambda v: v.memset(epsB[:], EPS6))
        p.add("scalar", lambda sc: sc.mul(b2sb[:], b2sb[:], 0.5),
              waits={"ld0": p.c("ld0")}, inc="bias_ok")
        BOK = p.c("bias_ok")

        x_src = x_in if not second else x1_in

        def fm_dram(t, t0, n):
            return t.rearrange("c p t -> p c t")[:, :, t0:t0 + n]

        def fm_sbuf(t):
            return _ap(t).rearrange("p (c t) -> p c t", c=NCH)

        def front(i):
            t0 = i * TAU
            sl = i % 2
            CS = lambda c: slice(c * TAU, (c + 1) * TAU)

            ldw = {}
            if i == 0:
                ldw.update(gate)
            if i >= 2:
                ldw["X1"] = p.c("X1") - (1 if second else 0)
                ldw["STT"] = p.c("STT") - 1
            if not second:
                p.dma("sync",
                      (lambda sl=sl, t0=t0: lambda s: s.dma_start(
                          out=fm_sbuf(xt[sl]), in_=fm_dram(x_src, t0, TAU)))(),
                      waits=ldw, stream="Lx")
            else:
                if i >= 1:
                    ldw["FXD"] = p.c("FXD")
                    ldw["OPC"] = p.c("OPC") - 8
                    ldw["STx1"] = p.c("STx1")
                p.dma("sync",
                      (lambda sl=sl, t0=t0: lambda s: s.dma_start(
                          out=fm_sbuf(x1in_t[sl]),
                          in_=fm_dram(x1_in, t0, TAU)))(),
                      waits=ldw, stream="Lx")
                p.dma("sync",
                      (lambda sl=sl, t0=t0: lambda s: s.dma_start(
                          out=fm_sbuf(hl_t[sl]),
                          in_=fm_dram(hloc_in, t0, TAU)))(),
                      waits=dict(ldw), stream="Lh")
                p.dma("sync",
                      (lambda sl=sl, t0=t0: lambda s: s.dma_start(
                          out=fm_sbuf(p_t[sl]),
                          in_=fm_dram(p_in, t0, TAU)))(),
                      waits=dict(ldw), stream="Lp")
            l_now = {"Lx": p.c("Lx")}
            if second:
                l_now.update({"Lh": p.c("Lh"), "Lp": p.c("Lp")})

            if second:
                def _fixup(v, sl=sl):
                    last = None
                    for c in range(NCH):
                        last = v.scalar_tensor_tensor(
                            hfix[sl][:, CS(c)], p_t[sl][:, CS(c)],
                            hcar[:, c:c + 1], hl_t[sl][:, CS(c)],
                            ALU.mult, ALU.add)
                    return last
                fw = dict(l_now)
                fw["HCAR"] = p.c("HCAR")
                if i == 0:
                    fw.update(gate)
                if i >= 1:
                    fw["WOMM"] = p.c("WOMM")
                p.add("vector", _fixup, waits=fw, inc="FXD")
                for oc in range(NCH):
                    ww = {"FXD": p.c("FXD"), "wo_cast": NCH}
                    if i > 0 or oc >= 2:
                        ww["OPC"] = p.c("OPC") - 1 if (i > 0 or oc >= 2) else 0
                    def _womm(t, oc=oc, sl=sl):
                        last = None
                        for kc in range(NCH):
                            lt = wotb[:, kc * D + oc * 128:
                                      kc * D + (oc + 1) * 128]
                            last = t.matmul(
                                ops[oc % 2][:],
                                lt, hfix[sl][:, CS(kc)],
                                start=(kc == 0), stop=(kc == NCH - 1))
                        return last
                    p.add("tensor", _womm, waits=ww, inc="WOMM")
                    xw = {"WOMM": p.c("WOMM"), "bias_ok": BOK}
                    if i >= 1:
                        xw["X1"] = p.c("X1")
                    p.add("vector",
                          (lambda oc=oc, sl=sl: lambda v:
                           v.scalar_tensor_tensor(
                               xt[sl][:, CS(oc)],
                               ops[oc % 2][:],
                               bosb[:, oc:oc + 1], x1in_t[sl][:, CS(oc)],
                               ALU.add, ALU.add))(),
                          waits=xw, inc="OPC")
                xready = {"OPC": p.c("OPC")}
            else:
                xready = dict(l_now)

            castw = dict(xready)
            if i == 0:
                castw.update(gate)
            if i >= 1:
                castw["STT"] = p.c("STT")
            p.add("scalar",
                  (lambda sl=sl: lambda sc: sc.activation(
                      xsq[sl][:], xt[sl][:], AF.Square))(),
                  waits=castw, inc="SC1")
            scw = {"SC1": p.c("SC1")}
            scw.update(xready)
            if i >= 1:
                scw["STT"] = p.c("STT")
            if i >= 1 and not second:
                scw["LNS"] = p.c("LNS")
            def _chunksums(v, sl=sl):
                v.tensor_reduce(sacc[:],
                                xt[sl].ap().rearrange("p (c t) -> p t c",
                                                      c=NCH),
                                AX.X, ALU.add)
                return v.tensor_reduce(
                    sacc2[:], xsq[sl].ap().rearrange("p (c t) -> p t c",
                                                     c=NCH),
                    AX.X, ALU.add)
            p.add("vector", _chunksums, waits=scw, inc="SCC")
            stw = {"SCC": p.c("SCC"), "wq_init": 1, "STT": p.c("STT")}
            if i >= 1:
                stw["RA"] = p.c("RA")
                stw["LNV"] = p.c("LNV")
            def _ln1stats(t, sl=sl):
                t.matmul(statb[0:1, 0:TAU], onesc[:], sacc[:],
                         start=True, stop=True)
                return t.matmul(statb[0:1, TAU:2 * TAU], onesc[:], sacc2[:],
                                start=True, stop=True)
            p.add("tensor", _ln1stats, waits=stw, inc="STT")

            mmw = dict(xready)
            if i == 0:
                mmw.update(gate)
            if i >= 1:
                mmw["TPM"] = p.c("TPM")
                mmw["TPA"] = p.c("TPA")
            def _cmaxmin(v, sl=sl):
                xv = xt[sl].ap().rearrange("p (c t) -> p t c", c=NCH)
                v.tensor_reduce(cmx[:], xv, AX.X, ALU.max)
                return v.tensor_reduce(cmn[:], xv, AX.X, ALU.min, negate=True)
            p.add("vector", _cmaxmin, waits=mmw, inc="MMR")
            tbase = p.c("TPM")
            rbase = p.c("CRD")
            k1c, k2c, ec2c = p.c("K1"), p.c("K2"), p.c("EC2")
            tpd = [(cmx, 0, repA, {"K1": k1c, "K2": k2c}),
                   (cmx, 1, repB, {"EC2": ec2c, "K2": k2c}),
                   (cmn, 0, repA, {}),
                   (cmn, 1, repB, {})]
            for tj, (src, h, bank, xw) in enumerate(tpd):
                tw = {"MMR": p.c("MMR")}
                tw.update(xw)
                if i == 0 and tj == 0:
                    tw["Lid"] = 16
                if tj >= 2:
                    tw["CRD"] = rbase + tj - 1
                p.add("tensor",
                      (lambda src=src, h=h, bank=bank: lambda t: t.matmul(
                          bank[:, 0:128], src[:, h * 128:(h + 1) * 128],
                          W["idm"][:], start=True, stop=True,
                          is_transpose=True))(),
                      waits=tw, inc="TPM")
                p.add("vector",
                      (lambda tj=tj, bank=bank: lambda v: v.tensor_reduce(
                          colb[:, tj:tj + 1], bank[:, 0:128], AX.X,
                          ALU.max))(),
                      waits={"TPM": tbase + tj + 1}, inc="CRD")
            def _extmm(t):
                t.matmul(repA[0:1, TAU:TAU + 128], colb[:, 0:1], W["idm"][:],
                         start=True, stop=True, is_transpose=True)
                t.matmul(repA[0:1, TAU + 128:2 * TAU], colb[:, 1:2],
                         W["idm"][:], start=True, stop=True,
                         is_transpose=True)
                t.matmul(repB[0:1, TAU:TAU + 128], colb[:, 2:3], W["idm"][:],
                         start=True, stop=True, is_transpose=True)
                return t.matmul(repB[0:1, TAU + 128:2 * TAU], colb[:, 3:4],
                                W["idm"][:], start=True, stop=True,
                                is_transpose=True)
            p.add("tensor", _extmm,
                  waits={"CRD": rbase + 4, "K1": k1c, "K2": k2c}, inc="EXM")

            raw = {"STT": p.c("STT")}
            if i >= 1:
                raw["BC1"] = p.c("BC1")
                raw["LNR"] = p.c("LNR")
            def _rmA(v):
                v.tensor_scalar(rw["m"][:], statb[0:1, 0:TAU], 1.0 / D, None,
                                ALU.mult)
                v.tensor_scalar(rw["e2"][:], statb[0:1, TAU:2 * TAU], 1.0 / D,
                                None, ALU.mult)
                v.tensor_tensor(rw["v"][:], rw["m"][:], rw["m"][:], ALU.mult)
                return v.tensor_tensor(rw["v"][:], rw["e2"][:], rw["v"][:],
                                       ALU.subtract)
            p.add("vector", _rmA, waits=raw, inc="RA")
            p.add("scalar",
                  lambda sc: sc.activation(rw["r"][:], rw["v"][:], AF.Sqrt,
                                           bias=epsA[:]),
                  waits={"RA": p.c("RA")}, inc="RB")
            def _rmB(v):
                v.reciprocal(rw["r"][:], rw["r"][:])
                v.tensor_tensor(rw["rr"][:], rw["r"][:], rw["r"][:], ALU.mult)
                return v.tensor_tensor(rw["vr"][:], rw["v"][:], rw["rr"][:],
                                       ALU.mult)
            p.add("vector", _rmB, waits={"RB": p.c("RB")}, inc="RC")
            p.add("scalar",
                  lambda sc: sc.activation(rw["q"][:], rw["vr"][:], AF.Sqrt,
                                           bias=epsB[:]),
                  waits={"RC": p.c("RC")}, inc="RD")
            def _rmC(v):
                v.reciprocal(rw["q"][:], rw["q"][:])
                v.tensor_tensor(rw["R"][:], rw["r"][:], rw["q"][:], ALU.mult)
                v.tensor_tensor(rw["t3"][:], rw["Mx"][:], rw["m"][:],
                                ALU.subtract)
                v.tensor_tensor(rw["t4"][:], rw["m"][:], rw["Mn"][:],
                                ALU.add)
                v.tensor_tensor(rw["M1"][:], rw["t3"][:], rw["t4"][:], ALU.max)
                v.tensor_tensor(rw["a"][:], rw["R"][:], rw["M1"][:], ALU.mult)
                v.tensor_scalar(rw["aa"][:], rw["a"][:], EPS5, None, ALU.max)
                v.reciprocal(rw["s"][:], rw["aa"][:])
                v.tensor_scalar(rw["s"][:], rw["s"][:], 127.0, None, ALU.mult)
                v.tensor_tensor(rw["S1"][:], rw["R"][:], rw["s"][:], ALU.mult)
                v.tensor_tensor(rw["mS1"][:], rw["m"][:], rw["S1"][:],
                                ALU.mult)
                return v.tensor_scalar(rw["e1"][:], rw["aa"][:],
                                       mwc[0:1, 0:1], None, ALU.mult)
            p.add("vector", _rmC,
                  waits={"RD": p.c("RD"), "EXM": p.c("EXM")}, inc="RM1")

            bcw = {"RM1": p.c("RM1"), "BC1": p.c("BC1")}
            if i >= 1:
                bcw["K1"] = p.c("K1")
                bcw["EC2"] = p.c("EC2")
                bcw["K2"] = p.c("K2")
            def _bc1(t):
                t.matmul(repA[:, 0:TAU], onesf[:], rw["S1"][:], start=True,
                         stop=True)
                t.matmul(repA[:, TAU:2 * TAU], onesf[:], rw["mS1"][:],
                         start=True, stop=True)
                return t.matmul(repB[:, 0:TAU], onesf[:], rw["e1"][:],
                                start=True, stop=True)
            p.add("tensor", _bc1, waits=bcw, inc="BC1")
            ecw = {"BC1": p.c("BC1")}
            if i >= 1:
                ecw["HEV"] = p.c("HEV")
                ecw["K2"] = p.c("K2")
            p.add("scalar",
                  lambda sc: sc.copy(e1sb[:], repB[:, 0:TAU]),
                  waits=ecw, inc="EC1")

            k1w = {"BC1": p.c("BC1")}
            k1w.update(xready)
            if second and i >= 1:
                k1w["MM1"] = p.c("MM1")
            elif i >= 2:
                k1w["MM1"] = p.c("MM1") - NHC // 2
            def _k1(v, sl=sl):
                last = None
                for c in range(NCH):
                    v.tensor_tensor(sacc[:], xt[sl][:, CS(c)], repA[:, 0:TAU],
                                    ALU.mult)
                    v.tensor_tensor(sacc[:], sacc[:], repA[:, TAU:2 * TAU],
                                    ALU.subtract)
                    v.tensor_scalar(sacc[:], sacc[:], RNDC, RNDC, ALU.add,
                                    ALU.subtract)
                    last = v.tensor_scalar(k1[sl][:, CS(c)], sacc[:], 127.0,
                                           -128.0, ALU.min, ALU.max)
                return last
            p.add("vector", _k1, waits=k1w, inc="K1")

        def mid(i):
            sl = i % 2
            CS = lambda c: slice(c * TAU, (c + 1) * TAU)
            base_hev = p.c("HEV")
            base_hgd = p.c("HGD")
            for mc in range(0, NHC, 2):
                q = mc // 2
                w1w = {"K1": p.c("K1")}
                w1w.update(W["wq_done"] if i == 0 and mc == 0 else {})
                if q >= 1:
                    w1w["HEV"] = base_hev + q - 1
                elif i >= 1:
                    w1w["HEV"] = base_hev
                if q == 0:
                    w1w["MM1"] = p.c("MM1")
                def _mm1(t, mc=mc, sl=sl):
                    last = None
                    for j in (0, 1):
                        for kc in range(NCH):
                            lt = w1q[:, kc * HID + (mc + j) * 128:
                                     kc * HID + (mc + j + 1) * 128]
                            last = t.matmul(
                                hps[j][:], lt,
                                k1[sl][:, CS(kc)],
                                start=(kc == 0), stop=(kc == NCH - 1))
                    return last
                p.add("tensor", _mm1, waits=w1w, inc="MM1")
                hevw = {"MM1": p.c("MM1"), "EC1": p.c("EC1")}
                if q >= 1:
                    hevw["HGD"] = base_hgd + q - 1
                elif i >= 1:
                    hevw["HGD"] = base_hgd
                def _hev(v):
                    v.tensor_tensor(htmp[0][:], hps[0][:], e1sb[:], ALU.mult)
                    return v.tensor_tensor(htmp[1][:], hps[1][:], e1sb[:],
                                           ALU.mult)
                p.add("vector", _hev, waits=hevw, inc="HEV")
                gw = {"HEV": p.c("HEV")}
                if i >= 1:
                    gw["K2"] = p.c("K2")
                    gw["AMX"] = p.c("AMX")
                def _gelu(sc, mc=mc):
                    last = None
                    for j in (0, 1):
                        sc.activation(hbuf[:, (mc + j) * TAU:
                                           (mc + j + 1) * TAU],
                                      htmp[j][:], AF.Gelu,
                                      bias=b1sb[:, mc + j:mc + j + 1])
                        last = sc.activation(
                            hsq[j][:],
                            hbuf[:, (mc + j) * TAU:(mc + j + 1) * TAU],
                            AF.Square)
                    return last
                p.add("scalar", _gelu, waits=gw, inc="HGD")
                sqw = {"HGD": p.c("HGD")}
                if mc == 0 and i >= 1:
                    sqw["RE"] = p.c("RE")
                    sqw["LNV"] = p.c("LNV")
                def _sq(t, mc=mc):
                    t.matmul(stat2b[0:1, 0:TAU], onesc[:], hsq[0][:],
                             start=(mc == 0), stop=False)
                    return t.matmul(stat2b[0:1, 0:TAU], onesc[:], hsq[1][:],
                                    start=False, stop=(mc == NHC - 2))
                p.add("tensor", _sq, waits=sqw, inc="SQS")

        def tail(i):
            t0 = i * TAU
            sl = i % 2
            CS = lambda c: slice(c * TAU, (c + 1) * TAU)
            p.add("vector",
                  lambda v: v.tensor_reduce(
                      hmax[:], _ap(hbuf).rearrange("p (c t) -> p t c", c=NHC),
                      AX.X, ALU.max, apply_absolute_value=True),
                  waits={"HGD": p.c("HGD"), "TPM": p.c("TPM")},
                  inc="AMX")
            abase = p.c("TPA")
            arbase = p.c("CRA")
            p.add("tensor",
                  lambda t: t.matmul(repA[:, 0:128], hmax[:, 0:128],
                                     W["idm"][:], start=True, stop=True,
                                     is_transpose=True),
                  waits={"AMX": p.c("AMX"), "K1": p.c("K1"),
                         "RM1": p.c("RM1")}, inc="TPA")
            p.add("tensor",
                  lambda t: t.matmul(repB[:, 0:128], hmax[:, 128:256],
                                     W["idm"][:], start=True, stop=True,
                                     is_transpose=True),
                  waits={"AMX": p.c("AMX"), "EC1": p.c("EC1"),
                         "RM1": p.c("RM1")}, inc="TPA")
            p.add("vector",
                  lambda v: v.tensor_reduce(colb[:, 4:5], repA[:, 0:128],
                                            AX.X, ALU.max),
                  waits={"TPA": abase + 1, "EXA": p.c("EXA")}, inc="CRA")
            p.add("vector",
                  lambda v: v.tensor_reduce(colb[:, 5:6], repB[:, 0:128],
                                            AX.X, ALU.max),
                  waits={"TPA": abase + 2}, inc="CRA")
            def _exta(t):
                t.matmul(repA[0:1, TAU:TAU + 128], colb[:, 4:5], W["idm"][:],
                         start=True, stop=True, is_transpose=True)
                return t.matmul(repA[0:1, TAU + 128:2 * TAU], colb[:, 5:6],
                                W["idm"][:], start=True, stop=True,
                                is_transpose=True)
            p.add("tensor", _exta,
                  waits={"CRA": arbase + 2, "RM1": p.c("RM1"),
                         "K1": p.c("K1")}, inc="EXA")
            p.add("vector",
                  lambda v: v.tensor_scalar(rw["h2"][:], stat2b[0:1, 0:TAU],
                                            1.0 / HID, None, ALU.mult),
                  waits={"SQS": p.c("SQS")}, inc="RE")
            p.add("scalar",
                  lambda sc: sc.activation(rw["q2"][:], rw["h2"][:], AF.Sqrt,
                                           bias=epsB[:]),
                  waits={"RE": p.c("RE")}, inc="RF")
            def _rm2(v):
                v.reciprocal(rw["q2"][:], rw["q2"][:])
                v.tensor_tensor(rw["a2"][:], rw["q2"][:], rw["A2"][:],
                                ALU.mult)
                v.tensor_scalar(rw["aa2"][:], rw["a2"][:], EPS5, None,
                                ALU.max)
                v.reciprocal(rw["s2"][:], rw["aa2"][:])
                v.tensor_scalar(rw["s2"][:], rw["s2"][:], 127.0, None,
                                ALU.mult)
                v.tensor_tensor(rw["S2"][:], rw["q2"][:], rw["s2"][:],
                                ALU.mult)
                return v.tensor_scalar(rw["e2t"][:], rw["aa2"][:],
                                       mwc[0:1, 1:2], None, ALU.mult)
            p.add("vector", _rm2, waits={"RF": p.c("RF"), "EXA": p.c("EXA")},
                  inc="RM2")
            bc2w = {"RM2": p.c("RM2"), "EC1": p.c("EC1"),
                    "BC2": p.c("BC2"), "RM1": p.c("RM1")}
            if i >= 1:
                bc2w["K2"] = p.c("K2")
            def _bc2(t):
                t.matmul(repB[:, TAU:2 * TAU], onesf[:], rw["S2"][:],
                         start=True, stop=True)
                return t.matmul(repB[:, 0:TAU], onesf[:], rw["e2t"][:],
                                start=True, stop=True)
            p.add("tensor", _bc2, waits=bc2w, inc="BC2")
            ec2w = {"BC2": p.c("BC2")}
            if i >= 1:
                ec2w["OPC"] = p.c("OPC")
            p.add("scalar",
                  lambda sc: sc.copy(e2sb[:], repB[:, 0:TAU]),
                  waits=ec2w, inc="EC2")

            base_mm2 = p.c("MM2")
            k2w = {"BC2": p.c("BC2"), "EC2": p.c("EC2"), "STT": p.c("STT")}
            if i >= 1:
                k2w["MM2"] = base_mm2
            def _k2(v):
                last = None
                for hc in range(NHC):
                    v.tensor_tensor(sacc[:],
                                    hbuf[:, hc * TAU:(hc + 1) * TAU],
                                    repB[:, TAU:2 * TAU], ALU.mult)
                    v.tensor_scalar(sacc[:], sacc[:], RNDC, RNDC,
                                    ALU.add, ALU.subtract)
                    last = v.tensor_scalar(
                        k2[:, hc * TAU:(hc + 1) * TAU],
                        sacc[:], 127.0, -128.0, ALU.min, ALU.max)
                return last
            p.add("vector", _k2, waits=k2w, inc="K2")

            for oc in range(NCH):
                m2w = {"K2": p.c("K2")}
                if i == 0 and oc == 0:
                    m2w.update(W["wq_done"])
                if oc >= 2:
                    m2w["OPC"] = p.c("OPC") - 1
                elif second or i >= 1:
                    m2w["OPC"] = p.c("OPC")
                def _mm2(t, oc=oc):
                    last = None
                    for kc in range(NHC):
                        lt = w2q[:, kc * D + oc * 128:kc * D + (oc + 1) * 128]
                        last = t.matmul(
                            ops[oc % 2][:], lt,
                            k2[:, kc * TAU:(kc + 1) * TAU],
                            start=(kc == 0), stop=(kc == NHC - 1))
                    return last
                p.add("tensor", _mm2, waits=m2w, inc="MM2")
                ew = {"MM2": p.c("MM2"), "EC2": p.c("EC2"), "bias_ok": BOK}
                if i >= 1 and oc == 0:
                    ew["STx1"] = p.c("STx1")
                    ew["SQMs"] = p.c("SQMs")
                    ew["SC2"] = p.c("SC2")
                def _epi(v, oc=oc, sl=sl):
                    v.tensor_tensor(x1t[:, CS(oc)],
                                    ops[oc % 2][:],
                                    e2sb[:], ALU.mult)
                    return v.scalar_tensor_tensor(
                        x1t[:, CS(oc)], x1t[:, CS(oc)], b2sb[:, oc:oc + 1],
                        xt[sl][:, CS(oc)], ALU.add, ALU.add)
                p.add("vector", _epi, waits=ew, inc="OPC")
            p.add("vector", lambda v: None, waits={"OPC": p.c("OPC")},
                  inc="X1")

            if second:
                p.dma("gpsimd",
                      (lambda t0=t0: lambda gp: gp.dma_start(
                          out=fm_dram(out_fm, t0, TAU),
                          in_=fm_sbuf(x1t)))(),
                      waits={"X1": p.c("X1")}, stream="STx1")
                return
            cguard = {"X1": p.c("X1")}
            if i >= 1:
                cguard["LNS"] = p.c("LNS")
            p.add("scalar",
                  lambda sc: sc.activation(x1sq[:], x1t[:], AF.Square),
                  waits=dict(cguard), inc="SQMs")
            sc2w = {"SQMs": p.c("SQMs"), "X1": p.c("X1"),
                    "STT": p.c("STT")}
            def _chunksums2(v):
                v.tensor_reduce(sacc[:],
                                x1t.ap().rearrange("p (c t) -> p t c",
                                                   c=NCH),
                                AX.X, ALU.add)
                return v.tensor_reduce(
                    sacc2[:], x1sq.ap().rearrange("p (c t) -> p t c",
                                                  c=NCH),
                    AX.X, ALU.add)
            p.add("vector", _chunksums2, waits=sc2w, inc="SC2")
            lnw = {"SC2": p.c("SC2"), "STT": p.c("STT"),
                   "LNS": p.c("LNS"), "RA": p.c("RA")}
            if i >= 1:
                lnw["LNR"] = p.c("LNR")
            def _ln2stats(t):
                t.matmul(stat3b[0:1, 0:TAU], onesc[:], sacc[:],
                         start=True, stop=True)
                return t.matmul(stat2b[0:1, TAU:2 * TAU], onesc[:], sacc2[:],
                                start=True, stop=True)
            p.add("tensor", _ln2stats, waits=lnw, inc="LNS")
            def _lnrows(v):
                v.tensor_scalar(rw["m2"][:], stat3b[0:1, 0:TAU], 1.0 / D,
                                None, ALU.mult)
                v.tensor_scalar(m2row[:], rw["m2"][:], 1.0, None,
                                ALU.mult)
                v.tensor_scalar(rw["v"][:], stat2b[0:1, TAU:2 * TAU], 1.0 / D,
                                None, ALU.mult)
                v.tensor_tensor(rw["t3"][:], rw["m2"][:], rw["m2"][:],
                                ALU.mult)
                return v.tensor_tensor(rw["v"][:], rw["v"][:], rw["t3"][:],
                                       ALU.subtract)
            rwW = {"LNS": p.c("LNS"), "RA": p.c("RA"), "BC1": p.c("BC1"),
                   "STm2": p.c("STm2"), "STr2": p.c("STr2")}
            p.add("vector", _lnrows, waits=rwW, inc="LNV")
            p.add("scalar",
                  lambda sc: sc.activation(rw["lnt"][:], rw["v"][:], AF.Sqrt,
                                           bias=epsA[:]),
                  waits={"LNV": p.c("LNV")}, inc="LNQ")
            p.add("vector",
                  lambda v: v.reciprocal(r2row[:], rw["lnt"][:]),
                  waits={"LNQ": p.c("LNQ")}, inc="LNR")

            p.dma("gpsimd",
                  (lambda t0=t0: lambda gp: gp.dma_start(
                      out=fm_dram(x1_out, t0, TAU),
                      in_=fm_sbuf(x1t)))(),
                  waits={"X1": p.c("X1")}, stream="STx1")
            p.dma("gpsimd",
                  (lambda t0=t0: lambda gp: gp.dma_start(
                      out=rows_out[0:1, t0:t0 + TAU], in_=m2row[:]))(),
                  waits={"LNR": p.c("LNR")}, stream="STm2")
            p.dma("gpsimd",
                  (lambda t0=t0: lambda gp: gp.dma_start(
                      out=rows_out[1:2, t0:t0 + TAU], in_=r2row[:]))(),
                  waits={"LNR": p.c("LNR")}, stream="STr2")

        if second:
            for i in range(n_subtiles):
                front(i)
                mid(i)
                tail(i)
        else:
            front(0)
            for i in range(n_subtiles):
                mid(i)
                if i + 1 < n_subtiles:
                    front(i + 1)
                tail(i)

    return None



def emit_lru_phase(p, nc, pes, io, W, gate=None):
    gate = dict(gate or {})
    x1_in, rows_in = io["x1_in"], io["rows_in"]
    wit, wgt = io["wit"], io["wgt"]
    biv, bgv = io["biv"], io["bgv"]
    hloc_out, p_out = io["hloc_out"], io["p_out"]
    if True:
        sb = lambda *a: pes.enter_context(nc.sbuf_tensor(*a))
        witb = sb("witb", [128, NCH * D], BF16)
        wgtb = sb("wgtb", [128, NCH * D], BF16)
        stage = sb("lstage", [128, D], F32)
        onesf = W["onesf"]
        bisb = sb("bisb", [128, NCH], F32)
        bgsb = sb("bgsb", [128, NCH], F32)
        onecol = sb("onecol", [128, 1], F32)
        zeros = sb("zeros", [128, TAU2], F32)
        x1bt = [sb("x1bt0", [128, NCH * TAU2], F32)] * 2
        rmt = [sb("rmt0", [1, TAU2], F32)] * 2
        rrt = [sb("rrt0", [1, TAU2], F32)] * 2
        xlf = sb("xlf", [128, TAU2], F32)
        xln = sb("xln", [128, NCH * TAU2], BF16)
        ur = [sb(f"ur{i}", [128, TAU2], F32) for i in range(2)]
        gr = [sb(f"gr{i}", [128, TAU2], F32) for i in range(2)]
        ogr = [sb(f"ogr{i}", [128, TAU2], F32) for i in range(2)]
        br = [sb(f"br{i}", [128, TAU2], F32) for i in range(2)]
        hb = [io["bigbuf"].ap()[:, 0:NCH * TAU2]] * 2
        pb = [io["bigbuf"].ap()[:, NCH * TAU2:2 * NCH * TAU2]] * 2
        pbf = sb("pbf", [128, NCH * TAU2], BF16)
        hinit = [sb(f"hinit{c}", [128, 1], F32) for c in range(NCH)]
        pinit = [sb(f"pinit{c}", [128, 1], F32) for c in range(NCH)]


        pt = lambda *a: pes.enter_context(nc.psum_tensor(*a))
        ups = pt("ups", [128, 2 * TAU2], F32)
        gps = pt("gps", [128, 2 * TAU2], F32)
        repm = pt("repm", [128, TAU2], F32)
        repr_ = pt("repr", [128, TAU2], F32)

        p.add("vector", lambda v: v.memset(onecol[:], 1.0))
        p.add("vector", lambda v: v.memset(zeros[:], 0.0), inc="init")
        p.dma("sync", lambda s: s.dma_start(out=bisb[:], in_=biv[:]),
              waits=dict(gate), stream="ld0")
        p.dma("sync", lambda s: s.dma_start(out=bgsb[:], in_=bgv[:]),
              waits=dict(gate), stream="ld0")
        LD0C = p.c("ld0")
        for wi, (wt, wb) in enumerate(((wit, witb), (wgt, wgtb))):
            for kc in range(NCH):
                wldw = {"wcast": p.c("wcast")}
                if wi == 0 and kc == 0:
                    wldw.update(gate)
                p.dma("sync",
                      (lambda wt=wt, kc=kc: lambda s: s.dma_start(
                          out=stage[:], in_=wt[kc]))(),
                      waits=wldw, stream="wld")
                p.add("scalar",
                      (lambda wb=wb, kc=kc: lambda sc: sc.copy(
                          wb[:, kc * D:(kc + 1) * D], stage[:]))(),
                      waits={"wld": 16 * (p.c("wld") // 16)}, inc="wcast")

        def fm_dram(t, t0, n):
            return t.rearrange("c p t -> p c t")[:, :, t0:t0 + n]

        def fm_sbuf(t, tau):
            return _ap(t).rearrange("p (c t) -> p c t", c=NCH)

        for j in range(NST2):
            t0 = j * TAU2
            sl = j % 2
            CS = lambda c: slice(c * TAU2, (c + 1) * TAU2)

            ldw = {}
            if j == 0:
                ldw.update(gate)
            if j >= 1:
                ldw["XLN"] = p.c("XLN")
                ldw["BCM"] = p.c("BCM")
            p.dma("sync",
                  (lambda sl=sl, t0=t0: lambda s: s.dma_start(
                      out=fm_sbuf(x1bt[sl], TAU2),
                      in_=fm_dram(x1_in, t0, TAU2)))(),
                  waits=ldw, stream="Lx")
            p.dma("sync",
                  (lambda sl=sl, t0=t0: lambda s: s.dma_start(
                      out=rmt[sl][:], in_=rows_in[0:1, t0:t0 + TAU2]))(),
                  waits=dict(ldw), stream="Lm")
            p.dma("sync",
                  (lambda sl=sl, t0=t0: lambda s: s.dma_start(
                      out=rrt[sl][:], in_=rows_in[1:2, t0:t0 + TAU2]))(),
                  waits=dict(ldw), stream="Lr")
            lnow = {"Lx": p.c("Lx"), "Lm": p.c("Lm"), "Lr": p.c("Lr")}

            bw = dict(lnow); bw["init"] = p.c("init")
            if j >= 1:
                bw["XLN"] = p.c("XLN")
            def _bc(t, sl=sl):
                t.matmul(repm[:], onesf[:], rmt[sl][:], start=True,
                         stop=True)
                return t.matmul(repr_[:], onesf[:], rrt[sl][:],
                                start=True, stop=True)
            p.add("tensor", _bc, waits=bw, inc="BCM")

            xw = {"BCM": p.c("BCM")}
            if j >= 1:
                xw["UMM"] = p.c("UMM")
            def _xln(v, sl=sl):
                last = None
                for c in range(NCH):
                    v.tensor_tensor(xlf[:], x1bt[sl][:, CS(c)], repm[:],
                                    ALU.subtract)
                    last = v.tensor_tensor(xln[:, CS(c)], xlf[:], repr_[:],
                                          ALU.mult)
                return last
            p.add("vector", _xln, waits=xw, inc="XLN")

            for oc in range(NCH):
                mw = {"XLN": p.c("XLN"), "wcast": 2 * NCH,
                      "UMM": p.c("UMM")}
                if oc >= 2 or j >= 1:
                    mw["UGD"] = p.c("UGD") - (1 if oc >= 2 else 0)
                def _umm(t, oc=oc):
                    for kc in range(NCH):
                        lt = witb[:, kc * D + oc * 128:kc * D + (oc + 1) * 128]
                        t.matmul(ups[:, (oc % 2) * TAU2:(oc % 2 + 1) * TAU2],
                                 lt, xln[:, CS(kc)], start=(kc == 0),
                                 stop=(kc == NCH - 1))
                    last = None
                    for kc in range(NCH):
                        lt = wgtb[:, kc * D + oc * 128:kc * D + (oc + 1) * 128]
                        last = t.matmul(
                            gps[:, (oc % 2) * TAU2:(oc % 2 + 1) * TAU2],
                            lt, xln[:, CS(kc)], start=(kc == 0),
                            stop=(kc == NCH - 1))
                    return last
                p.add("tensor", _umm, waits=mw, inc="UMM")

                def _acts(sc, oc=oc):
                    sc.activation(ur[oc % 2][:],
                                  ups[:, (oc % 2) * TAU2:(oc % 2 + 1) * TAU2],
                                  AF.Identity, bias=bisb[:, oc:oc + 1])
                    sc.activation(gr[oc % 2][:],
                                  gps[:, (oc % 2) * TAU2:(oc % 2 + 1) * TAU2],
                                  AF.Sigmoid, bias=bgsb[:, oc:oc + 1])
                    return sc.activation(ogr[oc % 2][:], gr[oc % 2][:],
                                         AF.Identity, bias=onecol[:],
                                         scale=-1.0)
                aw = {"UMM": p.c("UMM"), "ld0": LD0C}
                if j >= 1 or oc >= 2:
                    aw["SCN"] = p.c("SCN") - (1 if oc >= 2 else 0)
                p.add("scalar", _acts, waits=aw, inc="UGD")

                def _bin(v, oc=oc):
                    return v.tensor_tensor(br[oc % 2][:], ogr[oc % 2][:],
                                           ur[oc % 2][:], ALU.mult)
                p.add("vector", _bin, waits={"UGD": p.c("UGD")}, inc="BIN")

                def _scan(v, oc=oc, sl=sl, j=j):
                    if j == 0:
                        ih, ip = 0.0, 1.0
                    else:
                        ih, ip = hinit[oc][:], pinit[oc][:]
                    v.tensor_tensor_scan(hb[sl][:, CS(oc)], gr[oc % 2][:],
                                         br[oc % 2][:], ih, ALU.mult, ALU.add)
                    v.tensor_tensor_scan(pb[sl][:, CS(oc)],
                                         gr[oc % 2][:], zeros[:], ip,
                                         ALU.mult, ALU.add)
                    v.tensor_copy(hinit[oc][:],
                                  hb[sl][:, oc * TAU2 + TAU2 - 1:
                                         oc * TAU2 + TAU2])
                    return v.tensor_copy(pinit[oc][:],
                                         pb[sl][:, oc * TAU2 + TAU2 - 1:
                                                oc * TAU2 + TAU2])
                sw = {"BIN": p.c("BIN"), "init": p.c("init")}
                if j >= 1 and oc == 0:
                    sw["SH"] = p.c("SH")
                    sw["SP"] = p.c("SP")
                p.add("vector", _scan, waits=sw, inc="SCN")

            p.add("scalar", lambda sc: sc.copy(carry_sb0[:], carry_sb0[:])
                  if False else None, waits={"SCN": p.c("SCN")}, inc="HCP") \
                if False else None
            p.dma("gpsimd",
                  (lambda t0=t0, sl=sl: lambda gp: gp.dma_start(
                      out=fm_dram(hloc_out, t0, TAU2),
                      in_=fm_sbuf(hb[sl], TAU2)))(),
                  waits={"SCN": p.c("SCN")}, stream="SH")
            cwp = {"SCN": p.c("SCN")}
            if j >= 1:
                cwp["SP"] = p.c("SP")
            p.add("scalar",
                  (lambda sl=sl: lambda sc: sc.activation(
                      pbf[:], pb[sl][:], AF.Identity))(),
                  waits=cwp, inc="HCP")
            p.dma("gpsimd",
                  (lambda t0=t0, sl=sl: lambda gp: gp.dma_start(
                      out=fm_dram(p_out, t0, TAU2),
                      in_=fm_sbuf(pbf, TAU2)))(),
                  waits={"HCP": p.c("HCP")}, stream="SP")
            if j == NST2 - 1:
                carry_sb = sb("carry_sb", [128, NCH], F32)
                def _carrycp(v):
                    last = None
                    for c in range(NCH):
                        last = v.tensor_copy(carry_sb[:, c:c + 1],
                                             hinit[c][:])
                    return last
                p.add("vector", _carrycp,
                      waits={"SCN": p.c("SCN")}, inc="CCP")
                p.dma("gpsimd",
                      lambda gp: gp.dma_start(out=io["carry_out"][:],
                                              in_=carry_sb[:]),
                      waits={"CCP": p.c("CCP")}, stream="SC")
    return



def build_fused():
    nc = bass.Bass("TRN2", target_bir_lowering=False)
    es = ExitStack()
    dram = lambda name, shape, dt, kind="Internal", space="Local": \
        nc.dram_tensor(name, shape, dt, kind=kind, addr_space=space).ap()

    io = {}
    io["w1t"] = dram("w1t", [NCH, 128, HID], F32, "ExternalInput")
    io["w2t"] = dram("w2t", [NHC, 128, D], F32, "ExternalInput")
    io["wit"] = dram("wit", [NCH, 128, D], F32, "ExternalInput")
    io["wgt"] = dram("wgt", [NCH, 128, D], F32, "ExternalInput")
    io["wot"] = dram("wot", [NCH, 128, D], F32, "ExternalInput")
    io["b1v"] = dram("b1v", [128, NHC], F32, "ExternalInput")
    io["b2v"] = dram("b2v", [128, NCH], F32, "ExternalInput")
    io["biv"] = dram("biv", [128, NCH], F32, "ExternalInput")
    io["bgv"] = dram("bgv", [128, NCH], F32, "ExternalInput")
    io["bov"] = dram("bov", [128, NCH], F32, "ExternalInput")
    io["selv"] = dram("selv", [128, 2], F32, "ExternalInput")
    io["idm"] = dram("idm", [128, 128], F32, "ExternalInput")
    io["x_in"] = dram("x_in", [NCH, 128, TPC], F32, "ExternalInput")
    io["out_fm"] = dram("out_fm", [NCH, 128, TPC], F32, "ExternalOutput")
    io["x1_out"] = io["x1_in"] = dram("x1d", [NCH, 128, TPC], F32)
    io["rows_out"] = io["rows_in"] = dram("rowsd", [2, TPC], F32)
    io["hloc_out"] = io["hloc_in"] = dram("hd", [NCH, 128, TPC], F32)
    io["p_out"] = io["p_in"] = dram("pd", [NCH, 128, TPC], BF16)
    io["carry_out"] = dram("carryd", [128, NCH], F32)
    io["carry_gath"] = dram("cgath", [2, 128, NCH], F32, "Internal", "Local")

    with es:
        p = Prog(nc, es)
        sb = lambda *a: es.enter_context(nc.sbuf_tensor(*a))
        bigbuf = sb("bigbuf", [128, NHC * TAU], F32)
        io["bigbuf"] = bigbuf
        stage = bigbuf.ap()[:, 0:HID]
        wabs = bigbuf.ap()[:, HID:2 * HID]
        W = emit_weight_quant(p, sb, io, stage, wabs)
        idsb = sb("idsb", [128, 128], F32)
        W["idm"] = idsb
        p.dma("sync", lambda s: s.dma_start(out=idsb[:], in_=io["idm"][:]),
              stream="Lid")
        wotb = sb("wotb", [128, NCH * D], BF16)
        io["wotb"] = wotb
        wostage = stage[:, 0:D]
        for kc in range(NCH):
            wol = {"wo_cast": p.c("wo_cast")}
            if kc == 0:
                wol["wq_q"] = NCH + NHC
            p.dma("sync",
                  (lambda kc=kc: lambda s: s.dma_start(
                      out=wostage, in_=io["wot"][kc]))(),
                  waits=wol, stream="wo_ld")
            p.add("scalar",
                  (lambda kc=kc: lambda sc: sc.copy(
                      wotb[:, kc * D:(kc + 1) * D], wostage))(),
                  waits={"wo_ld": p.c("wo_ld")}, inc="wo_cast")
        selsb = sb("selsb", [128, 2], F32)
        gathsb = sb("gathsb", [128, 2 * NCH], F32)
        hcar = sb("hcar", [128, NCH], F32)
        io["hcar"] = hcar
        p.dma("sync", lambda s: s.dma_start(out=selsb[:], in_=io["selv"][:]),
              stream="ld0")

        with ExitStack() as pes:
            emit_ffn_phase(p, nc, pes, W, io, second=False,
                           gate={"wo_cast": NCH})
        g1 = {"STx1": p.c("STx1"), "STm2": p.c("STm2"),
              "STr2": p.c("STr2"), "LNR": p.c("LNR")}

        with ExitStack() as pes:
            emit_lru_phase(p, nc, pes, io, W, gate=g1)

        p.add("gpsimd",
              lambda gp: gp.collective_compute(
                  "AllGather", ALU.bypass,
                  replica_groups=[[0, 1], [2, 3], [4, 5], [6, 7]],
                  ins=[io["carry_out"][:]], outs=[io["carry_gath"][:]]),
              waits={"SC": p.c("SC")}, inc="CCG")
        p.dma("gpsimd",
              lambda gp: gp.dma_start(out=gathsb[:, 0:NCH],
                                      in_=io["carry_gath"][0]),
              waits={"CCG": p.c("CCG")}, stream="LG")
        p.dma("gpsimd",
              lambda gp: gp.dma_start(out=gathsb[:, NCH:2 * NCH],
                                      in_=io["carry_gath"][1]),
              waits={"CCG": p.c("CCG")}, stream="LG")
        def _sel(v):
            v.tensor_scalar(hcar[:], gathsb[:, 0:NCH], selsb[:, 0:1], None,
                            ALU.mult)
            return v.scalar_tensor_tensor(hcar[:], gathsb[:, NCH:2 * NCH],
                                          selsb[:, 1:2], hcar[:],
                                          ALU.mult, ALU.add)
        p.add("vector", _sel,
              waits={"LG": p.c("LG"), "ld0": p.c("ld0")}, inc="HCAR")
        g2 = {"SH": p.c("SH"), "SP": p.c("SP"), "HCAR": p.c("HCAR"),
              "HCP": p.c("HCP"), "CCP": p.c("CCP")}

        with ExitStack() as pes:
            emit_ffn_phase(p, nc, pes, W, io, second=True, gate=g2)

        fin = {"STx1": p.c("STx1")}
        p.add("gpsimd", lambda gp: None, waits=fin)
        p.realize()
    return nc



def _prep_host(inputs):
    x = np.ascontiguousarray(np.asarray(inputs["L0_attn_x"], dtype=np.float32))
    w1 = np.asarray(inputs["L0_ffn_w1"], np.float32)
    w2 = np.asarray(inputs["L0_ffn_w2"], np.float32)
    wi = np.asarray(inputs["L0_attn_Wi"], np.float32)
    wg = np.asarray(inputs["L0_attn_Wg"], np.float32)
    wo = np.asarray(inputs["L0_attn_Wo"], np.float32)
    C = lambda a: np.ascontiguousarray(a)
    g = {
        "w1t": C(w1.T.reshape(NCH, 128, HID)),
        "w2t": C(w2.T.reshape(NHC, 128, D)),
        "wit": C(wi.T.reshape(NCH, 128, D)),
        "wgt": C(wg.T.reshape(NCH, 128, D)),
        "wot": C(wo.T.reshape(NCH, 128, D)),
        "b1v": C(np.asarray(inputs["L0_ffn_b1"], np.float32)
                 .reshape(NHC, 128).T),
        "b2v": C(np.asarray(inputs["L0_ffn_b2"], np.float32)
                 .reshape(NCH, 128).T),
        "biv": C(np.asarray(inputs["L0_attn_bi"], np.float32)
                 .reshape(NCH, 128).T),
        "bgv": C(np.asarray(inputs["L0_attn_bg"], np.float32)
                 .reshape(NCH, 128).T),
        "bov": C(np.asarray(inputs["L0_attn_bo"], np.float32)
                 .reshape(NCH, 128).T),
    }
    xs = []
    for c in range(NCORE):
        s, hc = c // 2, c % 2
        sl = x[s, hc * TPC:(hc + 1) * TPC, :]
        xs.append(C(sl.T.reshape(NCH, 128, TPC)))
    return g, xs


def kernel(**inputs):
    g, xs = _prep_host(inputs)
    cores = list(range(NCORE))
    nc = build_fused()
    sel0 = np.zeros((128, 2), np.float32)
    sel1 = np.zeros((128, 2), np.float32)
    sel1[:, 0] = 1.0
    maps = []
    for c in cores:
        maps.append({
            "w1t": g["w1t"], "w2t": g["w2t"], "wit": g["wit"],
            "wgt": g["wgt"], "wot": g["wot"], "b1v": g["b1v"],
            "b2v": g["b2v"], "biv": g["biv"], "bgv": g["bgv"],
            "bov": g["bov"], "x_in": xs[c],
            "selv": sel1 if c % 2 else sel0,
            "idm": np.eye(128, dtype=np.float32),
        })
    res = run_bass_kernel_spmd(nc, maps, core_ids=cores).results
    out = np.empty((BATCH, SEQ, D), np.float32)
    for c in cores:
        s, hc = c // 2, c % 2
        o = np.asarray(res[c]["out_fm"]).reshape(D, TPC)
        out[s, hc * TPC:(hc + 1) * TPC, :] = o.T
    return out


# version 11
# speedup vs baseline: 1.2090x; 1.0030x over previous
import numpy as np
import ml_dtypes
from contextlib import ExitStack

import concourse.bass as bass
import concourse.mybir as mybir
from concourse.bass_utils import run_bass_kernel_spmd

F32 = mybir.dt.float32
BF16 = mybir.dt.bfloat16
FP8 = mybir.dt.float8e4
AF = mybir.ActivationFunctionType
ALU = mybir.AluOpType
AX = mybir.AxisListType

D = 1024
HID = 4096
NCH = D // 128
NHC = HID // 128
SEQ = 8192
BATCH = 4
NCORE = 8
TPC = SEQ // 2
TAU = 256
NST = TPC // TAU
TAU2 = 512
NST2 = TPC // TAU2
RNDC = 1.5 * 2.0**23
EPS5 = 1e-5
EPS6 = 1e-6

ENGINES = ("sync", "scalar", "vector", "tensor", "gpsimd")

import os as _os
CHAIN_EDGES = _os.environ.get("BASS_SIM_FP64") == "1" or \
    _os.environ.get("KERNEL_FORCE_CHAIN_EDGES") == "1"
DEBUG_DUMPS = False



class Prog:

    def __init__(self, nc, es):
        self.nc = nc
        self.es = es
        self.ops = {e: [] for e in ENGINES}
        self.sems = {}
        self.cnt = {}

    def sem(self, name):
        if name not in self.sems:
            self.sems[name] = self.es.enter_context(self.nc.semaphore(name))
            self.cnt[name] = 0
        return name

    def add(self, eng, emit, waits=None, inc=None, amt=1):
        w = dict(waits or {})
        incs = []
        if inc is not None:
            self.sem(inc)
            self.cnt[inc] += amt
            incs = [(inc, amt)]
        for s in w:
            self.sem(s)
        self.ops[eng].append((w, emit, incs))

    def c(self, name):
        self.sem(name)
        return self.cnt[name]

    def dma(self, eng, emit, waits=None, stream="L"):
        w = dict(waits or {})
        self.sem(stream)
        w[stream] = self.cnt[stream]
        self.add(eng, emit, w, inc=stream, amt=16)

    def realize(self):
        with self.nc.Block() as blk:
            for eng in ENGINES:
                seq = self.ops[eng]
                if not seq:
                    continue
                chain_sem = None
                if CHAIN_EDGES and eng in ("vector", "scalar", "gpsimd"):
                    chain_sem = self.es.enter_context(
                        self.nc.semaphore(f"chain_{eng}"))

                def make(seq, chain_sem):
                    def body(e):
                        ee = _ChainEngine(e, chain_sem) if chain_sem else e
                        for (waits, emit, incs) in seq:
                            if chain_sem:
                                ee.flush_pending()
                            for s, v in waits.items():
                                if v > 0:
                                    e.wait_ge(self.sems[s], v)
                            ins = emit(ee)
                            for s, a in incs:
                                if ins is None:
                                    ins = e.engine_nop()
                                    if chain_sem:
                                        ee.chain_bare(ins)
                                if chain_sem:
                                    ee.take_inc(ins, self.sems[s], a)
                                else:
                                    ins.then_inc(self.sems[s], a)
                        if chain_sem:
                            ee.finalize()
                    return body
                getattr(blk, eng)(make(list(seq), chain_sem))


class _ChainEngine:

    _SKIP = {"dma_start", "wait_ge", "then_inc", "engine_nop", "bass",
             "value_load", "partition_id", "flush_pending", "take_inc",
             "finalize", "chain_bare"}

    def __init__(self, eng, sem):
        self._eng = eng
        self._sem = sem
        self._n = 0
        self._last = None
        self._pending = []
        self._reccnt = {}

    def _finalize_last(self):
        if self._last is not None:
            self._last.then_inc(self._sem, 1)
            self._n += 1
            self._last = None

    def flush_pending(self):
        for (s, v) in self._pending:
            self._eng.wait_ge(s, v)
        self._pending = []

    def take_inc(self, ins, sem_handle, amt):
        key = id(sem_handle)
        self._reccnt[key] = self._reccnt.get(key, 0) + amt
        if ins is self._last:
            self._last = None
            ins.then_inc(sem_handle, amt)
            self._pending.append((sem_handle, self._reccnt[key]))
        else:
            ins.then_inc(sem_handle, amt)

    def chain_bare(self, ins):
        self._finalize_last()
        if self._n > 0:
            ins._wait_ge(self._sem, self._n)

    def finalize(self):
        self._finalize_last()

    def __getattr__(self, name):
        f = getattr(self._eng, name)
        if not callable(f) or name.startswith("_") or name in self._SKIP:
            return f

        def wrap(*a, **k):
            self._finalize_last()
            self.flush_pending()
            r = f(*a, **k)
            if hasattr(r, "then_inc"):
                if self._n > 0:
                    r._wait_ge(self._sem, self._n)
                self._last = r
            return r
        return wrap


def _ap(t):
    a = getattr(t, "ap", None)
    return t.ap() if callable(a) else t


def emit_weight_quant(p, sb, g, stage, wabs):
    nc = p.nc
    w1t, w2t = g["w1t"], g["w2t"]
    w1q = sb("w1q", [128, NCH * HID], FP8)
    w2q = sb("w2q", [128, NHC * D], FP8)
    acc = sb("wacc", [128, NCH + NHC], F32)
    accr = sb("waccr", [128, 2], F32)
    rowt = sb("wrowt", [1, 2], F32)
    mwc = sb("mwc", [1, 2], F32)
    ccol = sb("ccol", [128, 2], F32)
    onesf = sb("onesf", [1, 128], F32)
    onesb = sb("onesb", [128, 1], BF16)
    onesc = sb("onesc", [128, 1], F32)
    with nc.psum_tensor("cps", [128, 2], F32) as cps:
        pass

    plans = [(w1t, NCH, HID, 0, 127.0), (w2t, NHC, D, 1, 254.0)]
    p.add("vector", lambda v: v.memset(onesf[:], 1.0))
    p.add("vector", lambda v: v.memset(onesb[:], 1.0))
    p.add("vector", lambda v: v.memset(onesc[:], 1.0), inc="wq_init")

    for (wt, n, flen, wi, div) in plans:
        col0 = 0 if wi == 0 else NCH
        for k in range(n):
            p.dma("sync",
                  (lambda wt=wt, k=k, flen=flen:
                   lambda s: s.dma_start(out=stage[:, 0:flen], in_=wt[k],
                                         max_dma_last_dim=2048))(),
                  waits={"wq_abs": max(0, p.c("wq_abs"))},
                  stream="wq_ld")
            p.add("vector",
                  (lambda k=k, flen=flen, col0=col0:
                   lambda v: v.tensor_reduce(
                       acc[:, col0 + k:col0 + k + 1], stage[:, 0:flen],
                       AX.X, ALU.add, apply_absolute_value=True))(),
                  waits={"wq_ld": p.c("wq_ld")},
                  inc="wq_abs")
        total = 128.0 * n * flen
        p.add("vector",
              (lambda n=n, wi=wi, col0=col0: lambda v: v.tensor_reduce(
                  accr[:, wi:wi + 1], acc[:, col0:col0 + n], AX.X,
                  ALU.add))(),
              waits={"wq_abs": p.c("wq_abs")}, inc="wq_red")
        p.add("tensor",
              (lambda wi=wi: lambda t: t.matmul(
                  cps[0:1, wi:wi + 1], onesc[:], accr[:, wi:wi + 1],
                  start=True, stop=True))(),
              waits={"wq_red": p.c("wq_red"), "wq_init": 1,
                     "wq_cc": p.c("wq_cc")}, inc="wq_gp")
        p.add("vector",
              (lambda wi=wi, total=total: lambda v: v.tensor_scalar(
                  rowt[0:1, wi:wi + 1], cps[0:1, wi:wi + 1],
                  1.0 / total, EPS5, ALU.mult, ALU.max))(),
              waits={"wq_gp": p.c("wq_gp")}, inc="wq_rmA")
        p.add("vector",
              (lambda wi=wi, div=div: lambda v: v.tensor_scalar(
                  mwc[0:1, wi:wi + 1], rowt[0:1, wi:wi + 1],
                  1.0 / div, None, ALU.mult))(),
              waits={"wq_rmA": p.c("wq_rmA")}, inc="wq_rmB")
        p.add("vector",
              (lambda wi=wi: lambda v: v.reciprocal(
                  rowt[0:1, wi:wi + 1], rowt[0:1, wi:wi + 1]))(),
              waits={"wq_rmB": p.c("wq_rmB")}, inc="wq_rm")
        p.add("tensor",
              (lambda wi=wi: lambda t: t.matmul(
                  cps[:, wi:wi + 1], onesf[:], rowt[0:1, wi:wi + 1],
                  start=True, stop=True))(),
              waits={"wq_rm": p.c("wq_rm"), "wq_init": 1,
                     "wq_bc": p.c("wq_bc")}, inc="wq_bc")
        p.add("scalar",
              (lambda wi=wi: lambda sc: sc.copy(ccol[:, wi:wi + 1],
                                                cps[:, wi:wi + 1]))(),
              waits={"wq_bc": p.c("wq_bc")}, inc="wq_cc")

    for (wt, n, flen, wi, div) in plans:
        wq = w1q if wi == 0 else w2q
        for k in range(n):
            ldg = {"wq_q": p.c("wq_q")}
            if wi == 0 and k == 0:
                ldg["wq_abs"] = NCH + NHC
            p.dma("sync",
                  (lambda wt=wt, k=k, flen=flen:
                   lambda s: s.dma_start(out=stage[:, 0:flen], in_=wt[k],
                                         max_dma_last_dim=2048))(),
                  waits=ldg, stream="wq_ld")
            def _quant(v, k=k, flen=flen, wi=wi, wq=wq):
                v.tensor_scalar(stage[:, 0:flen], stage[:, 0:flen],
                                ccol[:, wi:wi + 1], RNDC, ALU.mult, ALU.add)
                v.tensor_scalar(stage[:, 0:flen], stage[:, 0:flen],
                                RNDC, 1.0, ALU.subtract, ALU.min)
                return v.tensor_scalar(wq[:, k * flen:(k + 1) * flen],
                                       stage[:, 0:flen], -1.0, None, ALU.max)
            p.add("vector", _quant,
                  waits={"wq_ld": p.c("wq_ld"), "wq_cc": 1 + wi},
                  inc="wq_q")
    return dict(w1q=w1q, w2q=w2q, mwc=mwc, onesf=onesf, onesb=onesb,
                onesc=onesc,
                wq_done={"wq_q": p.c("wq_q")})



def emit_ffn_phase(p, nc, pes, W, io, second, n_subtiles=NST,
                   debug_stage=99, gate=None):
    gate = dict(gate or {})
    b1v, b2v = io["b1v"], io["b2v"]
    if not second:
        x_in, x1_out, rows_out = io["x_in"], io["x1_out"], io["rows_out"]
    else:
        x1_in, hloc_in, p_in = io["x1_in"], io["hloc_in"], io["p_in"]
        bov, out_fm = io["bov"], io["out_fm"]
        hcar, wotb = io["hcar"], io["wotb"]
    tag = "fb_" if second else "fa_"
    if True:
        sb = lambda nm, *a: pes.enter_context(nc.sbuf_tensor(tag + nm, *a))
        hbuf = io["bigbuf"].ap()[:, 0:NHC * TAU]
        w1q, w2q = W["w1q"], W["w2q"]
        onesf, onesb, mwc = W["onesf"], W["onesb"], W["mwc"]
        onesc = W["onesc"]

        if not second:
            xt = [sb(f"xt{i}", [128, NCH * TAU], F32) for i in range(2)]
        else:
            xt = [sb("xt0", [128, NCH * TAU], F32)] * 2
        xsq = [sb("xsq0", [128, NCH * TAU], F32)] * 2
        sacc = sb("sacc", [128, TAU], F32)
        sacc2 = sb("sacc2", [128, TAU], F32)
        qtmp2 = sacc
        if not second:
            k1 = [sb(f"k1_{i}", [128, NCH * TAU], BF16) for i in range(2)]
        else:
            k1 = [sb("k1_0", [128, NCH * TAU], BF16)] * 2
        qtmp = None
        hsq = [sb(f"hsq{i}", [128, TAU], F32) for i in range(2)]
        htmp = [sb(f"htmp{i}", [128, TAU], F32) for i in range(2)]
        k2 = sb("k2", [128, NHC * TAU], BF16)
        x1t = sb("x1t", [128, NCH * TAU], F32)
        if not second:
            x1sq = sb("x1sq", [128, NCH * TAU], F32)
        cmx = sb("cmx", [128, TAU], F32)
        cmn = sb("cmn", [128, TAU], F32)
        hmax = cmx
        colb = sb("colb", [128, 6], F32)
        e1sb = sb("e1sb", [128, TAU], F32)
        e2sb = sb("e2sb", [128, TAU], F32)
        b1sb = sb("b1sb", [128, NHC], F32)
        b2sb = sb("b2sb", [128, NCH], F32)
        if not second:
            m2row = sb("m2row", [1, TAU], F32)
            r2row = sb("r2row", [1, TAU], F32)
        epsA = sb("epsA", [1, 1], F32)
        epsB = sb("epsB", [1, 1], F32)
        rw = {}
        for grp in (("m",), ("e2", "rr", "t4"), ("R",), ("t3",),
                    ("M1", "a2"), ("a", "aa2"), ("aa",), ("s", "s2"),
                    ("m2",), ("v",), ("r",),
                    ("vr", "h2", "lnt"), ("q", "q2"), ("S1", "S2"),
                    ("mS1", "e2t"), ("e1",)):
            t = sb(f"rw_{grp[0]}", [1, TAU], F32).ap()
            for n in grp:
                rw[n] = t
        if second:
            x1in_t = [x1t] * 2
            hl_t = [sb("hl0", [128, NCH * TAU], F32)] * 2
            p_t = [sb("pt0", [128, NCH * TAU], BF16)] * 2
            hfix = [sb("hfix0", [128, NCH * TAU], BF16)] * 2
            bosb = sb("bosb", [128, NCH], F32)

        pt = lambda nm, *a: pes.enter_context(nc.psum_tensor(tag + nm, *a))
        hps = [pt(f"hps{i}", [128, TAU], F32) for i in range(2)]
        ops = [pt(f"ops{i}", [128, TAU], F32) for i in range(2)]
        statb = pt("statb", [1, 2 * TAU], F32)
        stat2b = pt("stat2b", [1, 2 * TAU], F32)
        stat3b = statb
        repA = pt("repA", [128, 2 * TAU], F32)
        repB = pt("repB", [128, 2 * TAU], F32)
        rw["Mx"] = repA[0:1, TAU:2 * TAU]
        rw["Mn"] = repB[0:1, TAU:2 * TAU]
        rw["A2"] = repA[0:1, TAU:2 * TAU]

        p.dma("sync", lambda s: s.dma_start(out=b1sb[:], in_=b1v[:]),
              waits=dict(gate), stream="ld0")
        p.dma("sync", lambda s: s.dma_start(out=b2sb[:], in_=b2v[:]),
              waits=dict(gate), stream="ld0")
        if second:
            p.dma("sync", lambda s: s.dma_start(out=bosb[:], in_=bov[:]),
                  waits=dict(gate), stream="ld0")
        p.add("vector", lambda v: v.memset(epsA[:], EPS5))
        p.add("vector", lambda v: v.memset(epsB[:], EPS6))
        p.add("scalar", lambda sc: sc.mul(b2sb[:], b2sb[:], 0.5),
              waits={"ld0": p.c("ld0")}, inc="bias_ok")
        BOK = p.c("bias_ok")

        x_src = x_in if not second else x1_in

        def fm_dram(t, t0, n):
            return t.rearrange("c p t -> p c t")[:, :, t0:t0 + n]

        def fm_sbuf(t):
            return _ap(t).rearrange("p (c t) -> p c t", c=NCH)

        def front(i):
            t0 = i * TAU
            sl = i % 2
            CS = lambda c: slice(c * TAU, (c + 1) * TAU)

            ldw = {}
            if i == 0:
                ldw.update(gate)
            if i >= 2:
                ldw["X1"] = p.c("X1") - (1 if second else 0)
                ldw["STT"] = p.c("STT") - 1
            if not second:
                p.dma("sync",
                      (lambda sl=sl, t0=t0: lambda s: s.dma_start(
                          out=fm_sbuf(xt[sl]), in_=fm_dram(x_src, t0, TAU)))(),
                      waits=ldw, stream="Lx")
            else:
                if i >= 1:
                    ldw["FXD"] = p.c("FXD")
                    ldw["OPC"] = p.c("OPC") - 8
                    ldw["STx1"] = p.c("STx1")
                p.dma("sync",
                      (lambda sl=sl, t0=t0: lambda s: s.dma_start(
                          out=fm_sbuf(x1in_t[sl]),
                          in_=fm_dram(x1_in, t0, TAU)))(),
                      waits=ldw, stream="Lx")
                p.dma("sync",
                      (lambda sl=sl, t0=t0: lambda s: s.dma_start(
                          out=fm_sbuf(hl_t[sl]),
                          in_=fm_dram(hloc_in, t0, TAU)))(),
                      waits=dict(ldw), stream="Lh")
                p.dma("sync",
                      (lambda sl=sl, t0=t0: lambda s: s.dma_start(
                          out=fm_sbuf(p_t[sl]),
                          in_=fm_dram(p_in, t0, TAU)))(),
                      waits=dict(ldw), stream="Lp")
            l_now = {"Lx": p.c("Lx")}
            if second:
                l_now.update({"Lh": p.c("Lh"), "Lp": p.c("Lp")})

            if second:
                def _fixup(v, sl=sl):
                    last = None
                    for c in range(NCH):
                        last = v.scalar_tensor_tensor(
                            hfix[sl][:, CS(c)], p_t[sl][:, CS(c)],
                            hcar[:, c:c + 1], hl_t[sl][:, CS(c)],
                            ALU.mult, ALU.add)
                    return last
                fw = dict(l_now)
                fw["HCAR"] = p.c("HCAR")
                if i == 0:
                    fw.update(gate)
                if i >= 1:
                    fw["WOMM"] = p.c("WOMM")
                p.add("vector", _fixup, waits=fw, inc="FXD")
                for oc in range(NCH):
                    ww = {"FXD": p.c("FXD"), "wo_cast": NCH}
                    if i > 0 or oc >= 2:
                        ww["OPC"] = p.c("OPC") - 1 if (i > 0 or oc >= 2) else 0
                    def _womm(t, oc=oc, sl=sl):
                        last = None
                        for kc in range(NCH):
                            lt = wotb[:, kc * D + oc * 128:
                                      kc * D + (oc + 1) * 128]
                            last = t.matmul(
                                ops[oc % 2][:],
                                lt, hfix[sl][:, CS(kc)],
                                start=(kc == 0), stop=(kc == NCH - 1))
                        return last
                    p.add("tensor", _womm, waits=ww, inc="WOMM")
                    xw = {"WOMM": p.c("WOMM"), "bias_ok": BOK}
                    if i >= 1:
                        xw["X1"] = p.c("X1")
                    p.add("vector",
                          (lambda oc=oc, sl=sl: lambda v:
                           v.scalar_tensor_tensor(
                               xt[sl][:, CS(oc)],
                               ops[oc % 2][:],
                               bosb[:, oc:oc + 1], x1in_t[sl][:, CS(oc)],
                               ALU.add, ALU.add))(),
                          waits=xw, inc="OPC")
                xready = {"OPC": p.c("OPC")}
            else:
                xready = dict(l_now)

            castw = dict(xready)
            if i == 0:
                castw.update(gate)
            if i >= 1:
                castw["STT"] = p.c("STT")
            p.add("scalar",
                  (lambda sl=sl: lambda sc: sc.activation(
                      xsq[sl][:], xt[sl][:], AF.Square))(),
                  waits=castw, inc="SC1")
            scw = {"SC1": p.c("SC1")}
            scw.update(xready)
            if i >= 1:
                scw["STT"] = p.c("STT")
            if i >= 1 and not second:
                scw["LNS"] = p.c("LNS")
            def _chunksums(v, sl=sl):
                v.tensor_reduce(sacc[:],
                                xt[sl].ap().rearrange("p (c t) -> p t c",
                                                      c=NCH),
                                AX.X, ALU.add)
                return v.tensor_reduce(
                    sacc2[:], xsq[sl].ap().rearrange("p (c t) -> p t c",
                                                     c=NCH),
                    AX.X, ALU.add)
            p.add("vector", _chunksums, waits=scw, inc="SCC")
            stw = {"SCC": p.c("SCC"), "wq_init": 1, "STT": p.c("STT")}
            if i >= 1:
                stw["RA"] = p.c("RA")
                stw["LNV"] = p.c("LNV")
            def _ln1stats(t, sl=sl):
                t.matmul(statb[0:1, 0:TAU], onesc[:], sacc[:],
                         start=True, stop=True)
                return t.matmul(statb[0:1, TAU:2 * TAU], onesc[:], sacc2[:],
                                start=True, stop=True)
            p.add("tensor", _ln1stats, waits=stw, inc="STT")

            mmw = dict(xready)
            if i == 0:
                mmw.update(gate)
            if i >= 1:
                mmw["TPM"] = p.c("TPM")
                mmw["TPA"] = p.c("TPA")
            def _cmaxmin(v, sl=sl):
                xv = xt[sl].ap().rearrange("p (c t) -> p t c", c=NCH)
                v.tensor_reduce(cmx[:], xv, AX.X, ALU.max)
                return v.tensor_reduce(cmn[:], xv, AX.X, ALU.min, negate=True)
            p.add("vector", _cmaxmin, waits=mmw, inc="MMR")
            tbase = p.c("TPM")
            rbase = p.c("CRD")
            k1c, k2c, ec2c = p.c("K1"), p.c("K2"), p.c("EC2")
            tpd = [(cmx, 0, repA, {"K1": k1c, "K2": k2c,
                                   "KA": p.c("KA")}),
                   (cmx, 1, repB, {"EC2": ec2c, "K2": k2c}),
                   (cmn, 0, repA, {}),
                   (cmn, 1, repB, {})]
            for tj, (src, h, bank, xw) in enumerate(tpd):
                tw = {"MMR": p.c("MMR")}
                tw.update(xw)
                if i == 0 and tj == 0:
                    tw["Lid"] = 16
                if tj >= 2:
                    tw["CRD"] = rbase + tj - 1
                p.add("tensor",
                      (lambda src=src, h=h, bank=bank: lambda t: t.matmul(
                          bank[:, 0:128], src[:, h * 128:(h + 1) * 128],
                          W["idm"][:], start=True, stop=True,
                          is_transpose=True))(),
                      waits=tw, inc="TPM")
                p.add("vector",
                      (lambda tj=tj, bank=bank: lambda v: v.tensor_reduce(
                          colb[:, tj:tj + 1], bank[:, 0:128], AX.X,
                          ALU.max))(),
                      waits={"TPM": tbase + tj + 1}, inc="CRD")
            def _extmm(t):
                t.matmul(repA[0:1, TAU:TAU + 128], colb[:, 0:1], W["idm"][:],
                         start=True, stop=True, is_transpose=True)
                t.matmul(repA[0:1, TAU + 128:2 * TAU], colb[:, 1:2],
                         W["idm"][:], start=True, stop=True,
                         is_transpose=True)
                t.matmul(repB[0:1, TAU:TAU + 128], colb[:, 2:3], W["idm"][:],
                         start=True, stop=True, is_transpose=True)
                return t.matmul(repB[0:1, TAU + 128:2 * TAU], colb[:, 3:4],
                                W["idm"][:], start=True, stop=True,
                                is_transpose=True)
            p.add("tensor", _extmm,
                  waits={"CRD": rbase + 4, "K1": k1c, "K2": k2c}, inc="EXM")

            raw = {"STT": p.c("STT")}
            if i >= 1:
                raw["BC1"] = p.c("BC1")
                raw["LNR"] = p.c("LNR")
            def _rmA(v):
                v.tensor_scalar(rw["m"][:], statb[0:1, 0:TAU], 1.0 / D, None,
                                ALU.mult)
                v.tensor_scalar(rw["e2"][:], statb[0:1, TAU:2 * TAU], 1.0 / D,
                                None, ALU.mult)
                v.tensor_tensor(rw["v"][:], rw["m"][:], rw["m"][:], ALU.mult)
                return v.tensor_tensor(rw["v"][:], rw["e2"][:], rw["v"][:],
                                       ALU.subtract)
            p.add("vector", _rmA, waits=raw, inc="RA")
            p.add("scalar",
                  lambda sc: sc.activation(rw["r"][:], rw["v"][:], AF.Sqrt,
                                           bias=epsA[:]),
                  waits={"RA": p.c("RA")}, inc="RB")
            def _rmB(v):
                v.reciprocal(rw["r"][:], rw["r"][:])
                v.tensor_tensor(rw["rr"][:], rw["r"][:], rw["r"][:], ALU.mult)
                return v.tensor_tensor(rw["vr"][:], rw["v"][:], rw["rr"][:],
                                       ALU.mult)
            p.add("vector", _rmB, waits={"RB": p.c("RB")}, inc="RC")
            p.add("scalar",
                  lambda sc: sc.activation(rw["q"][:], rw["vr"][:], AF.Sqrt,
                                           bias=epsB[:]),
                  waits={"RC": p.c("RC")}, inc="RD")
            def _rmC(v):
                v.reciprocal(rw["q"][:], rw["q"][:])
                v.tensor_tensor(rw["R"][:], rw["r"][:], rw["q"][:], ALU.mult)
                v.tensor_tensor(rw["t3"][:], rw["Mx"][:], rw["m"][:],
                                ALU.subtract)
                v.tensor_tensor(rw["t4"][:], rw["m"][:], rw["Mn"][:],
                                ALU.add)
                v.tensor_tensor(rw["M1"][:], rw["t3"][:], rw["t4"][:], ALU.max)
                v.tensor_tensor(rw["a"][:], rw["R"][:], rw["M1"][:], ALU.mult)
                v.tensor_scalar(rw["aa"][:], rw["a"][:], EPS5, None, ALU.max)
                v.reciprocal(rw["s"][:], rw["aa"][:])
                v.tensor_scalar(rw["s"][:], rw["s"][:], 127.0, None, ALU.mult)
                v.tensor_tensor(rw["S1"][:], rw["R"][:], rw["s"][:], ALU.mult)
                v.tensor_tensor(rw["mS1"][:], rw["m"][:], rw["S1"][:],
                                ALU.mult)
                return v.tensor_scalar(rw["e1"][:], rw["aa"][:],
                                       mwc[0:1, 0:1], None, ALU.mult)
            p.add("vector", _rmC,
                  waits={"RD": p.c("RD"), "EXM": p.c("EXM")}, inc="RM1")

            bcw = {"RM1": p.c("RM1"), "BC1": p.c("BC1")}
            if i >= 1:
                bcw["K1"] = p.c("K1")
                bcw["EC2"] = p.c("EC2")
                bcw["K2"] = p.c("K2")
            def _bc1(t):
                t.matmul(repA[:, 0:TAU], onesf[:], rw["S1"][:], start=True,
                         stop=True)
                t.matmul(repA[:, TAU:2 * TAU], onesf[:], rw["mS1"][:],
                         start=True, stop=True)
                return t.matmul(repB[:, 0:TAU], onesf[:], rw["e1"][:],
                                start=True, stop=True)
            p.add("tensor", _bc1, waits=bcw, inc="BC1")
            ecw = {"BC1": p.c("BC1")}
            if i >= 1:
                ecw["HEV"] = p.c("HEV")
                ecw["K2"] = p.c("K2")
            p.add("scalar",
                  lambda sc: sc.copy(e1sb[:], repB[:, 0:TAU]),
                  waits=ecw, inc="EC1")

            k1w = {"BC1": p.c("BC1")}
            k1w.update(xready)
            if second and i >= 1:
                k1w["MM1"] = p.c("MM1")
            elif i >= 2:
                k1w["MM1"] = p.c("MM1") - NHC // 2
            def _k1(v, sl=sl):
                last = None
                for c in range(NCH):
                    v.tensor_tensor(sacc[:], xt[sl][:, CS(c)], repA[:, 0:TAU],
                                    ALU.mult)
                    v.tensor_tensor(sacc[:], sacc[:], repA[:, TAU:2 * TAU],
                                    ALU.subtract)
                    v.tensor_scalar(sacc[:], sacc[:], RNDC, RNDC, ALU.add,
                                    ALU.subtract)
                    last = v.tensor_scalar(k1[sl][:, CS(c)], sacc[:], 127.0,
                                           -128.0, ALU.min, ALU.max)
                return last
            p.add("vector", _k1, waits=k1w, inc="K1")

        def mid(i):
            sl = i % 2
            CS = lambda c: slice(c * TAU, (c + 1) * TAU)
            base_hev = p.c("HEV")
            base_hgd = p.c("HGD")
            for mc in range(0, NHC, 2):
                q = mc // 2
                w1w = {"K1": p.c("K1")}
                w1w.update(W["wq_done"] if i == 0 and mc == 0 else {})
                if q >= 1:
                    w1w["HEV"] = base_hev + q - 1
                elif i >= 1:
                    w1w["HEV"] = base_hev
                if q == 0:
                    w1w["MM1"] = p.c("MM1")
                def _mm1(t, mc=mc, sl=sl):
                    last = None
                    for j in (0, 1):
                        for kc in range(NCH):
                            lt = w1q[:, kc * HID + (mc + j) * 128:
                                     kc * HID + (mc + j + 1) * 128]
                            last = t.matmul(
                                hps[j][:], lt,
                                k1[sl][:, CS(kc)],
                                start=(kc == 0), stop=(kc == NCH - 1))
                    return last
                p.add("tensor", _mm1, waits=w1w, inc="MM1")
                hevw = {"MM1": p.c("MM1"), "EC1": p.c("EC1")}
                if q >= 1:
                    hevw["HGD"] = base_hgd + q - 1
                elif i >= 1:
                    hevw["HGD"] = base_hgd
                def _hev(v):
                    v.tensor_tensor(htmp[0][:], hps[0][:], e1sb[:], ALU.mult)
                    return v.tensor_tensor(htmp[1][:], hps[1][:], e1sb[:],
                                           ALU.mult)
                p.add("vector", _hev, waits=hevw, inc="HEV")
                gw = {"HEV": p.c("HEV")}
                if i >= 1:
                    gw["K2"] = p.c("K2")
                    gw["AMX"] = p.c("AMX")
                def _gelu(sc, mc=mc):
                    last = None
                    for j in (0, 1):
                        sc.activation(hbuf[:, (mc + j) * TAU:
                                           (mc + j + 1) * TAU],
                                      htmp[j][:], AF.Gelu,
                                      bias=b1sb[:, mc + j:mc + j + 1])
                        last = sc.activation(
                            hsq[j][:],
                            hbuf[:, (mc + j) * TAU:(mc + j + 1) * TAU],
                            AF.Square)
                    return last
                p.add("scalar", _gelu, waits=gw, inc="HGD")
                sqw = {"HGD": p.c("HGD")}
                if mc == 0 and i >= 1:
                    sqw["RE"] = p.c("RE")
                    sqw["LNV"] = p.c("LNV")
                def _sq(t, mc=mc):
                    t.matmul(stat2b[0:1, 0:TAU], onesc[:], hsq[0][:],
                             start=(mc == 0), stop=False)
                    return t.matmul(stat2b[0:1, 0:TAU], onesc[:], hsq[1][:],
                                    start=False, stop=(mc == NHC - 2))
                p.add("tensor", _sq, waits=sqw, inc="SQS")

        def tail(i):
            t0 = i * TAU
            sl = i % 2
            CS = lambda c: slice(c * TAU, (c + 1) * TAU)
            p.add("vector",
                  lambda v: v.tensor_reduce(
                      hmax[:], _ap(hbuf).rearrange("p (c t) -> p t c", c=NHC),
                      AX.X, ALU.max, apply_absolute_value=True),
                  waits={"HGD": p.c("HGD"), "TPM": p.c("TPM")},
                  inc="AMX")
            abase = p.c("TPA")
            arbase = p.c("CRA")
            p.add("tensor",
                  lambda t: t.matmul(repA[:, 0:128], hmax[:, 0:128],
                                     W["idm"][:], start=True, stop=True,
                                     is_transpose=True),
                  waits={"AMX": p.c("AMX"), "K1": p.c("K1"),
                         "RM1": p.c("RM1")}, inc="TPA")
            p.add("tensor",
                  lambda t: t.matmul(repB[:, 0:128], hmax[:, 128:256],
                                     W["idm"][:], start=True, stop=True,
                                     is_transpose=True),
                  waits={"AMX": p.c("AMX"), "EC1": p.c("EC1"),
                         "RM1": p.c("RM1")}, inc="TPA")
            p.add("vector",
                  lambda v: v.tensor_reduce(colb[:, 4:5], repA[:, 0:128],
                                            AX.X, ALU.max),
                  waits={"TPA": abase + 1, "EXA": p.c("EXA")}, inc="CRA")
            p.add("vector",
                  lambda v: v.tensor_reduce(colb[:, 5:6], repB[:, 0:128],
                                            AX.X, ALU.max),
                  waits={"TPA": abase + 2}, inc="CRA")
            def _exta(t):
                t.matmul(repA[0:1, TAU:TAU + 128], colb[:, 4:5], W["idm"][:],
                         start=True, stop=True, is_transpose=True)
                return t.matmul(repA[0:1, TAU + 128:2 * TAU], colb[:, 5:6],
                                W["idm"][:], start=True, stop=True,
                                is_transpose=True)
            p.add("tensor", _exta,
                  waits={"CRA": arbase + 2, "RM1": p.c("RM1"),
                         "K1": p.c("K1")}, inc="EXA")
            p.add("vector",
                  lambda v: v.tensor_scalar(rw["h2"][:], stat2b[0:1, 0:TAU],
                                            1.0 / HID, None, ALU.mult),
                  waits={"SQS": p.c("SQS")}, inc="RE")
            p.add("scalar",
                  lambda sc: sc.activation(rw["q2"][:], rw["h2"][:], AF.Sqrt,
                                           bias=epsB[:]),
                  waits={"RE": p.c("RE")}, inc="RF")
            def _rm2(v):
                v.reciprocal(rw["q2"][:], rw["q2"][:])
                v.tensor_tensor(rw["a2"][:], rw["q2"][:], rw["A2"][:],
                                ALU.mult)
                v.tensor_scalar(rw["aa2"][:], rw["a2"][:], EPS5, None,
                                ALU.max)
                v.reciprocal(rw["s2"][:], rw["aa2"][:])
                v.tensor_scalar(rw["s2"][:], rw["s2"][:], 127.0, None,
                                ALU.mult)
                v.tensor_tensor(rw["S2"][:], rw["q2"][:], rw["s2"][:],
                                ALU.mult)
                return v.tensor_scalar(rw["e2t"][:], rw["aa2"][:],
                                       mwc[0:1, 1:2], None, ALU.mult)
            p.add("vector", _rm2, waits={"RF": p.c("RF"), "EXA": p.c("EXA")},
                  inc="RM2")
            bc2w = {"RM2": p.c("RM2"), "EC1": p.c("EC1"),
                    "BC2": p.c("BC2"), "RM1": p.c("RM1")}
            if i >= 1:
                bc2w["K2"] = p.c("K2")
            def _bc2(t):
                t.matmul(repB[:, TAU:2 * TAU], onesf[:], rw["S2"][:],
                         start=True, stop=True)
                return t.matmul(repB[:, 0:TAU], onesf[:], rw["e2t"][:],
                                start=True, stop=True)
            p.add("tensor", _bc2, waits=bc2w, inc="BC2")
            ec2w = {"BC2": p.c("BC2")}
            if i >= 1:
                ec2w["OPC"] = p.c("OPC")
            p.add("scalar",
                  lambda sc: sc.copy(e2sb[:], repB[:, 0:TAU]),
                  waits=ec2w, inc="EC2")

            base_mm2 = p.c("MM2")
            k2w = {"BC2": p.c("BC2"), "EC2": p.c("EC2"), "STT": p.c("STT")}
            if i >= 1:
                k2w["MM2"] = base_mm2
            k2base = p.c("K2")
            for hc2 in range(0, NHC, 2):
                def _k2(v, hc2=hc2):
                    last = None
                    for hc in (hc2, hc2 + 1):
                        v.tensor_tensor(sacc[:],
                                        hbuf[:, hc * TAU:(hc + 1) * TAU],
                                        repB[:, TAU:2 * TAU], ALU.mult)
                        v.tensor_scalar(sacc[:], sacc[:], RNDC, RNDC,
                                        ALU.add, ALU.subtract)
                        last = v.tensor_scalar(
                            k2[:, hc * TAU:(hc + 1) * TAU],
                            sacc[:], 127.0, -128.0, ALU.min, ALU.max)
                    return last
                p.add("vector", _k2, waits=dict(k2w) if hc2 == 0 else {},
                      inc="K2")
            for j in range(1, NHC // 2):
                p.add("tensor",
                      lambda t: t.matmul(repA[0:1, 0:128], W["idm"][:, 0:1],
                                         W["idm"][:], start=True, stop=True,
                                         is_transpose=True),
                      waits={"K2": k2base + j}, inc="KA")

            for oc in range(NCH):
                m2w = {"K2": p.c("K2")}
                if i == 0 and oc == 0:
                    m2w.update(W["wq_done"])
                if oc >= 2:
                    m2w["OPC"] = p.c("OPC") - 1
                elif second or i >= 1:
                    m2w["OPC"] = p.c("OPC")
                def _mm2(t, oc=oc):
                    last = None
                    for kc in range(NHC):
                        lt = w2q[:, kc * D + oc * 128:kc * D + (oc + 1) * 128]
                        last = t.matmul(
                            ops[oc % 2][:], lt,
                            k2[:, kc * TAU:(kc + 1) * TAU],
                            start=(kc == 0), stop=(kc == NHC - 1))
                    return last
                p.add("tensor", _mm2, waits=m2w, inc="MM2")
                ew = {"MM2": p.c("MM2"), "EC2": p.c("EC2"), "bias_ok": BOK}
                if i >= 1 and oc == 0:
                    ew["STx1"] = p.c("STx1")
                    ew["SQMs"] = p.c("SQMs")
                    ew["SC2"] = p.c("SC2")
                def _epi(v, oc=oc, sl=sl):
                    v.tensor_tensor(x1t[:, CS(oc)],
                                    ops[oc % 2][:],
                                    e2sb[:], ALU.mult)
                    return v.scalar_tensor_tensor(
                        x1t[:, CS(oc)], x1t[:, CS(oc)], b2sb[:, oc:oc + 1],
                        xt[sl][:, CS(oc)], ALU.add, ALU.add)
                p.add("vector", _epi, waits=ew, inc="OPC")
            p.add("vector", lambda v: None, waits={"OPC": p.c("OPC")},
                  inc="X1")

            if second:
                p.dma("gpsimd",
                      (lambda t0=t0: lambda gp: gp.dma_start(
                          out=fm_dram(out_fm, t0, TAU),
                          in_=fm_sbuf(x1t)))(),
                      waits={"X1": p.c("X1")}, stream="STx1")
                return
            cguard = {"X1": p.c("X1")}
            if i >= 1:
                cguard["LNS"] = p.c("LNS")
            p.add("scalar",
                  lambda sc: sc.activation(x1sq[:], x1t[:], AF.Square),
                  waits=dict(cguard), inc="SQMs")
            sc2w = {"SQMs": p.c("SQMs"), "X1": p.c("X1"),
                    "STT": p.c("STT")}
            def _chunksums2(v):
                v.tensor_reduce(sacc[:],
                                x1t.ap().rearrange("p (c t) -> p t c",
                                                   c=NCH),
                                AX.X, ALU.add)
                return v.tensor_reduce(
                    sacc2[:], x1sq.ap().rearrange("p (c t) -> p t c",
                                                  c=NCH),
                    AX.X, ALU.add)
            p.add("vector", _chunksums2, waits=sc2w, inc="SC2")
            lnw = {"SC2": p.c("SC2"), "STT": p.c("STT"),
                   "LNS": p.c("LNS"), "RA": p.c("RA")}
            if i >= 1:
                lnw["LNR"] = p.c("LNR")
            def _ln2stats(t):
                t.matmul(stat3b[0:1, 0:TAU], onesc[:], sacc[:],
                         start=True, stop=True)
                return t.matmul(stat2b[0:1, TAU:2 * TAU], onesc[:], sacc2[:],
                                start=True, stop=True)
            p.add("tensor", _ln2stats, waits=lnw, inc="LNS")
            def _lnrows(v):
                v.tensor_scalar(rw["m2"][:], stat3b[0:1, 0:TAU], 1.0 / D,
                                None, ALU.mult)
                v.tensor_scalar(m2row[:], rw["m2"][:], 1.0, None,
                                ALU.mult)
                v.tensor_scalar(rw["v"][:], stat2b[0:1, TAU:2 * TAU], 1.0 / D,
                                None, ALU.mult)
                v.tensor_tensor(rw["t3"][:], rw["m2"][:], rw["m2"][:],
                                ALU.mult)
                return v.tensor_tensor(rw["v"][:], rw["v"][:], rw["t3"][:],
                                       ALU.subtract)
            rwW = {"LNS": p.c("LNS"), "RA": p.c("RA"), "BC1": p.c("BC1"),
                   "STm2": p.c("STm2"), "STr2": p.c("STr2")}
            p.add("vector", _lnrows, waits=rwW, inc="LNV")
            p.add("scalar",
                  lambda sc: sc.activation(rw["lnt"][:], rw["v"][:], AF.Sqrt,
                                           bias=epsA[:]),
                  waits={"LNV": p.c("LNV")}, inc="LNQ")
            p.add("vector",
                  lambda v: v.reciprocal(r2row[:], rw["lnt"][:]),
                  waits={"LNQ": p.c("LNQ")}, inc="LNR")

            p.dma("gpsimd",
                  (lambda t0=t0: lambda gp: gp.dma_start(
                      out=fm_dram(x1_out, t0, TAU),
                      in_=fm_sbuf(x1t)))(),
                  waits={"X1": p.c("X1")}, stream="STx1")
            p.dma("gpsimd",
                  (lambda t0=t0: lambda gp: gp.dma_start(
                      out=rows_out[0:1, t0:t0 + TAU], in_=m2row[:]))(),
                  waits={"LNR": p.c("LNR")}, stream="STm2")
            p.dma("gpsimd",
                  (lambda t0=t0: lambda gp: gp.dma_start(
                      out=rows_out[1:2, t0:t0 + TAU], in_=r2row[:]))(),
                  waits={"LNR": p.c("LNR")}, stream="STr2")

        if second:
            for i in range(n_subtiles):
                front(i)
                mid(i)
                tail(i)
        else:
            front(0)
            for i in range(n_subtiles):
                mid(i)
                if i + 1 < n_subtiles:
                    front(i + 1)
                tail(i)

    return None



def emit_lru_phase(p, nc, pes, io, W, gate=None):
    gate = dict(gate or {})
    x1_in, rows_in = io["x1_in"], io["rows_in"]
    wit, wgt = io["wit"], io["wgt"]
    biv, bgv = io["biv"], io["bgv"]
    hloc_out, p_out = io["hloc_out"], io["p_out"]
    if True:
        sb = lambda *a: pes.enter_context(nc.sbuf_tensor(*a))
        witb = sb("witb", [128, NCH * D], BF16)
        wgtb = sb("wgtb", [128, NCH * D], BF16)
        stage = sb("lstage", [128, D], F32)
        onesf = W["onesf"]
        bisb = sb("bisb", [128, NCH], F32)
        bgsb = sb("bgsb", [128, NCH], F32)
        onecol = sb("onecol", [128, 1], F32)
        zeros = sb("zeros", [128, TAU2], F32)
        x1bt = [sb("x1bt0", [128, NCH * TAU2], F32)] * 2
        rmt = [sb("rmt0", [1, TAU2], F32)] * 2
        rrt = [sb("rrt0", [1, TAU2], F32)] * 2
        xlf = sb("xlf", [128, TAU2], F32)
        xln = sb("xln", [128, NCH * TAU2], BF16)
        ur = [sb(f"ur{i}", [128, TAU2], F32) for i in range(2)]
        gr = [sb(f"gr{i}", [128, TAU2], F32) for i in range(2)]
        ogr = [sb(f"ogr{i}", [128, TAU2], F32) for i in range(2)]
        br = [sb(f"br{i}", [128, TAU2], F32) for i in range(2)]
        hb = [io["bigbuf"].ap()[:, 0:NCH * TAU2]] * 2
        pb = [io["bigbuf"].ap()[:, NCH * TAU2:2 * NCH * TAU2]] * 2
        pbf = sb("pbf", [128, NCH * TAU2], BF16)
        hinit = [sb(f"hinit{c}", [128, 1], F32) for c in range(NCH)]
        pinit = [sb(f"pinit{c}", [128, 1], F32) for c in range(NCH)]


        pt = lambda *a: pes.enter_context(nc.psum_tensor(*a))
        ups = pt("ups", [128, 2 * TAU2], F32)
        gps = pt("gps", [128, 2 * TAU2], F32)
        repm = pt("repm", [128, TAU2], F32)
        repr_ = pt("repr", [128, TAU2], F32)

        p.add("vector", lambda v: v.memset(onecol[:], 1.0))
        p.add("vector", lambda v: v.memset(zeros[:], 0.0), inc="init")
        p.dma("sync", lambda s: s.dma_start(out=bisb[:], in_=biv[:]),
              waits=dict(gate), stream="ld0")
        p.dma("sync", lambda s: s.dma_start(out=bgsb[:], in_=bgv[:]),
              waits=dict(gate), stream="ld0")
        LD0C = p.c("ld0")
        for wi, (wt, wb) in enumerate(((wit, witb), (wgt, wgtb))):
            for kc in range(NCH):
                wldw = {"wcast": p.c("wcast")}
                if wi == 0 and kc == 0:
                    wldw.update(gate)
                p.dma("sync",
                      (lambda wt=wt, kc=kc: lambda s: s.dma_start(
                          out=stage[:], in_=wt[kc]))(),
                      waits=wldw, stream="wld")
                p.add("scalar",
                      (lambda wb=wb, kc=kc: lambda sc: sc.copy(
                          wb[:, kc * D:(kc + 1) * D], stage[:]))(),
                      waits={"wld": 16 * (p.c("wld") // 16)}, inc="wcast")

        def fm_dram(t, t0, n):
            return t.rearrange("c p t -> p c t")[:, :, t0:t0 + n]

        def fm_sbuf(t, tau):
            return _ap(t).rearrange("p (c t) -> p c t", c=NCH)

        for j in range(NST2):
            t0 = j * TAU2
            sl = j % 2
            CS = lambda c: slice(c * TAU2, (c + 1) * TAU2)

            ldw = {}
            if j == 0:
                ldw.update(gate)
            if j >= 1:
                ldw["XLN"] = p.c("XLN")
                ldw["BCM"] = p.c("BCM")
            p.dma("sync",
                  (lambda sl=sl, t0=t0: lambda s: s.dma_start(
                      out=fm_sbuf(x1bt[sl], TAU2),
                      in_=fm_dram(x1_in, t0, TAU2)))(),
                  waits=ldw, stream="Lx")
            p.dma("sync",
                  (lambda sl=sl, t0=t0: lambda s: s.dma_start(
                      out=rmt[sl][:], in_=rows_in[0:1, t0:t0 + TAU2]))(),
                  waits=dict(ldw), stream="Lm")
            p.dma("sync",
                  (lambda sl=sl, t0=t0: lambda s: s.dma_start(
                      out=rrt[sl][:], in_=rows_in[1:2, t0:t0 + TAU2]))(),
                  waits=dict(ldw), stream="Lr")
            lnow = {"Lx": p.c("Lx"), "Lm": p.c("Lm"), "Lr": p.c("Lr")}

            bw = dict(lnow); bw["init"] = p.c("init")
            if j >= 1:
                bw["XLN"] = p.c("XLN")
            def _bc(t, sl=sl):
                t.matmul(repm[:], onesf[:], rmt[sl][:], start=True,
                         stop=True)
                return t.matmul(repr_[:], onesf[:], rrt[sl][:],
                                start=True, stop=True)
            p.add("tensor", _bc, waits=bw, inc="BCM")

            xw = {"BCM": p.c("BCM")}
            if j >= 1:
                xw["UMM"] = p.c("UMM")
            def _xln(v, sl=sl):
                last = None
                for c in range(NCH):
                    v.tensor_tensor(xlf[:], x1bt[sl][:, CS(c)], repm[:],
                                    ALU.subtract)
                    last = v.tensor_tensor(xln[:, CS(c)], xlf[:], repr_[:],
                                          ALU.mult)
                return last
            p.add("vector", _xln, waits=xw, inc="XLN")

            for oc in range(NCH):
                mw = {"XLN": p.c("XLN"), "wcast": 2 * NCH,
                      "UMM": p.c("UMM")}
                if oc >= 2 or j >= 1:
                    mw["UGD"] = p.c("UGD") - (1 if oc >= 2 else 0)
                def _umm(t, oc=oc):
                    for kc in range(NCH):
                        lt = witb[:, kc * D + oc * 128:kc * D + (oc + 1) * 128]
                        t.matmul(ups[:, (oc % 2) * TAU2:(oc % 2 + 1) * TAU2],
                                 lt, xln[:, CS(kc)], start=(kc == 0),
                                 stop=(kc == NCH - 1))
                    last = None
                    for kc in range(NCH):
                        lt = wgtb[:, kc * D + oc * 128:kc * D + (oc + 1) * 128]
                        last = t.matmul(
                            gps[:, (oc % 2) * TAU2:(oc % 2 + 1) * TAU2],
                            lt, xln[:, CS(kc)], start=(kc == 0),
                            stop=(kc == NCH - 1))
                    return last
                p.add("tensor", _umm, waits=mw, inc="UMM")

                def _acts(sc, oc=oc):
                    sc.activation(ur[oc % 2][:],
                                  ups[:, (oc % 2) * TAU2:(oc % 2 + 1) * TAU2],
                                  AF.Identity, bias=bisb[:, oc:oc + 1])
                    sc.activation(gr[oc % 2][:],
                                  gps[:, (oc % 2) * TAU2:(oc % 2 + 1) * TAU2],
                                  AF.Sigmoid, bias=bgsb[:, oc:oc + 1])
                    return sc.activation(ogr[oc % 2][:], gr[oc % 2][:],
                                         AF.Identity, bias=onecol[:],
                                         scale=-1.0)
                aw = {"UMM": p.c("UMM"), "ld0": LD0C}
                if j >= 1 or oc >= 2:
                    aw["SCN"] = p.c("SCN") - (1 if oc >= 2 else 0)
                p.add("scalar", _acts, waits=aw, inc="UGD")

                def _bin(v, oc=oc):
                    return v.tensor_tensor(br[oc % 2][:], ogr[oc % 2][:],
                                           ur[oc % 2][:], ALU.mult)
                p.add("vector", _bin, waits={"UGD": p.c("UGD")}, inc="BIN")

                def _scan(v, oc=oc, sl=sl, j=j):
                    if j == 0:
                        ih, ip = 0.0, 1.0
                    else:
                        ih, ip = hinit[oc][:], pinit[oc][:]
                    v.tensor_tensor_scan(hb[sl][:, CS(oc)], gr[oc % 2][:],
                                         br[oc % 2][:], ih, ALU.mult, ALU.add)
                    v.tensor_tensor_scan(pb[sl][:, CS(oc)],
                                         gr[oc % 2][:], zeros[:], ip,
                                         ALU.mult, ALU.add)
                    v.tensor_copy(hinit[oc][:],
                                  hb[sl][:, oc * TAU2 + TAU2 - 1:
                                         oc * TAU2 + TAU2])
                    return v.tensor_copy(pinit[oc][:],
                                         pb[sl][:, oc * TAU2 + TAU2 - 1:
                                                oc * TAU2 + TAU2])
                sw = {"BIN": p.c("BIN"), "init": p.c("init")}
                if j >= 1 and oc == 0:
                    sw["SH"] = p.c("SH")
                    sw["SP"] = p.c("SP")
                p.add("vector", _scan, waits=sw, inc="SCN")

            p.add("scalar", lambda sc: sc.copy(carry_sb0[:], carry_sb0[:])
                  if False else None, waits={"SCN": p.c("SCN")}, inc="HCP") \
                if False else None
            p.dma("gpsimd",
                  (lambda t0=t0, sl=sl: lambda gp: gp.dma_start(
                      out=fm_dram(hloc_out, t0, TAU2),
                      in_=fm_sbuf(hb[sl], TAU2)))(),
                  waits={"SCN": p.c("SCN")}, stream="SH")
            cwp = {"SCN": p.c("SCN")}
            if j >= 1:
                cwp["SP"] = p.c("SP")
            p.add("scalar",
                  (lambda sl=sl: lambda sc: sc.activation(
                      pbf[:], pb[sl][:], AF.Identity))(),
                  waits=cwp, inc="HCP")
            p.dma("gpsimd",
                  (lambda t0=t0, sl=sl: lambda gp: gp.dma_start(
                      out=fm_dram(p_out, t0, TAU2),
                      in_=fm_sbuf(pbf, TAU2)))(),
                  waits={"HCP": p.c("HCP")}, stream="SP")
            if j == NST2 - 1:
                carry_sb = sb("carry_sb", [128, NCH], F32)
                def _carrycp(v):
                    last = None
                    for c in range(NCH):
                        last = v.tensor_copy(carry_sb[:, c:c + 1],
                                             hinit[c][:])
                    return last
                p.add("vector", _carrycp,
                      waits={"SCN": p.c("SCN")}, inc="CCP")
                p.dma("gpsimd",
                      lambda gp: gp.dma_start(out=io["carry_out"][:],
                                              in_=carry_sb[:]),
                      waits={"CCP": p.c("CCP")}, stream="SC")
    return



def build_fused():
    nc = bass.Bass("TRN2", target_bir_lowering=False)
    es = ExitStack()
    dram = lambda name, shape, dt, kind="Internal", space="Local": \
        nc.dram_tensor(name, shape, dt, kind=kind, addr_space=space).ap()

    io = {}
    io["w1t"] = dram("w1t", [NCH, 128, HID], F32, "ExternalInput")
    io["w2t"] = dram("w2t", [NHC, 128, D], F32, "ExternalInput")
    io["wit"] = dram("wit", [NCH, 128, D], F32, "ExternalInput")
    io["wgt"] = dram("wgt", [NCH, 128, D], F32, "ExternalInput")
    io["wot"] = dram("wot", [NCH, 128, D], F32, "ExternalInput")
    io["b1v"] = dram("b1v", [128, NHC], F32, "ExternalInput")
    io["b2v"] = dram("b2v", [128, NCH], F32, "ExternalInput")
    io["biv"] = dram("biv", [128, NCH], F32, "ExternalInput")
    io["bgv"] = dram("bgv", [128, NCH], F32, "ExternalInput")
    io["bov"] = dram("bov", [128, NCH], F32, "ExternalInput")
    io["selv"] = dram("selv", [128, 2], F32, "ExternalInput")
    io["idm"] = dram("idm", [128, 128], F32, "ExternalInput")
    io["x_in"] = dram("x_in", [NCH, 128, TPC], F32, "ExternalInput")
    io["out_fm"] = dram("out_fm", [NCH, 128, TPC], F32, "ExternalOutput")
    io["x1_out"] = io["x1_in"] = dram("x1d", [NCH, 128, TPC], F32)
    io["rows_out"] = io["rows_in"] = dram("rowsd", [2, TPC], F32)
    io["hloc_out"] = io["hloc_in"] = dram("hd", [NCH, 128, TPC], F32)
    io["p_out"] = io["p_in"] = dram("pd", [NCH, 128, TPC], BF16)
    io["carry_out"] = dram("carryd", [128, NCH], F32)
    io["carry_gath"] = dram("cgath", [2, 128, NCH], F32, "Internal", "Local")

    with es:
        p = Prog(nc, es)
        sb = lambda *a: es.enter_context(nc.sbuf_tensor(*a))
        bigbuf = sb("bigbuf", [128, NHC * TAU], F32)
        io["bigbuf"] = bigbuf
        stage = bigbuf.ap()[:, 0:HID]
        wabs = bigbuf.ap()[:, HID:2 * HID]
        W = emit_weight_quant(p, sb, io, stage, wabs)
        idsb = sb("idsb", [128, 128], F32)
        W["idm"] = idsb
        p.dma("sync", lambda s: s.dma_start(out=idsb[:], in_=io["idm"][:]),
              stream="Lid")
        wotb = sb("wotb", [128, NCH * D], BF16)
        io["wotb"] = wotb
        wostage = stage[:, 0:D]
        for kc in range(NCH):
            wol = {"wo_cast": p.c("wo_cast")}
            if kc == 0:
                wol["wq_q"] = NCH + NHC
            p.dma("sync",
                  (lambda kc=kc: lambda s: s.dma_start(
                      out=wostage, in_=io["wot"][kc]))(),
                  waits=wol, stream="wo_ld")
            p.add("scalar",
                  (lambda kc=kc: lambda sc: sc.copy(
                      wotb[:, kc * D:(kc + 1) * D], wostage))(),
                  waits={"wo_ld": p.c("wo_ld")}, inc="wo_cast")
        selsb = sb("selsb", [128, 2], F32)
        gathsb = sb("gathsb", [128, 2 * NCH], F32)
        hcar = sb("hcar", [128, NCH], F32)
        io["hcar"] = hcar
        p.dma("sync", lambda s: s.dma_start(out=selsb[:], in_=io["selv"][:]),
              stream="ld0")

        with ExitStack() as pes:
            emit_ffn_phase(p, nc, pes, W, io, second=False,
                           gate={"wo_cast": NCH})
        g1 = {"STx1": p.c("STx1"), "STm2": p.c("STm2"),
              "STr2": p.c("STr2"), "LNR": p.c("LNR")}

        with ExitStack() as pes:
            emit_lru_phase(p, nc, pes, io, W, gate=g1)

        p.add("gpsimd",
              lambda gp: gp.collective_compute(
                  "AllGather", ALU.bypass,
                  replica_groups=[[0, 1], [2, 3], [4, 5], [6, 7]],
                  ins=[io["carry_out"][:]], outs=[io["carry_gath"][:]]),
              waits={"SC": p.c("SC")}, inc="CCG")
        p.dma("gpsimd",
              lambda gp: gp.dma_start(out=gathsb[:, 0:NCH],
                                      in_=io["carry_gath"][0]),
              waits={"CCG": p.c("CCG")}, stream="LG")
        p.dma("gpsimd",
              lambda gp: gp.dma_start(out=gathsb[:, NCH:2 * NCH],
                                      in_=io["carry_gath"][1]),
              waits={"CCG": p.c("CCG")}, stream="LG")
        def _sel(v):
            v.tensor_scalar(hcar[:], gathsb[:, 0:NCH], selsb[:, 0:1], None,
                            ALU.mult)
            return v.scalar_tensor_tensor(hcar[:], gathsb[:, NCH:2 * NCH],
                                          selsb[:, 1:2], hcar[:],
                                          ALU.mult, ALU.add)
        p.add("vector", _sel,
              waits={"LG": p.c("LG"), "ld0": p.c("ld0")}, inc="HCAR")
        g2 = {"SH": p.c("SH"), "SP": p.c("SP"), "HCAR": p.c("HCAR"),
              "HCP": p.c("HCP"), "CCP": p.c("CCP")}

        with ExitStack() as pes:
            emit_ffn_phase(p, nc, pes, W, io, second=True, gate=g2)

        fin = {"STx1": p.c("STx1")}
        p.add("gpsimd", lambda gp: None, waits=fin)
        p.realize()
    return nc



def _prep_host(inputs):
    x = np.ascontiguousarray(np.asarray(inputs["L0_attn_x"], dtype=np.float32))
    w1 = np.asarray(inputs["L0_ffn_w1"], np.float32)
    w2 = np.asarray(inputs["L0_ffn_w2"], np.float32)
    wi = np.asarray(inputs["L0_attn_Wi"], np.float32)
    wg = np.asarray(inputs["L0_attn_Wg"], np.float32)
    wo = np.asarray(inputs["L0_attn_Wo"], np.float32)
    C = lambda a: np.ascontiguousarray(a)
    g = {
        "w1t": C(w1.T.reshape(NCH, 128, HID)),
        "w2t": C(w2.T.reshape(NHC, 128, D)),
        "wit": C(wi.T.reshape(NCH, 128, D)),
        "wgt": C(wg.T.reshape(NCH, 128, D)),
        "wot": C(wo.T.reshape(NCH, 128, D)),
        "b1v": C(np.asarray(inputs["L0_ffn_b1"], np.float32)
                 .reshape(NHC, 128).T),
        "b2v": C(np.asarray(inputs["L0_ffn_b2"], np.float32)
                 .reshape(NCH, 128).T),
        "biv": C(np.asarray(inputs["L0_attn_bi"], np.float32)
                 .reshape(NCH, 128).T),
        "bgv": C(np.asarray(inputs["L0_attn_bg"], np.float32)
                 .reshape(NCH, 128).T),
        "bov": C(np.asarray(inputs["L0_attn_bo"], np.float32)
                 .reshape(NCH, 128).T),
    }
    xs = []
    for c in range(NCORE):
        s, hc = c // 2, c % 2
        sl = x[s, hc * TPC:(hc + 1) * TPC, :]
        xs.append(C(sl.T.reshape(NCH, 128, TPC)))
    return g, xs


def kernel(**inputs):
    g, xs = _prep_host(inputs)
    cores = list(range(NCORE))
    nc = build_fused()
    sel0 = np.zeros((128, 2), np.float32)
    sel1 = np.zeros((128, 2), np.float32)
    sel1[:, 0] = 1.0
    maps = []
    for c in cores:
        maps.append({
            "w1t": g["w1t"], "w2t": g["w2t"], "wit": g["wit"],
            "wgt": g["wgt"], "wot": g["wot"], "b1v": g["b1v"],
            "b2v": g["b2v"], "biv": g["biv"], "bgv": g["bgv"],
            "bov": g["bov"], "x_in": xs[c],
            "selv": sel1 if c % 2 else sel0,
            "idm": np.eye(128, dtype=np.float32),
        })
    res = run_bass_kernel_spmd(nc, maps, core_ids=cores).results
    out = np.empty((BATCH, SEQ, D), np.float32)
    for c in cores:
        s, hc = c // 2, c % 2
        o = np.asarray(res[c]["out_fm"]).reshape(D, TPC)
        out[s, hc * TPC:(hc + 1) * TPC, :] = o.T
    return out
